# version 09
# baseline (speedup 1.0000x reference)
import numpy as np
import ml_dtypes
import concourse.bass as bass
import concourse.mybir as mybir
from concourse.bass_utils import run_bass_kernel_spmd
from concourse.ap import AP as _BassAP
from contextlib import ExitStack

F32 = mybir.dt.float32
BF16 = mybir.dt.bfloat16
I32 = mybir.dt.int32
U32 = mybir.dt.uint32
AF = mybir.ActivationFunctionType
ALU = mybir.AluOpType
AX = mybir.AxisListType.X

NCORES = 8
T = 8192
D = 2048
TPC = T // NCORES
BF = ml_dtypes.bfloat16

ENGINES = ("sync", "gpsimd", "vector", "scalar", "tensor")


class SemModel:
    __slots__ = ("handle", "count", "name")

    def __init__(self, handle, name):
        self.handle = handle
        self.count = 0
        self.name = name


class Deferred:

    def __init__(self, fn, ndim=2):
        self.fn = fn
        self.ndim = ndim


class _EngProxy:

    def __init__(self, b, eng):
        self._b = b
        self._e = eng

    def __getattr__(self, op):
        def call(*args, **kwargs):
            item = [op, args, kwargs, []]
            self._b.q[self._e].append(item)
            return item

        return call


class Builder:
    def __init__(self, nc, ctx):
        self.nc = nc
        self.ctx = ctx
        self.root_ctx = ctx
        self.q = {e: [] for e in ENGINES}
        self._fence_sems = {}
        self.y = _EngProxy(self, "sync")
        self.g = _EngProxy(self, "gpsimd")
        self.v = _EngProxy(self, "vector")
        self.s = _EngProxy(self, "scalar")
        self.t = _EngProxy(self, "tensor")

    def sem(self, name):
        if not hasattr(self, "_semcache"):
            self._semcache = {}
        if name in self._semcache:
            return self._semcache[name]
        h = self.root_ctx.enter_context(self.nc.semaphore(name))
        sm = SemModel(h, name)
        self._semcache[name] = sm
        return sm

    def scope(self):
        import contextlib

        @contextlib.contextmanager
        def _sc():
            old = self.ctx
            with ExitStack() as es:
                self.ctx = es
                try:
                    yield
                finally:
                    self.ctx = old
        return _sc()

    def slot_sems(self, name, n):
        return [self.sem(f"{name}{j}") for j in range(n)]

    def inc(self, item, sem, n=1):
        sem.count += n
        if item is not None:
            item[3].append((sem, n))
        return sem.count

    def fence(self, engine):
        q = self.q[engine]
        for item in reversed(q):
            if item[0] != "wait_ge":
                if item[3]:
                    sem, n = item[3][-1]
                    self.q[engine].append(["wait_ge", (sem, sem.count), {}, []])
                    return
                fs = self._fence_sems.get(engine)
                if fs is None:
                    fs = self.sem("F_" + engine)
                    self._fence_sems[engine] = fs
                fs.count += 1
                item[3].append((fs, 1))
                self.q[engine].append(["wait_ge", (fs, fs.count), {}, []])
                return

    def phase_mark(self, name):
        if not PHASE_MARKERS:
            return
        PHASE_MARKS.append(name)
        if not hasattr(self, "_mkbuf"):
            self._mkbuf = self.root_ctx.enter_context(
                self.nc.sbuf_tensor("b_mkbuf", [1, 16], F32))
            self._mksem = self.sem("B_MK")
        for snm in ("VE", "TE"):
            sm = self._semcache.get(snm)
            if sm is not None:
                self.wait("gpsimd", sm)
        self.inc(self.g.memset(self._mkbuf[0:1, 0:1], 1.0), self._mksem)

    def wait(self, engine, sem, val=None):
        v = sem.count if val is None else val
        if v > 0:
            self.q[engine].append(["wait_ge", (sem, v), {}, []])

    def sbuf(self, name, shape, dtype):
        return self.ctx.enter_context(self.nc.sbuf_tensor(name, shape, dtype))

    def psum(self, name, shape, dtype):
        return self.root_ctx.enter_context(self.nc.psum_tensor(name, shape, dtype))

    def finalize(self):
        nc = self.nc
        block = self.ctx.enter_context(nc.Block())

        def runner(engine_name):
            def run(eng):
                for op, args, kwargs, incs in self.q[engine_name]:
                    if op == "__scope__":
                        continue
                    if op == "wait_ge":
                        sem, v = args
                        eng.wait_ge(sem.handle, v)
                        continue
                    args = [a.fn(eng) if isinstance(a, Deferred) else a
                            for a in args]
                    kwargs = {k2: (v2.fn(eng) if isinstance(v2, Deferred)
                                   else v2) for k2, v2 in kwargs.items()}
                    ins = getattr(eng, op)(*args, **kwargs)
                    for sem, n in incs:
                        ins.then_inc(sem.handle, n)
            return run

        block.sync(runner("sync"))
        block.gpsimd(runner("gpsimd"))
        block.vector(runner("vector"))
        block.scalar(runner("scalar"))
        block.tensor(runner("tensor"))


def common_const_setup(b, need_identb=True, need_identf=False,
                       const_vals=()):
    out = {}
    CONST = b.sem("CONST")
    cc = {}
    for i, val in enumerate(const_vals):
        t = b.sbuf(f"c_val{i}", [128, 1], F32)
        b.inc(b.v.memset(t[:], float(val)), CONST)
        cc[float(val)] = t
    out["cc"] = cc
    diag = b.sbuf("c_diag", [128, 128], F32)
    it = b.g.iota(diag[:], pattern=[[1, 128]], base=0, channel_multiplier=-1,
                  allow_small_or_imprecise_dtypes=True)
    b.inc(it, CONST)
    b.wait("vector", CONST)
    if need_identb:
        identb = b.sbuf("c_identb", [128, 128], BF16)
        ib = b.v.tensor_scalar(out=identb[:], in0=diag[:], scalar1=0.0,
                               scalar2=None, op0=ALU.is_equal)
        b.inc(ib, CONST)
        out["identb"] = identb
    if need_identf:
        identf = b.sbuf("c_identf", [128, 128], F32)
        if_ = b.v.tensor_scalar(out=identf[:], in0=diag[:], scalar1=0.0,
                                scalar2=None, op0=ALU.is_equal)
        b.inc(if_, CONST)
        out["identf"] = identf
    out["CONST"] = CONST
    return out


def emit_ln_transpose(b, consts, x_dram, lnT, n_tok, sems, x_sb=None,
                      name="ln"):
    TE, VE, SE = sems["TE"], sems["VE"], sems["SE"]
    XS = sems["XS"]
    nt = n_tok // 128
    if isinstance(x_dram, list):
        x_aps = x_dram
    else:
        x_aps = [x_dram[i].rearrange("a p c -> p a c") for i in range(nt)]
    xbuf = x_sb if x_sb is not None else [
        b.sbuf(name + "_xb0", [128, D], F32), b.sbuf(name + "_xb1", [128, D], F32)]
    sq = b.sbuf(name + "_sq", [128, D], F32)
    ssum = b.sbuf(name + "_ssum", [128, 1], F32)
    ssq = b.sbuf(name + "_ssq", [128, 1], F32)
    m = b.sbuf(name + "_m", [128, 1], F32)
    var = b.sbuf(name + "_var", [128, 1], F32)
    rstd = b.sbuf(name + "_rstd", [128, 1], F32)
    nmr = b.sbuf(name + "_nmr", [128, 1], F32)
    lntok = b.sbuf(name + "_tok", [128, D], BF16)
    identb = consts["identb"]
    pT2 = consts["pT2"]
    ve_evac = []

    se_done = []
    ve_var = []
    for i in range(nt):
        xb = xbuf[i] if x_sb is not None else xbuf[i % 2]
        if x_sb is None and i >= 2:
            b.wait("sync", VE, se_done[i - 2])
        src = x_aps[i]
        if src.ndim == 3:
            d = b.y.dma_start(out=xb[:].rearrange("p (a c) -> p a c", a=4),
                              in_=src)
        else:
            d = b.y.dma_start(out=xb[:], in_=src)
        b.inc(d, XS[i % 2], 16)
        b.wait("vector", XS[i % 2])
        b.wait("scalar", XS[i % 2])
        if i >= 1:
            b.wait("scalar", VE, ve_var[i - 1])
        b.inc(b.v.reduce_sum(out=ssum[:], in_=xb[:], axis=AX), VE)
        b.fence("vector")
        b.inc(b.s.activation(out=sq[:], in_=xb[:], func=AF.Square,
                             accum_out=ssq[:]), SE)
        b.wait("vector", SE)
        b.inc(b.v.tensor_scalar(out=m[:], in0=ssum[:], scalar1=1.0 / D,
                                scalar2=None, op0=ALU.mult), VE)
        b.fence("vector")
        b.inc(b.v.tensor_mul(out=nmr[:], in0=m[:], in1=m[:]), VE)
        b.fence("vector")
        ve_var.append(b.inc(b.v.scalar_tensor_tensor(out=var[:], in0=ssq[:],
                                                      scalar=1.0 / D,
                                                      in1=nmr[:], op0=ALU.mult,
                                                      op1=ALU.subtract), VE))
        b.fence("vector")
        b.wait("scalar", VE)
        b.inc(b.s.activation(out=var[:], in_=var[:], func=AF.Sqrt,
                             bias=consts["cc"][1e-5][:]), SE)
        b.wait("vector", SE)
        b.inc(b.v.reciprocal(out=rstd[:], in_=var[:]), VE)
        b.fence("vector")
        b.inc(b.v.scalar_tensor_tensor(out=nmr[:], in0=m[:], scalar=-1.0,
                                       in1=rstd[:], op0=ALU.mult,
                                       op1=ALU.mult), VE)
        b.fence("vector")
        b.wait("vector", TE)
        lnt = b.v.tensor_scalar(out=lntok[:], in0=xb[:], scalar1=rstd[:],
                                scalar2=nmr[:], op0=ALU.mult, op1=ALU.add)
        se_done.append(b.inc(lnt, VE))
        b.wait("tensor", VE)
        for c in range(16):
            k = i * 16 + c
            if k >= 2:
                b.wait("tensor", VE, ve_evac[k - 2])
            tr = b.t.transpose(out=pT2[k % 2][:, 0:128],
                               in_=lntok[:, c * 128:(c + 1) * 128],
                               identity=identb[:])
            b.inc(tr, TE)
            b.wait("vector", TE)
            ev = b.v.tensor_copy(out=lnT[c][:, i * 128:(i + 1) * 128],
                                 in_=pT2[k % 2][:, 0:128])
            ve_evac.append(b.inc(ev, VE))


_sc_uid = [0]


def emit_scale_cols(b, consts, sems, src_dram, n, dst_col, pF):
    TE, VE = sems["TE"], sems["VE"]
    _sc_uid[0] += 1
    CS = b.sem(f"CS{_sc_uid[0]}_" + src_dram.tensor.name)
    nt = (n + 127) // 128
    row = b.sbuf(f"sc_row{_sc_uid[0]}_" + src_dram.tensor.name, [nt, 128], F32)
    d = b.y.dma_start(out=row[:], in_=src_dram.rearrange("(a p) -> a p", p=128))
    b.inc(d, CS, 16)
    b.wait("tensor", CS)
    b.wait("tensor", consts["CONST"])
    tr = b.t.transpose(out=pF[:, 0:nt], in_=row[:], identity=consts["identf"][0:nt, 0:nt])
    b.inc(tr, TE)
    b.wait("vector", TE)
    b.inc(b.v.tensor_copy(out=dst_col[:, 0:nt], in_=pF[:, 0:nt]), VE)
    b.fence("vector")
    b.wait("tensor", VE)


def emit_f0_prep(b, nc_, gwT, uwT, dwT, tag):
    PREP = b.sem(tag + "PREP")
    NIT_ = gwT.shape[0]
    gwTb = nc_.dram_tensor(tag + "gwTb", [NIT_, 128, D], BF16,
                           kind="Internal").ap()
    uwTb = nc_.dram_tensor(tag + "uwTb", [NIT_, 128, D], BF16,
                           kind="Internal").ap()
    dwTb = nc_.dram_tensor(tag + "dwTb", [4, NIT_, 128, 512], BF16,
                           kind="Internal").ap()
    prep_n = 0
    for src, dst in ((gwT, gwTb), (uwT, uwTb), (dwT, dwTb)):
        nq = src.shape[0]
        for q in range(4):
            lo = (nq // 4) * q
            hi = (nq // 4) * (q + 1)
            b.inc(b.g.dma_start(out=dst[lo:hi], in_=src[lo:hi]), PREP, 16)
            prep_n += 16
    return (gwTb, uwTb, dwTb), PREP, prep_n


def build_ffn_program(nc, b, sems, consts, x, o_x, gwT, uwT, dwT, scales,
                      inter, half=0.5, tag="f", psums=None):
    TE, VE, SE = sems["TE"], sems["VE"], sems["SE"]
    XS = sems["XS"]
    GW = b.slot_sems("fGW", 3)
    UW = b.slot_sems("fUW", 3)
    DW = b.slot_sems("fDW", 6)
    XE = b.slot_sems("fXE", 2)
    SS = b.slot_sems("fSS", 2)
    NIT = inter // 128
    w_dtype = gwT.dtype
    nc_ = b.nc
    if w_dtype != BF16:
        (gwT, uwT, dwT), PREP, prep_n = emit_f0_prep(b, nc_, gwT, uwT, dwT,
                                                     tag)
        b.wait("sync", PREP, prep_n)
        b.wait("gpsimd", PREP, prep_n)

    if psums is None:
        psums = [b.psum(tag + f"ps{i}", [128, 512], F32) for i in range(6)]
    psG, psU = psums[0], psums[1]
    psD = psums[2:6]
    pF = psD[0]

    lnT = [b.sbuf(tag + f"lnT{c}", [128, TPC], BF16) for c in range(16)]
    h = [b.sbuf(tag + f"h{i}", [128, TPC], BF16) for i in range(NIT)]
    gw_buf = [b.sbuf(tag + f"gwb{j}", [128, 16 * 128], BF16) for j in range(3)]
    uw_buf = [b.sbuf(tag + f"uwb{j}", [128, 16 * 128], BF16) for j in range(3)]
    dw_buf = [b.sbuf(tag + f"dwb{j}", [128, 512], BF16) for j in range(6)]
    xe_buf = [b.sbuf(tag + f"xe{j}", [128, 512], F32) for j in range(2)]
    oe_buf = [b.sbuf(tag + f"oe{j}", [128, 512], F32) for j in range(2)]
    ge_buf = xe_buf

    if scales is not None:
        gs, us, ds = scales
        gs_col = b.sbuf(tag + "gs_col", [128, NIT], F32)
        us_col = b.sbuf(tag + "us_col", [128, NIT], F32)
        dsbc = b.sbuf(tag + "dsbc", [128, D], F32)
        ones1 = b.sbuf(tag + "ones1", [1, 128], F32)
        dsrow = b.sbuf(tag + "dsrow", [1, D], F32)
        emit_scale_cols(b, consts, sems, gs, inter, gs_col, pF)
        emit_scale_cols(b, consts, sems, us, inter, us_col, pF)
        CSd = b.sem(tag + "CSd")
        b.inc(b.y.dma_start(out=dsrow[:], in_=ds[None, :]), CSd, 16)
        b.wait("vector", consts["CONST"])
        b.inc(b.v.memset(ones1[:], 1.0), VE)
        b.wait("tensor", VE)
        b.wait("tensor", CSd)
        for jj in range(4):
            mm = b.t.matmul(out=pF[:, 0:512], lhsT=ones1[:],
                            rhs=dsrow[:, jj * 512:(jj + 1) * 512],
                            start=True, stop=True)
            b.inc(mm, TE)
            b.wait("vector", TE)
            b.inc(b.v.tensor_scalar(out=dsbc[:, jj * 512:(jj + 1) * 512],
                                    in0=pF[:, 0:512], scalar1=half,
                                    scalar2=None, op0=ALU.mult), VE)
            b.fence("vector")
            b.wait("tensor", VE)

    b.wait("scalar", consts["CONST"])
    emit_ln_transpose(b, consts, x, lnT, TPC, sems, name=tag + "ln")
    b.wait("tensor", sems["VE"])

    def load_giu(it, j):
        b.inc(b.y.dma_start(out=gw_buf[j][:], in_=gwT[it]), GW[j], 16)
        b.inc(b.y.dma_start(out=uw_buf[j][:], in_=uwT[it]), UW[j], 16)

    te_gate = []
    ve_h = []
    se_g = {}
    ve_g = {}
    load_giu(0, 0)
    load_giu(1, 1)
    for it in range(NIT):
        j = it % 3
        b.wait("tensor", GW[j])
        b.wait("tensor", UW[j])
        if it == 0 and scales is not None:
            b.wait("tensor", VE)
        if it + 2 < NIT:
            if it >= 1:
                b.wait("sync", TE, te_gate[it - 1])
            load_giu(it + 2, (it + 2) % 3)
        mm = None
        for ts in range(2):
            gidx = it * 2 + ts
            bs = gidx % 2
            psG_ = psums[0] if bs == 0 else psums[2]
            psU_ = psums[1] if bs == 0 else psums[3]
            if gidx >= 2:
                b.wait("tensor", SE, se_g[gidx - 2])
                b.wait("tensor", VE, ve_g[gidx - 2])
            for c in range(16):
                b.t.matmul(out=psG_[:], lhsT=gw_buf[j][:, c * 128:(c + 1) * 128],
                           rhs=lnT[c][:, ts * 512:(ts + 1) * 512],
                           start=(c == 0), stop=(c == 15))
                mm = b.t.matmul(out=psU_[:], lhsT=uw_buf[j][:, c * 128:(c + 1) * 128],
                                rhs=lnT[c][:, ts * 512:(ts + 1) * 512],
                                start=(c == 0), stop=(c == 15))
            mmg = b.inc(mm, TE)
            b.wait("scalar", TE, mmg)
            b.wait("scalar", VE)
            if scales is not None:
                si = b.s.activation(out=ge_buf[ts][:], in_=psG_[:], func=AF.Silu,
                                    scale=gs_col[:, it:it + 1])
            else:
                si = b.s.activation(out=ge_buf[ts][:], in_=psG_[:], func=AF.Silu)
            se_g[gidx] = b.inc(si, SE)
            b.wait("vector", SE, se_g[gidx])
            if scales is not None:
                b.inc(b.v.tensor_scalar(out=oe_buf[ts][:], in0=psU_[:],
                                        scalar1=us_col[:, it:it + 1],
                                        scalar2=None, op0=ALU.mult), VE)
                b.fence("vector")
                hv = b.v.tensor_mul(out=h[it][:, ts * 512:(ts + 1) * 512],
                                    in0=ge_buf[ts][:], in1=oe_buf[ts][:])
            else:
                hv = b.v.tensor_mul(out=h[it][:, ts * 512:(ts + 1) * 512],
                                    in0=ge_buf[ts][:], in1=psU_[:])
            ve_g[gidx] = b.inc(hv, VE)
        te_gate.append(TE.count)
        ve_h.append(VE.count)

    def load_dw(os_, ic, j):
        b.inc(b.y.dma_start(out=dw_buf[j][:], in_=dwT[os_, ic]), DW[j], 16)

    te_down = []
    ld_idx = 0
    b.wait("tensor", VE, ve_h[-1])
    b.wait("tensor", SE)
    ve_epi = []
    for tg in range(2):
        for os_ in range(4):
            for ic in range(NIT):
                j = ld_idx % 6
                if ld_idx >= 4:
                    b.wait("sync", TE, te_down[ld_idx - 4])
                load_dw(os_, ic, j)
                b.wait("tensor", DW[j])
                if ic == 0 and ld_idx > 0:
                    b.wait("tensor", VE)
                mm = None
                for tt in range(4):
                    t_abs = tg * 4 + tt
                    mm = b.t.matmul(out=psD[tt][:],
                                    lhsT=h[ic][:, t_abs * 128:(t_abs + 1) * 128],
                                    rhs=dw_buf[j][:],
                                    start=(ic == 0), stop=(ic == NIT - 1))
                b.inc(mm, TE)
                te_down.append(TE.count)
                ld_idx += 1
            for tt in range(4):
                t_abs = tg * 4 + tt
                jj = tt % 2
                if len(ve_epi) >= 2:
                    b.wait("sync", VE, ve_epi[-2])
                else:
                    b.wait("sync", VE, ve_h[-1])
                dx = b.y.dma_start(out=xe_buf[jj][:], in_=x[t_abs, os_])
                b.inc(dx, XE[jj], 16)
                b.wait("vector", XE[jj])
                b.wait("vector", TE)
                if len(ve_epi) >= 2:
                    b.wait("vector", SS[jj])
                if scales is not None:
                    b.inc(b.v.tensor_mul(out=oe_buf[jj][:], in0=psD[tt][:],
                                         in1=dsbc[:, os_ * 512:(os_ + 1) * 512]), VE)
                    b.fence("vector")
                    av = b.v.tensor_add(out=oe_buf[jj][:], in0=oe_buf[jj][:],
                                        in1=xe_buf[jj][:])
                else:
                    av = b.v.scalar_tensor_tensor(out=oe_buf[jj][:], in0=psD[tt][:],
                                                  scalar=half, in1=xe_buf[jj][:],
                                                  op0=ALU.mult, op1=ALU.add)
                ve_epi.append(b.inc(av, VE))
                b.wait("sync", VE)
                so = b.y.dma_start(out=o_x[t_abs, os_], in_=oe_buf[jj][:])
                b.inc(so, SS[jj], 16)
    b.wait("sync", SS[0])
    b.wait("sync", SS[1])
    return {"lnT": lnT, "h": h, "gw_buf": gw_buf,
            "wbf": (gwT, uwT, dwT), "SS": SS}


def build_ffn_launch(inter, with_scales, w_dtype, half=0.5, debug=False):
    nc = bass.Bass("TRN2", target_bir_lowering=False, debug=False)
    NIT_ = inter // 128
    x = nc.dram_tensor("x", [TPC // 128, 4, 128, 512], F32, kind="ExternalInput").ap()
    gwT = nc.dram_tensor("gwT", [NIT_, 128, D], w_dtype, kind="ExternalInput").ap()
    uwT = nc.dram_tensor("uwT", [NIT_, 128, D], w_dtype, kind="ExternalInput").ap()
    dwT = nc.dram_tensor("dwT", [4, NIT_, 128, 512], w_dtype, kind="ExternalInput").ap()
    scales = None
    if with_scales:
        gs = nc.dram_tensor("gs", [inter], F32, kind="ExternalInput").ap()
        us = nc.dram_tensor("us", [inter], F32, kind="ExternalInput").ap()
        ds = nc.dram_tensor("ds", [D], F32, kind="ExternalInput").ap()
        scales = (gs, us, ds)
    o_x = nc.dram_tensor("o_x", [TPC // 128, 4, 128, 512], F32,
                         kind="ExternalOutput").ap()
    dbg_spec = [("lnT0", [128, TPC], BF16), ("h0", [128, TPC], BF16),
                ("gw0", [128, 2048], BF16), ("hall", [inter, TPC], BF16)]
    dbg = {nm: nc.dram_tensor(nm, sh, dt, kind="ExternalOutput").ap()
           for nm, sh, dt in dbg_spec} if debug else None

    with ExitStack() as ctx:
        b = Builder(nc, ctx)
        sems = {k: b.sem(k) for k in ("TE", "VE", "SE", "CS")}
        sems["XS"] = b.slot_sems("XS", 2)
        consts = common_const_setup(b, need_identb=True, need_identf=with_scales,
                                    const_vals=(1e-5,))
        consts["pT2"] = [b.psum("pT0", [128, 1024], BF16),
                         b.psum("pT1", [128, 1024], BF16)]
        refs = build_ffn_program(nc, b, sems, consts, x, o_x, gwT, uwT, dwT,
                                 scales, inter, half=half)
        if debug:
            DBG = b.sem("DBG")
            b.wait("gpsimd", sems["VE"])
            b.wait("gpsimd", sems["TE"])
            for nm, src in (("lnT0", refs["lnT"][0]), ("h0", refs["h"][0]),
                            ("gw0", refs["gw_buf"][0])):
                b.inc(b.g.dma_start(out=dbg[nm], in_=src[:]), DBG, 16)
            for ic in range(inter // 128):
                b.inc(b.g.dma_start(out=dbg["hall"][ic * 128:(ic + 1) * 128, :],
                                    in_=refs["h"][ic][:]), DBG, 16)
            b.wait("gpsimd", DBG)
        b.finalize()
    return nc



_prog_cache = {}


def _cached(key, fn):
    if key not in _prog_cache:
        _prog_cache[key] = fn()
    return _prog_cache[key]


def tile_gu(w):
    inter = w.shape[0]
    return np.ascontiguousarray(
        w.reshape(inter // 128, 128, 16, 128).transpose(0, 3, 2, 1)
         .reshape(inter // 128, 128, D))


def tile_dw(w):
    inter = w.shape[1]
    return np.ascontiguousarray(
        w.reshape(4, 512, inter // 128, 128).transpose(0, 2, 3, 1))


def tile_x(x):
    return np.ascontiguousarray(
        x.reshape(TPC // 128, 128, 4, 512).transpose(0, 2, 1, 3))


def untile_x(x4):
    return np.ascontiguousarray(
        np.asarray(x4).transpose(0, 2, 1, 3).reshape(TPC, D))


def run_ffn_f1(x_full, w, half=0.5):
    inter = 5504
    nc = _cached(("ffn", inter, False, "bf16"),
                 lambda: build_ffn_launch(inter, False, BF16, half=half))
    in_maps = []
    for c in range(NCORES):
        in_maps.append({
            "x": tile_x(x_full[c * TPC:(c + 1) * TPC]),
            "gwT": w["g"], "uwT": w["u"], "dwT": w["d"],
        })
    res = run_bass_kernel_spmd(nc, in_maps, core_ids=list(range(NCORES)))
    return np.concatenate([untile_x(r["o_x"]) for r in res.results], axis=0)


def run_ffn_f0(x_full, gw, uw, dw, gs, us, ds, half=0.5, debug=False):
    inter = gw.shape[0]
    nc = _cached(("ffn", inter, True, "f32", debug),
                 lambda: build_ffn_launch(inter, True, F32, half=half,
                                          debug=debug))
    gwT = tile_gu(gw)
    uwT = tile_gu(uw)
    dwT = tile_dw(dw)
    in_maps = []
    for c in range(NCORES):
        in_maps.append({
            "x": tile_x(x_full[c * TPC:(c + 1) * TPC]),
            "gwT": gwT, "uwT": uwT, "dwT": dwT,
            "gs": gs, "us": us, "ds": ds,
        })
    res = run_bass_kernel_spmd(nc, in_maps, core_ids=list(range(NCORES)))
    if debug:
        return res
    return np.concatenate([untile_x(r["o_x"]) for r in res.results], axis=0)





NH0 = 16
NKV0 = 4
HALO = 512
NTOK0 = TPC + HALO


def emit_attn0(b, consts, sems, psums, io, tag="a0"):
    nc = b.nc
    TE, VE, SE = sems["TE"], sems["VE"], sems["SE"]
    xh = io["xh"]
    wq, wk, wv, wo = io["wq"], io["wk"], io["wv"], io["wo"]
    lam, cosr, sinr, mask = io["lam"], io["cosr"], io["sinr"], io["mask"]
    WS = b.slot_sems(tag + "WS", 3)
    WOS = b.slot_sems(tag + "WOS", 4)
    MS = b.slot_sems(tag + "MS", 2)
    XE = b.slot_sems(tag + "XE", 2)
    SS = b.slot_sems(tag + "SS", 2)
    ONE = b.sem(tag + "ONE")
    if True:
        pA, pB, pAx, pBx, pO, pW = psums[0:6]
        pT0, pT1 = consts["pT2"]

        lnT = [b.sbuf(tag + f"lnT{c}", [128, NTOK0], BF16) for c in range(16)]
        att = [lnT[h][:, 0:TPC] for h in range(16)]
        qh = [b.sbuf(tag + f"qh{h}", [128, TPC], BF16) for h in range(NH0)]
        kh = [b.sbuf(tag + f"kh{h}", [128, NTOK0], BF16) for h in range(NKV0)]
        vT = [b.sbuf(tag + f"vT{h}", [128, NTOK0], BF16) for h in range(NKV0)]
        wbuf = [b.sbuf(tag + f"wbuf{j}", [128, D], BF16) for j in range(3)]
        wobuf = [b.sbuf(tag + f"wobuf{j}", [128, 512], BF16) for j in range(4)]
        mbuf = [b.sbuf(tag + f"mbuf{j}", [128, 640], BF16) for j in range(2)]
        cos_s = b.sbuf(tag + "cos_s", [128, NTOK0], BF16)
        sin_s = b.sbuf(tag + "sin_s", [128, NTOK0], BF16)
        qstage = b.sbuf(tag + "qstage", [128, 512], F32)
        rstage = b.sbuf(tag + "rstage", [128, 512], F32)
        t1s = b.sbuf(tag + "t1s", [128, 512], F32)
        vstage = b.sbuf(tag + "vstage", [128, 512], BF16)
        def ph2(nm, shape, dt):
            return [b.sbuf(tag + nm + str(s_), shape, dt) for s_ in range(2)]
        m1 = ph2("m1", [128, 640], F32)
        m2 = ph2("m2", [128, 640], F32)
        e1 = ph2("e1", [128, 640], F32)
        e2 = ph2("e2", [128, 640], F32)
        pb = b.sbuf(tag + "pb", [128, 640], BF16)
        pbT = b.sbuf(tag + "pbT", [128, 640], BF16)
        Z1 = ph2("Z1", [128, 1], F32)
        Z2 = ph2("Z2", [128, 1], F32)
        cc1 = b.sbuf(tag + "cc1", [128, 1], F32)
        Rr = b.sbuf(tag + "Rr", [128, 1], F32)
        den = b.sbuf(tag + "den", [128, 1], F32)
        lamrow = b.sbuf(tag + "lamrow", [1, NH0], F32)
        nlam = b.sbuf(tag + "nlam", [128, NH0], F32)
        ones1 = b.sbuf(tag + "ones1", [1, 128], F32)
        xe_buf = [b.sbuf(tag + f"xe{j}", [128, 512], F32) for j in range(2)]
        oe_buf = [b.sbuf(tag + f"oe{j}", [128, 512], F32) for j in range(2)]

        n_one = 0
        for dst, src in ((cos_s, cosr), (sin_s, sinr)):
            b.inc(b.y.dma_start(out=dst[:], in_=src), ONE, 16)
            n_one += 16
        b.inc(b.y.dma_start(out=lamrow[:], in_=lam[None, :]), ONE, 16)
        n_one += 16
        b.wait("vector", ONE, n_one)
        b.inc(b.v.memset(ones1[:], 1.0), VE)
        b.fence("vector")
        b.wait("tensor", VE)
        mmn = b.t.matmul(out=pW[:, 0:NH0], lhsT=ones1[:], rhs=lamrow[:],
                         start=True, stop=True)
        b.inc(mmn, TE)
        b.wait("vector", TE)
        b.inc(b.v.tensor_scalar(out=nlam[:], in0=pW[:, 0:NH0], scalar1=-1.0,
                                scalar2=None, op0=ALU.mult), VE)
        b.fence("vector")
        b.wait("tensor", VE)

        emit_ln_transpose(b, consts, xh, lnT, NTOK0, sems, name=tag + "ln")
        b.wait("tensor", VE)

        wload_n = [0, 0, 0]

        def load_w(src, idx, j):
            b.inc(b.y.dma_start(out=wbuf[j][:], in_=src[idx]), WS[j], 16)
            wload_n[j] += 1

        te_proj = []

        plan = ([("q", h) for h in range(NH0)] +
                [("k", h) for h in range(NKV0)] +
                [("v", h) for h in range(NKV0)])
        srcs = {"q": wq, "k": wk, "v": wv}
        load_w(srcs[plan[0][0]], plan[0][1], 0)
        load_w(srcs[plan[1][0]], plan[1][1], 1)
        for pi, (kind, h) in enumerate(plan):
            j = pi % 3
            b.wait("tensor", WS[j], wload_n[j] * 16)
            if pi + 2 < len(plan):
                if pi >= 1:
                    b.wait("sync", TE, te_proj[pi - 1])
                load_w(srcs[plan[pi + 2][0]], plan[pi + 2][1], (pi + 2) % 3)
            nslab = 2 if kind == "q" else 3
            off = HALO if kind == "q" else 0
            for tsl in range(nslab):
                ps = pA if tsl % 2 == 0 else pB
                if pi > 0 or tsl > 1:
                    b.wait("tensor", VE)
                mm = None
                for c in range(16):
                    mm = b.t.matmul(out=ps[:],
                                    lhsT=wbuf[j][:, c * 128:(c + 1) * 128],
                                    rhs=lnT[c][:, off + tsl * 512:off + (tsl + 1) * 512],
                                    start=(c == 0), stop=(c == 15))
                b.inc(mm, TE)
                b.wait("vector", TE)
                col0 = tsl * 512
                if kind == "v":
                    b.inc(b.v.tensor_copy(out=vstage[:], in_=ps[:]), VE)
                    b.fence("vector")
                    for m in range(4):
                        k_ = tsl * 4 + m
                        b.wait("tensor", VE)
                        tr = b.t.transpose(out=consts["pT2"][k_ % 2][:, 0:128],
                                           in_=vstage[:, m * 128:(m + 1) * 128],
                                           identity=consts["identb"][:])
                        b.inc(tr, TE)
                        b.wait("vector", TE)
                        b.inc(b.v.tensor_copy(
                            out=vT[h][:, (tsl * 4 + m) * 128:(tsl * 4 + m + 1) * 128],
                            in_=consts["pT2"][k_ % 2][:, 0:128]), VE)
                else:
                    dst = qh[h] if kind == "q" else kh[h]
                    cos_sl = cos_s[:, off + col0:off + col0 + 512]
                    sin_sl = sin_s[:, off + col0:off + col0 + 512]
                    b.inc(b.v.tensor_copy(out=qstage[:], in_=ps[:]), VE)
                    b.fence("vector")
                    b.inc(b.v.tensor_scalar(out=rstage[64:128, :],
                                            in0=qstage[0:64, :], scalar1=1.0,
                                            scalar2=None, op0=ALU.mult), VE)
                    b.inc(b.v.tensor_scalar(out=rstage[0:64, :],
                                            in0=qstage[64:128, :], scalar1=-1.0,
                                            scalar2=None, op0=ALU.mult), VE)
                    b.fence("vector")
                    b.inc(b.v.tensor_mul(out=t1s[:], in0=qstage[:], in1=cos_sl), VE)
                    b.inc(b.v.tensor_mul(out=rstage[:], in0=rstage[:], in1=sin_sl), VE)
                    b.fence("vector")
                    b.inc(b.v.tensor_add(out=dst[:, col0:col0 + 512],
                                         in0=t1s[:], in1=rstage[:]), VE)
                b.fence("vector")
            te_proj.append(TE.count)
        b.wait("tensor", VE)

        mload = [0, 0]
        b.inc(b.y.dma_start(out=mbuf[0][:], in_=mask[0]), MS[0], 16)
        mload[0] += 1
        ve_att = {}
        mevac = {}
        expi = {}
        pbts = {}
        mmsc = {}
        trl = {}

        def stage1(i):
            qb, h = i // NH0, i % NH0
            jm = qb % 2
            kv = h // 4
            s = i % 2
            qsl = qh[h][:, qb * 128:(qb + 1) * 128]
            band0 = qb * 128
            if h == 0:
                b.wait("tensor", MS[jm], mload[jm] * 16)
                if qb + 1 < 8:
                    if qb >= 1:
                        b.wait("sync", VE, mevac[qb * NH0 - 1])
                    b.inc(b.y.dma_start(out=mbuf[(qb + 1) % 2][:],
                                        in_=mask[qb + 1]),
                          MS[(qb + 1) % 2], 16)
                    mload[(qb + 1) % 2] += 1
            if i >= 1:
                b.wait("tensor", VE, mevac[i - 1])
            b.t.matmul(out=pA[:], lhsT=qsl[0:64, :],
                       rhs=kh[kv][0:64, band0:band0 + 512],
                       start=True, stop=True)
            b.t.matmul(out=pAx[:, 0:128], lhsT=qsl[0:64, :],
                       rhs=kh[kv][0:64, band0 + 512:band0 + 640],
                       start=True, stop=True)
            b.t.matmul(out=pB[:], lhsT=qsl[64:128, :],
                       rhs=kh[kv][64:128, band0:band0 + 512],
                       start=True, stop=True)
            mm = b.t.matmul(out=pBx[:, 0:128], lhsT=qsl[64:128, :],
                            rhs=kh[kv][64:128, band0 + 512:band0 + 640],
                            start=True, stop=True)
            mmsc[i] = b.inc(mm, TE)
            b.wait("vector", TE, mmsc[i])
            if i >= 2:
                b.wait("vector", SE, expi[i - 2])
            b.inc(b.v.scalar_tensor_tensor(out=m1[s][:, 0:512], in0=pA[:],
                                           scalar=0.125,
                                           in1=mbuf[jm][:, 0:512],
                                           op0=ALU.mult, op1=ALU.add), VE)
            b.inc(b.v.scalar_tensor_tensor(out=m1[s][:, 512:640],
                                           in0=pAx[:, 0:128],
                                           scalar=0.125,
                                           in1=mbuf[jm][:, 512:640],
                                           op0=ALU.mult, op1=ALU.add), VE)
            b.inc(b.v.scalar_tensor_tensor(out=m2[s][:, 0:512], in0=pB[:],
                                           scalar=0.125,
                                           in1=mbuf[jm][:, 0:512],
                                           op0=ALU.mult, op1=ALU.add), VE)
            mevac[i] = b.inc(b.v.scalar_tensor_tensor(
                out=m2[s][:, 512:640], in0=pBx[:, 0:128], scalar=0.125,
                in1=mbuf[jm][:, 512:640], op0=ALU.mult, op1=ALU.add), VE)
            b.fence("vector")
            b.wait("scalar", VE, mevac[i])
            if i >= 2:
                b.wait("scalar", VE, pbts[i - 2])
            nb = consts["cc"][-20.0]
            b.inc(b.s.activation(out=e1[s][:], in_=m1[s][:], func=AF.Exp,
                                 bias=nb[:], accum_out=Z1[s][:]), SE)
            expi[i] = b.inc(b.s.activation(out=e2[s][:], in_=m2[s][:],
                                           func=AF.Exp, bias=nb[:],
                                           accum_out=Z2[s][:]), SE)

        def stage2(i):
            qb, h = i // NH0, i % NH0
            kv = h // 4
            s = i % 2
            b.wait("vector", SE, expi[i])
            b.inc(b.v.reciprocal(out=cc1[:], in_=Z2[s][:]), VE)
            b.fence("vector")
            b.inc(b.v.tensor_mul(out=cc1[:], in0=cc1[:], in1=Z1[s][:]), VE)
            b.fence("vector")
            b.inc(b.v.tensor_scalar(out=cc1[:], in0=cc1[:],
                                    scalar1=nlam[:, h:h + 1],
                                    scalar2=None, op0=ALU.mult), VE)
            b.fence("vector")
            mst = b.inc(b.v.scalar_tensor_tensor(
                out=m1[s][:], in0=e2[s][:], scalar=cc1[:], in1=e1[s][:],
                op0=ALU.mult, op1=ALU.add), VE)
            b.fence("vector")
            b.wait("scalar", VE, mst)
            rl = b.inc(b.s.activation(out=e2[s][:], in_=m1[s][:],
                                      func=AF.Relu, accum_out=Rr[:]), SE)
            b.wait("vector", SE, rl)
            b.inc(b.v.scalar_tensor_tensor(out=den[:], in0=Z1[s][:],
                                           scalar=1e-6, in1=Rr[:],
                                           op0=ALU.mult, op1=ALU.add), VE)
            b.fence("vector")
            b.inc(b.v.reciprocal(out=den[:], in_=den[:]), VE)
            b.fence("vector")
            if i >= 1:
                b.wait("vector", TE, trl[i - 1])
            pbts[i] = b.inc(b.v.tensor_scalar(out=pb[:], in0=e2[s][:],
                                              scalar1=den[:], scalar2=None,
                                              op0=ALU.mult), VE)
            b.fence("vector")
            evm = {}
            trm = 0
            b.wait("tensor", VE, pbts[i])
            for m in range(5):
                if m >= 2:
                    b.wait("tensor", VE, evm[m - 2])
                trm = b.inc(b.t.transpose(
                    out=consts["pT2"][m % 2][:, 0:128],
                    in_=pb[:, m * 128:(m + 1) * 128],
                    identity=consts["identb"][:]), TE)
                b.wait("vector", TE, trm)
                evm[m] = b.inc(b.v.tensor_copy(
                    out=pbT[:, m * 128:(m + 1) * 128],
                    in_=consts["pT2"][m % 2][:, 0:128]), VE)
            trl[i] = trm
            b.fence("vector")
            b.wait("tensor", VE, evm[4])
            mm = None
            for m in range(5):
                mm = b.t.matmul(out=pO[:, 0:128],
                                lhsT=vT[kv][:, (qb + m) * 128:(qb + m + 1) * 128],
                                rhs=pbT[:, m * 128:(m + 1) * 128],
                                start=(m == 0), stop=(m == 4))
            vmm = b.inc(mm, TE)
            b.wait("vector", TE, vmm)
            ev = b.v.tensor_copy(out=att[h][:, qb * 128:(qb + 1) * 128],
                                 in_=pO[:, 0:128])
            ve_att[i] = b.inc(ev, VE)
            b.fence("vector")

        for i in range(8 * NH0):
            stage1(i)
            if i >= 1:
                stage2(i - 1)
        stage2(8 * NH0 - 1)
        b.wait("tensor", VE)

        woload = [0, 0, 0, 0]

        def load_wo(cc, os_, j):
            b.inc(b.y.dma_start(out=wobuf[j][:], in_=wo[cc, os_]), WOS[j], 16)
            woload[j] += 1

        te_wo = []
        ve_epi = []
        ld_idx = 0
        for tg in range(2):
            for os_ in range(4):
                for cc in range(16):
                    j = ld_idx % 4
                    if ld_idx >= 2:
                        b.wait("sync", TE, te_wo[ld_idx - 2])
                    load_wo(cc, os_, j)
                    b.wait("tensor", WOS[j], woload[j] * 16)
                    if cc == 0 and ld_idx > 0:
                        b.wait("tensor", VE)
                    mm = None
                    for tt in range(4):
                        t_abs = tg * 4 + tt
                        ps = (pA, pB, pAx, pBx)[tt]
                        mm = b.t.matmul(out=ps[:],
                                        lhsT=att[cc][:, t_abs * 128:(t_abs + 1) * 128],
                                        rhs=wobuf[j][:],
                                        start=(cc == 0), stop=(cc == 15))
                    b.inc(mm, TE)
                    te_wo.append(TE.count)
                    ld_idx += 1
                for tt in range(4):
                    t_abs = tg * 4 + tt
                    jj = tt % 2
                    if len(ve_epi) >= 2:
                        b.wait("sync", VE, ve_epi[-2])
                    dx = b.y.dma_start(out=xe_buf[jj][:],
                                       in_=io["resid"](t_abs, os_))
                    b.inc(dx, XE[jj], 16)
                    b.wait("vector", XE[jj])
                    b.wait("vector", TE)
                    if len(ve_epi) >= 2:
                        b.wait("vector", SS[jj])
                    ps = (pA, pB, pAx, pBx)[tt]
                    av = b.v.tensor_add(out=oe_buf[jj][:], in0=ps[:],
                                        in1=xe_buf[jj][:])
                    ve_epi.append(b.inc(av, VE))
                    b.wait("sync", VE)
                    so = b.y.dma_start(out=io["out"](t_abs, os_),
                                       in_=oe_buf[jj][:])
                    b.inc(so, SS[jj], 16)
        b.wait("sync", SS[0])
        b.wait("sync", SS[1])
    return {"SS": SS}


def build_attn0_launch():
    nc = bass.Bass("TRN2", target_bir_lowering=False, debug=False)
    xh = nc.dram_tensor("xh", [NTOK0 // 128, 4, 128, 512], F32,
                        kind="ExternalInput").ap()
    wq = nc.dram_tensor("wq", [NH0, 128, D], F32, kind="ExternalInput").ap()
    wk = nc.dram_tensor("wk", [NKV0, 128, D], F32, kind="ExternalInput").ap()
    wv = nc.dram_tensor("wv", [NKV0, 128, D], F32, kind="ExternalInput").ap()
    wo = nc.dram_tensor("wo", [16, 4, 128, 512], F32, kind="ExternalInput").ap()
    lam = nc.dram_tensor("lam", [NH0], F32, kind="ExternalInput").ap()
    cosr = nc.dram_tensor("cosr", [128, NTOK0], BF16, kind="ExternalInput").ap()
    sinr = nc.dram_tensor("sinr", [128, NTOK0], BF16, kind="ExternalInput").ap()
    mask = nc.dram_tensor("mask", [8, 128, 640], BF16, kind="ExternalInput").ap()
    o_x = nc.dram_tensor("o_x", [TPC // 128, 4, 128, 512], F32,
                         kind="ExternalOutput").ap()
    with ExitStack() as ctx:
        b = Builder(nc, ctx)
        sems = {k: b.sem(k) for k in ("TE", "VE", "SE")}
        sems["XS"] = b.slot_sems("XS", 2)
        consts = common_const_setup(b, need_identb=True,
                                    const_vals=(1e-5, -20.0))
        consts["pT2"] = [b.psum("pT0", [128, 1024], BF16),
                        b.psum("pT1", [128, 1024], BF16)]
        psums = [b.psum(f"ps{i}", [128, 512], F32) for i in range(6)]
        PREP = b.sem("PREP")
        wqb = nc.dram_tensor("wqb", [NH0, 128, D], BF16, kind="Internal").ap()
        wkb = nc.dram_tensor("wkb", [NKV0, 128, D], BF16, kind="Internal").ap()
        wvb = nc.dram_tensor("wvb", [NKV0, 128, D], BF16, kind="Internal").ap()
        wob = nc.dram_tensor("wob", [16, 4, 128, 512], BF16, kind="Internal").ap()
        prep_n = 0
        for src, dst, nq in ((wq, wqb, 4), (wk, wkb, 2), (wv, wvb, 2),
                             (wo, wob, 4)):
            for q in range(nq):
                lo = (src.shape[0] // nq) * q
                hi = (src.shape[0] // nq) * (q + 1)
                b.inc(b.g.dma_start(out=dst[lo:hi], in_=src[lo:hi]), PREP, 16)
                prep_n += 16
        b.wait("sync", PREP, prep_n)
        io = {"xh": xh, "wq": wqb, "wk": wkb, "wv": wvb, "wo": wob,
              "lam": lam, "cosr": cosr, "sinr": sinr, "mask": mask,
              "resid": lambda t, o_: xh[HALO // 128 + t, o_],
              "out": lambda t, o_: o_x[t, o_]}
        emit_attn0(b, consts, sems, psums, io)
        b.finalize()
    return nc


def attn0_host_inputs(ins, x1_full):
    wq4 = np.ascontiguousarray(
        ins["L0_attn_Wq"].reshape(16, 128, 16, 128).transpose(2, 1, 0, 3)
        .reshape(16, 128, D))
    wk4 = np.ascontiguousarray(
        ins["L0_attn_Wk"].reshape(16, 128, 4, 128).transpose(2, 1, 0, 3)
        .reshape(4, 128, D))
    wv4 = np.ascontiguousarray(
        ins["L0_attn_Wv"].reshape(16, 128, 4, 128).transpose(2, 1, 0, 3)
        .reshape(4, 128, D))
    wo4 = np.ascontiguousarray(
        ins["L0_attn_Wo"].reshape(16, 128, 4, 512).transpose(0, 2, 1, 3))
    inv = (10000.0 ** (-(np.arange(64, dtype=np.float64) * 2.0 / 128.0)))
    in_maps = []
    for c in range(NCORES):
        g0 = c * TPC
        xs = np.zeros((NTOK0, D), np.float32)
        lo = max(0, g0 - HALO)
        xs[HALO - (g0 - lo):HALO] = x1_full[lo:g0]
        xs[HALO:] = x1_full[g0:g0 + TPC]
        xh = np.ascontiguousarray(
            xs.reshape(NTOK0 // 128, 128, 4, 512).transpose(0, 2, 1, 3))
        pos = np.arange(g0 - HALO, g0 + TPC, dtype=np.float64)
        ang = pos[None, :] * inv[:, None]
        cosr = np.concatenate([np.cos(ang), np.cos(ang)], axis=0)
        sinr = np.concatenate([np.sin(ang), np.sin(ang)], axis=0)
        r = np.arange(128)[None, :, None]
        cix = np.arange(640)[None, None, :]
        qb = np.arange(8)[:, None, None]
        gj = g0 - HALO + qb * 128 + cix
        allowed = (cix >= r + 1) & (cix <= r + 512) & (gj >= 0)
        maskv = np.where(allowed, np.float32(0), np.float32(-1e30))
        in_maps.append({
            "xh": xh, "wq": wq4, "wk": wk4, "wv": wv4, "wo": wo4,
            "lam": ins["L0_attn_lam"],
            "cosr": cosr.astype(BF), "sinr": sinr.astype(BF),
            "mask": maskv.astype(BF),
        })
    return in_maps



GCH = 11264


def emit_ghost(b, sems, psums, io, tag="gh"):
    nc = b.nc
    TE, VE, SE = sems["TE"], sems["VE"], sems["SE"]
    lut = io["lut"]
    idxs = io["idxs"]
    outs = io["outs"]
    NCH = GCH // 1024
    ONE = b.sem(tag + "ONE")
    IS = b.slot_sems(tag + "IS", 2)
    OS = b.slot_sems(tag + "OS", 2)
    if True:
        lutrow = b.sbuf(tag + "lutrow", [1, 16], F32)
        lutbc = b.sbuf(tag + "lutbc", [128, 16], F32)
        ones1 = b.sbuf(tag + "ones1", [1, 128], F32)
        ibuf = [b.sbuf(tag + f"ibuf{j}", [128, 1024], I32) for j in range(2)]
        ibf = b.sbuf(tag + "ibf", [128, 1024], BF16)
        tmp = b.sbuf(tag + "tmp", [128, 1024], BF16)
        acc = [b.sbuf(tag + f"acc{j}", [128, 1024], BF16) for j in range(2)]
        pW = psums[0]

        b.inc(b.y.dma_start(out=lutrow[:], in_=lut[None, :]), ONE, 16)
        b.wait("vector", ONE, 16)
        b.inc(b.v.memset(ones1[:], 1.0), VE)
        b.fence("vector")
        b.wait("tensor", VE)
        b.inc(b.t.matmul(out=pW[:, 0:16], lhsT=ones1[:], rhs=lutrow[:],
                         start=True, stop=True), TE)
        b.wait("vector", TE)
        b.inc(b.v.tensor_copy(out=lutbc[:], in_=pW[:, 0:16]), VE)
        b.fence("vector")

        iload = [0, 0]
        ve_done = []
        ci = 0
        for nm in ("g", "u", "d"):
            for ch in range(NCH):
                j = ci % 2
                if ci >= 2:
                    b.wait("sync", VE, ve_done[ci - 2])
                d = b.y.dma_start(out=ibuf[j][:],
                                  in_=idxs[nm][:, ch * 1024:(ch + 1) * 1024])
                b.inc(d, IS[j], 16)
                iload[j] += 1
                b.wait("vector", IS[j], iload[j] * 16)
                if ci >= 2:
                    b.wait("vector", OS[j])
                b.inc(b.v.tensor_copy(out=ibf[:], in_=ibuf[j][:]), VE)
                b.fence("vector")
                for v in range(16):
                    b.inc(b.v.tensor_scalar(out=tmp[:], in0=ibf[:],
                                            scalar1=float(v),
                                            scalar2=lutbc[:, v:v + 1],
                                            op0=ALU.is_equal, op1=ALU.mult), VE)
                    b.fence("vector")
                    if v == 0:
                        b.inc(b.v.tensor_copy(out=acc[j][:], in_=tmp[:]), VE)
                    else:
                        b.inc(b.v.tensor_add(out=acc[j][:], in0=acc[j][:],
                                             in1=tmp[:]), VE)
                    b.fence("vector")
                ve_done.append(VE.count)
                b.wait("sync", VE)
                so = b.y.dma_start(out=outs[nm][:, ch * 1024:(ch + 1) * 1024],
                                   in_=acc[j][:])
                b.inc(so, OS[j], 16)
                ci += 1
        b.wait("sync", OS[0])
        b.wait("sync", OS[1])
    return {"OS": OS}


def build_ghost_launch():
    nc = bass.Bass("TRN2", target_bir_lowering=False, debug=False)
    lut = nc.dram_tensor("lut", [16], F32, kind="ExternalInput").ap()
    idxs = {}
    outs = {}
    for nm in ("g", "u", "d"):
        idxs[nm] = nc.dram_tensor("idx_" + nm, [128, GCH], I32,
                                  kind="ExternalInput").ap()
        outs[nm] = nc.dram_tensor("out_" + nm, [128, GCH], BF16,
                                  kind="ExternalOutput").ap()
    with ExitStack() as ctx:
        b = Builder(nc, ctx)
        sems = {k: b.sem(k) for k in ("TE", "VE", "SE")}
        psums = [b.psum("ps0", [128, 512], F32)]
        emit_ghost(b, sems, psums, {"lut": lut, "idxs": idxs, "outs": outs})
        b.finalize()
    return nc


def ghost_host_inputs(ins):
    NIT1 = 5504 // 128
    gidx = tile_gu(ins["L1_ffn_gate_idx"]).reshape(-1)
    uidx = tile_gu(ins["L1_ffn_up_idx"]).reshape(-1)
    didx = tile_dw(ins["L1_ffn_down_idx"]).reshape(-1)
    n = gidx.size
    per = 128 * GCH
    in_maps = []
    for c in range(NCORES):
        m = {"lut": ins["L1_ffn_lut"]}
        for nm, arr in (("g", gidx), ("u", uidx), ("d", didx)):
            sl = np.zeros(per, np.int32)
            lo = c * per
            hi = min(n, (c + 1) * per)
            if hi > lo:
                sl[:hi - lo] = arr[lo:hi]
            m["idx_" + nm] = sl.reshape(128, GCH)
        in_maps.append(m)
    return in_maps


def ghost_reassemble(results):
    NIT1 = 5504 // 128
    out = {}
    for nm, shape in (("g", (NIT1, 128, D)), ("u", (NIT1, 128, D)),
                      ("d", (4, NIT1, 128, 512))):
        flat = np.concatenate([np.asarray(r["out_" + nm]).reshape(-1)
                               for r in results])
        n = int(np.prod(shape))
        out[nm] = flat[:n].reshape(shape)
    return out



DN_SB = 512
DN_MARKERS = False
DN_MARKS = []
PHASE_MARKERS = False
PHASE_MARKS = []
DN_NSB = T // DN_SB
DN_C = 16
DN_NCH = DN_SB // DN_C
SQ128 = float(np.sqrt(128.0))


def emit_deltanet(b, consts, sems, psums, io, tag="dn"):
    nc = b.nc
    TE, VE, SE = sems["TE"], sems["VE"], sems["SE"]
    lnsl = io["ln_slice"]
    wqh, wkh, wvh = io["wqh"], io["wkh"], io["wvh"]
    wfa, wga, wfbh, wgbh = io["wfa"], io["wga"], io["wfbh"], io["wgbh"]
    wbt, convw, nega, dtb, nw = (io["wbt"], io["convw"], io["nega"],
                                 io["dtb"], io["nw"])
    ldram = nc.dram_tensor(tag + "ldram", [2, DN_NCH * 256], BF16,
                           kind="Internal").ap()
    ONE = b.sem(tag + "ONE")
    LS = b.slot_sems(tag + "LS", 6)
    OGS = b.slot_sems(tag + "OGS", 2)
    LWR = b.sem(tag + "LWR")
    LRD = b.slot_sems(tag + "LRD", 2)
    identb = consts["identb"]
    identf = consts["identf"]
    eps6 = consts["cc"][1e-6]
    pA, pB, pC, pD, pX, pO = psums[0:6]
    pT0, pT1 = consts["pT2"]

    def ph(nm, shape, dt):
        return [b.sbuf(tag + nm + str(h), shape, dt) for h in range(2)]

    DN_MARKS.clear()
    mkbuf = b.sbuf(tag + "mkbuf", [1, 16], F32)

    def mark(nm):
        if not DN_MARKERS:
            return
        DN_MARKS.append(nm)
        b.wait("gpsimd", VE)
        b.wait("gpsimd", TE)
        b.inc(b.g.memset(mkbuf[0:1, 0:1], 1.0), b.sem(tag + "MK"))

    ones1 = b.sbuf(tag + "ones1", [1, 128], F32)
    onescolb = b.sbuf(tag + "onescolb", [128, 1], BF16)
    onesrep = b.sbuf(tag + "onesrep", [128, 16], BF16)
    ones16f = b.sbuf(tag + "ones16f", [128, 16], F32)
    acol = b.sbuf(tag + "acol", [128, 2], F32)
    dtbc = b.sbuf(tag + "dtbc", [128, 2], F32)
    nwcol = b.sbuf(tag + "nwcol", [128, 1], F32)
    negarow = b.sbuf(tag + "negarow", [1, 2], F32)
    dtbrow = b.sbuf(tag + "dtbrow", [2, 128], F32)
    nwrow = b.sbuf(tag + "nwrow", [1, 128], F32)
    S = ph("S", [128, 128], F32)
    Sall = ph("Sall", [128, (DN_NCH + 1) * 128], BF16)
    qx = {nm: ph("qx" + nm, [128, DN_SB + 3], F32) for nm in "qkv"}
    qn = ph("qn", [128, DN_SB], BF16)
    kn = ph("kn", [128, DN_SB], BF16)
    vv = ph("vv", [128, DN_SB], F32)
    kb = ph("kb", [128, DN_SB], BF16)
    ktb = ph("ktb", [128, DN_SB], BF16)
    vb = ph("vb", [128, DN_SB], BF16)
    qt_ = ph("qt", [128, DN_SB], BF16)
    kbar = ph("kbar", [128, DN_SB], BF16)
    ball = ph("ball", [128, DN_SB], F32)
    eblast = ph("eblast", [128, DN_NCH], F32)
    g2r = ph("g2r", [128, DN_SB], BF16)
    rawh = ph("raw", [128, DN_SB], F32)
    brow = ph("brow", [1, DN_SB], F32)
    og_sb = ph("ogsb", [128, DN_SB], BF16)
    silq = ph("silq", [128, DN_SB], F32)
    silk = ph("silk", [128, DN_SB], F32)
    wlog = b.sbuf(tag + "wlog", [128, DN_SB], F32)
    eb_t = b.sbuf(tag + "ebt", [128, DN_SB], BF16)
    ebl_t = b.sbuf(tag + "eblt", [128, DN_SB], BF16)
    fa_t = b.sbuf(tag + "fa_t", [128, DN_SB], BF16)
    ga_t = b.sbuf(tag + "ga_t", [128, DN_SB], BF16)
    sp1 = b.sbuf(tag + "sp1", [128, DN_SB], F32)
    sp2 = b.sbuf(tag + "sp2", [128, DN_SB], F32)
    dr_ = b.sbuf(tag + "dr_", [128, DN_SB], F32)
    w1_ = b.sbuf(tag + "w1_", [128, DN_SB], F32)
    scr_bf = b.sbuf(tag + "scrbf", [128, DN_SB], BF16)
    scr2_bf = b.sbuf(tag + "scr2bf", [128, DN_SB], BF16)
    rr2 = [b.sbuf(tag + f"rr{i}", [1, DN_SB], F32) for i in range(2)]
    rstdr = b.sbuf(tag + "rstdr", [1, DN_SB], F32)
    Arep = b.sbuf(tag + "Arep", [16, DN_NCH * 256], BF16)
    Tt = ph("Tt", [16, DN_NCH * 16], BF16)
    tac = b.sbuf(tag + "tac", [16, DN_NCH], F32)
    dr3 = b.sbuf(tag + "dr3", [16, DN_NCH * 16], F32)
    LtT = ph("LtT", [16, DN_NCH * 16], BF16)
    Lsr = b.sbuf(tag + "Lsr", [1, 1024], BF16)
    lnbuf = [b.sbuf(tag + f"lnbuf{j}", [128, DN_SB], BF16) for j in range(6)]
    wfb_s = ph("wfb_s", [128, 128], BF16)
    wgb_s = ph("wgb_s", [128, 128], BF16)
    wbt_s = b.sbuf(tag + "wbt_s", [128, 32], BF16)
    cv_s = b.sbuf(tag + "cv_s", [128, 24], F32)

    n_one = [0]

    def oneload(dst_ap, src_ap):
        b.inc(b.g.dma_start(out=dst_ap, in_=src_ap), ONE, 16)
        n_one[0] += 16

    dnwbf = nc.dram_tensor(tag + "wbf", [8, 128, 2048], BF16,
                           kind="Internal").ap()
    wsrcs = [wqh[0], wqh[1], wkh[0], wkh[1], wvh[0], wvh[1], wfa, wga]
    for i2, s2 in enumerate(wsrcs):
        oneload(dnwbf[i2].rearrange("p (c i) -> p c i", c=16),
                s2.rearrange("c p i -> p c i"))
    for h in range(2):
        oneload(wfb_s[h][:], wfbh[h])
        oneload(wgb_s[h][:], wgbh[h])
    oneload(wbt_s[:].rearrange("p (c i) -> p c i", c=16),
            wbt.rearrange("c p i -> p c i"))
    oneload(cv_s[:].rearrange("p (n h i) -> p n h i", n=3, h=2),
            convw.rearrange("n h p i -> p n h i"))
    oneload(negarow[:], nega)
    oneload(dtbrow[:], dtb)
    oneload(nwrow[:], nw)
    b.wait("vector", ONE, n_one[0])
    b.wait("scalar", ONE, n_one[0])
    b.wait("tensor", ONE, n_one[0])
    for t_, v_ in ((ones1, 1.0), (onescolb, 1.0), (onesrep, 1.0),
                   (ones16f, 1.0)):
        b.inc(b.v.memset(t_[:], v_), VE)
    for h in range(2):
        b.inc(b.v.memset(S[h][:], 0.0), VE)
        b.inc(b.v.memset(Sall[h][:, 0:128], 0.0), VE)
        for nm in "qkv":
            b.inc(b.v.memset(qx[nm][h][:, 0:3], 0.0), VE)
    b.fence("vector")
    b.wait("scalar", VE)
    b.inc(b.s.activation(out=negarow[:], in_=negarow[:], func=AF.Exp), SE)
    b.wait("vector", SE)
    b.inc(b.v.tensor_scalar(out=negarow[:], in0=negarow[:], scalar1=-1.0,
                            scalar2=None, op0=ALU.mult), VE)
    b.fence("vector")
    b.wait("tensor", VE)
    b.inc(b.t.matmul(out=pA[:, 0:2], lhsT=ones1[:], rhs=negarow[:],
                     start=True, stop=True), TE)
    b.wait("vector", TE)
    b.inc(b.v.tensor_copy(out=acol[:], in_=pA[:, 0:2]), VE)
    b.fence("vector")
    b.wait("tensor", VE)
    tr = b.t.transpose(out=pX[:, 0:2], in_=dtbrow[:],
                       identity=consts["identf"][0:2, 0:2])
    b.inc(tr, TE)
    b.wait("vector", TE)
    b.inc(b.v.tensor_copy(out=dtbc[:], in_=pX[:, 0:2]), VE)
    b.fence("vector")
    b.wait("tensor", VE)
    tr = b.t.transpose(out=pD[:, 0:1], in_=nwrow[:],
                       identity=consts["identf"][0:1, 0:1])
    b.inc(tr, TE)
    b.wait("vector", TE)
    b.inc(b.v.tensor_copy(out=nwcol[:], in_=pD[:, 0:1]), VE)
    b.fence("vector")
    b.wait("tensor", VE)

    ln_n = [0] * 6
    ln_te = [0] * 6
    og_n = [0, 0]
    lwr_n = [0]
    lrd_n = [0, 0]
    wt_n = [0]
    WT = b.sem(tag + "WT")
    sall_carry = [0, 0]
    trec_last = [0, 0]
    cve_last = [0]
    Pd_stride = DN_NCH * 256

    def cview(t_ap, inner=16):
        return t_ap.rearrange("p (c x) -> p c x", x=inner)

    for sb in range(DN_NSB):
        t0 = sb * DN_SB

        mark(f"A_{sb}")
        wctx = ExitStack()
        _oldctx = b.ctx
        b.ctx = wctx
        wq_s = [b.sbuf(tag + f"wq{sb}_{h}", [128, 2048], BF16)
                for h in range(2)]
        wk_s = [b.sbuf(tag + f"wk{sb}_{h}", [128, 2048], BF16)
                for h in range(2)]
        wv_s = [b.sbuf(tag + f"wv{sb}_{h}", [128, 2048], BF16)
                for h in range(2)]
        wfa_s = b.sbuf(tag + f"wfa{sb}", [128, 2048], BF16)
        wga_s = b.sbuf(tag + f"wga{sb}", [128, 2048], BF16)
        b.ctx = _oldctx
        if sb == 0:
            b.wait("sync", ONE, n_one[0])
        else:
            b.wait("sync", TE)
            b.wait("sync", VE)
        for i2, dst in enumerate([wq_s[0], wq_s[1], wk_s[0], wk_s[1],
                                  wv_s[0], wv_s[1], wfa_s, wga_s]):
            b.inc(b.y.dma_start(out=dst[:], in_=dnwbf[i2]), WT, 16)
            wt_n[0] += 1
        b.wait("tensor", WT, wt_n[0] * 16)
        groups = [[("q", 0, pA), ("k", 0, pB), ("v", 0, pC)],
                  [("q", 1, pA), ("k", 1, pB), ("v", 1, pC)],
                  [("fa", None, pA), ("ga", None, pB),
                   ("bt0", None, pC), ("bt1", None, pD)]]
        for gi, grp in enumerate(groups):
            for c in range(16):
                j = (sb * 48 + gi * 16 + c) % 6
                if ln_n[j] >= 1:
                    b.wait("sync", TE, ln_te[j])
                d = b.y.dma_start(out=lnbuf[j][:],
                                  in_=lnsl(c, t0, DN_SB))
                b.inc(d, LS[j], 16)
                ln_n[j] += 1
                b.wait("tensor", LS[j], ln_n[j] * 16)
                if c == 0 and (sb > 0 or gi > 0):
                    b.wait("tensor", VE)
                    b.wait("tensor", SE)
                mm = None
                for kind, h, ps in grp:
                    if kind == "q":
                        lhs, oap = wq_s[h][:, c * 128:(c + 1) * 128], ps[:]
                    elif kind == "k":
                        lhs, oap = wk_s[h][:, c * 128:(c + 1) * 128], ps[:]
                    elif kind == "v":
                        lhs, oap = wv_s[h][:, c * 128:(c + 1) * 128], ps[:]
                    elif kind == "fa":
                        lhs, oap = wfa_s[:, c * 128:(c + 1) * 128], ps[:]
                    elif kind == "ga":
                        lhs, oap = wga_s[:, c * 128:(c + 1) * 128], ps[:]
                    elif kind == "bt0":
                        lhs, oap = wbt_s[:, c * 2:c * 2 + 1], ps[0:1, :]
                    else:
                        lhs, oap = wbt_s[:, c * 2 + 1:c * 2 + 2], ps[0:1, :]
                    mm = b.t.matmul(out=oap, lhsT=lhs, rhs=lnbuf[j][:],
                                    start=(c == 0), stop=(c == 15))
                b.inc(mm, TE)
                ln_te[j] = TE.count

            b.wait("vector", TE)
            b.wait("scalar", TE)
            if gi < 2:
                h = gi
                for nm, ps in (("q", pA), ("k", pB), ("v", pC)):
                    if sb > 0:
                        b.inc(b.v.tensor_copy(out=qx[nm][h][:, 0:3],
                                              in_=qx[nm][h][:, DN_SB:DN_SB + 3]), VE)
                        b.fence("vector")
                    b.inc(b.v.tensor_copy(out=qx[nm][h][:, 3:DN_SB + 3],
                                          in_=ps[:]), VE)
                    b.fence("vector")
            else:
                b.inc(b.v.tensor_copy(out=fa_t[:], in_=pA[:]), VE)
                b.inc(b.v.tensor_copy(out=ga_t[:], in_=pB[:]), VE)
                b.fence("vector")
                b.wait("scalar", VE)
                b.inc(b.s.activation(out=brow[0][:], in_=pC[0:1, :],
                                     func=AF.Sigmoid), SE)
                b.inc(b.s.activation(out=brow[1][:], in_=pD[0:1, :],
                                     func=AF.Sigmoid), SE)
                b.wait("vector", SE)

        convdst = {(0, "q"): dr_, (0, "k"): w1_, (0, "v"): sp2,
                   (1, "q"): sp1, (1, "k"): rawh[0], (1, "v"): rawh[1]}
        for h in range(2):
            for nm in "qkv":
                ni = "qkv".index(nm)
                dst = convdst[(h, nm)]

                def cvcol(tap):
                    o = ni * 8 + h * 4 + tap
                    return cv_s[:, o:o + 1]
                b.inc(b.v.tensor_scalar(out=dst[:],
                                        in0=qx[nm][h][:, 0:DN_SB],
                                        scalar1=cvcol(0), scalar2=None,
                                        op0=ALU.mult), VE)
                b.fence("vector")
                for tap in range(1, 4):
                    b.inc(b.v.scalar_tensor_tensor(
                        out=dst[:], in0=qx[nm][h][:, tap:tap + DN_SB],
                        scalar=cvcol(tap), in1=dst[:],
                        op0=ALU.mult, op1=ALU.add), VE)
                    b.fence("vector")
        b.wait("scalar", VE)
        for h in range(2):
            b.inc(b.s.activation(out=silq[h][:], in_=convdst[(h, "q")][:],
                                 func=AF.Silu), SE)
            b.inc(b.s.activation(out=silk[h][:], in_=convdst[(h, "k")][:],
                                 func=AF.Silu), SE)
            b.inc(b.s.activation(out=vv[h][:], in_=convdst[(h, "v")][:],
                                 func=AF.Silu), SE)
        b.fence("scalar")
        for wave in range(2):
            srcs = (silq[0], silq[1]) if wave == 0 else (silk[0], silk[1])
            dstn = qn if wave == 0 else kn
            if wave == 1:
                b.wait("scalar", TE)
                b.wait("scalar", VE)
            b.inc(b.s.activation(out=scr_bf[:], in_=srcs[0][:],
                                 func=AF.Square), SE)
            b.inc(b.s.activation(out=scr2_bf[:], in_=srcs[1][:],
                                 func=AF.Square), SE)
            b.wait("tensor", SE)
            b.wait("tensor", VE)
            b.inc(b.t.matmul(out=pA[0:1, :], lhsT=onescolb[:],
                             rhs=scr_bf[:], start=True, stop=True), TE)
            b.inc(b.t.matmul(out=pB[0:1, :], lhsT=onescolb[:],
                             rhs=scr2_bf[:], start=True, stop=True), TE)
            b.wait("scalar", TE)
            b.inc(b.s.activation(out=rr2[0][:], in_=pA[0:1, :],
                                 func=AF.Sqrt, bias=eps6[0:1, :]), SE)
            b.inc(b.s.activation(out=rr2[1][:], in_=pB[0:1, :],
                                 func=AF.Sqrt, bias=eps6[0:1, :]), SE)
            b.wait("vector", SE)
            if wave == 0:
                b.inc(b.v.tensor_scalar(out=rr2[0][:], in0=rr2[0][:],
                                        scalar1=SQ128, scalar2=None,
                                        op0=ALU.mult), VE)
                b.inc(b.v.tensor_scalar(out=rr2[1][:], in0=rr2[1][:],
                                        scalar1=SQ128, scalar2=None,
                                        op0=ALU.mult), VE)
                b.fence("vector")
            b.inc(b.v.reciprocal(out=rr2[0][:], in_=rr2[0][:]), VE)
            b.fence("vector")
            b.inc(b.v.reciprocal(out=rr2[1][:], in_=rr2[1][:]), VE)
            b.fence("vector")
            b.wait("tensor", VE)
            b.inc(b.t.matmul(out=pC[:], lhsT=ones1[:], rhs=rr2[0][:],
                             start=True, stop=True), TE)
            b.inc(b.t.matmul(out=pD[:], lhsT=ones1[:], rhs=rr2[1][:],
                             start=True, stop=True), TE)
            b.wait("vector", TE)
            b.inc(b.v.tensor_mul(out=dstn[0][:], in0=srcs[0][:],
                                 in1=pC[:]), VE)
            b.inc(b.v.tensor_mul(out=dstn[1][:], in0=srcs[1][:],
                                 in1=pD[:]), VE)
            b.fence("vector")
        b.wait("tensor", VE)
        wctx.close()

        mark(f"B_{sb}")
        b.inc(b.t.matmul(out=pA[:], lhsT=wfb_s[0][:], rhs=fa_t[:],
                         start=True, stop=True), TE)
        b.inc(b.t.matmul(out=pB[:], lhsT=wfb_s[1][:], rhs=fa_t[:],
                         start=True, stop=True), TE)
        b.inc(b.t.matmul(out=pC[:], lhsT=wgb_s[0][:], rhs=ga_t[:],
                         start=True, stop=True), TE)
        b.inc(b.t.matmul(out=pD[:], lhsT=wgb_s[1][:], rhs=ga_t[:],
                         start=True, stop=True), TE)
        b.wait("vector", TE)
        b.inc(b.v.tensor_scalar(out=rawh[0][:], in0=pA[:],
                                scalar1=dtbc[:, 0:1], scalar2=None,
                                op0=ALU.add), VE)
        b.inc(b.v.tensor_scalar(out=rawh[1][:], in0=pB[:],
                                scalar1=dtbc[:, 1:2], scalar2=None,
                                op0=ALU.add), VE)
        b.inc(b.v.tensor_copy(out=g2r[0][:], in_=pC[:]), VE)
        b.inc(b.v.tensor_copy(out=g2r[1][:], in_=pD[:]), VE)
        b.fence("vector")
        b.wait("scalar", VE)
        b.inc(b.s.activation(out=sp1[:], in_=rawh[0][:], func=AF.Abs), SE)
        b.inc(b.s.activation(out=sp2[:], in_=rawh[1][:], func=AF.Abs), SE)
        b.fence("scalar")
        b.inc(b.s.activation(out=sp1[:], in_=sp1[:], func=AF.Exp,
                             scale=-1.0), SE)
        b.inc(b.s.activation(out=sp2[:], in_=sp2[:], func=AF.Exp,
                             scale=-1.0), SE)
        b.fence("scalar")
        b.inc(b.s.activation(out=sp1[:], in_=sp1[:], func=AF.Ln, bias=1.0), SE)
        b.inc(b.s.activation(out=sp2[:], in_=sp2[:], func=AF.Ln, bias=1.0), SE)
        b.wait("vector", SE)
        b.wait("tensor", VE)
        for h in range(2):
            sph = sp1 if h == 0 else sp2
            b.inc(b.v.tensor_scalar(out=dr_[:], in0=rawh[h][:], scalar1=0.0,
                                    scalar2=None, op0=ALU.max), VE)
            b.fence("vector")
            b.inc(b.v.tensor_add(out=dr_[:], in0=dr_[:], in1=sph[:]), VE)
            b.fence("vector")
            b.inc(b.v.tensor_scalar(out=wlog[:], in0=dr_[:],
                                    scalar1=acol[:, h:h + 1], scalar2=None,
                                    op0=ALU.mult), VE)
            b.fence("vector")
            for i in range(DN_NCH):
                sl = slice(i * 16, (i + 1) * 16)
                b.inc(b.v.tensor_tensor_scan(
                    out=ball[h][:, sl], data0=ones16f[:],
                    data1=wlog[:, sl], initial=0.0,
                    op0=ALU.mult, op1=ALU.add), VE)
            b.fence("vector")
            for i in range(DN_NCH):
                sl = slice(i * 16, (i + 1) * 16)
                b.inc(b.v.tensor_scalar(
                    out=w1_[:, sl], in0=ball[h][:, sl],
                    scalar1=ball[h][:, i * 16 + 15:i * 16 + 16],
                    scalar2=None, op0=ALU.subtract), VE)
            b.fence("vector")
            if h == 1:
                b.wait("scalar", VE)
            else:
                b.wait("scalar", VE)
            b.inc(b.s.activation(out=eb_t[:], in_=ball[h][:],
                                 func=AF.Exp), SE)
            b.inc(b.s.activation(out=ebl_t[:], in_=w1_[:], func=AF.Exp,
                                 scale=-1.0), SE)
            b.inc(b.s.activation(
                out=eblast[h][:].unsqueeze(2),
                in_=ball[h][:].rearrange("p (c x) -> p c x", x=16)[:, :, 15:16],
                func=AF.Exp), SE)
            b.wait("tensor", VE)
            b.inc(b.t.matmul(out=pX[:], lhsT=ones1[:], rhs=brow[h][:],
                             start=True, stop=True), TE)
            b.wait("vector", TE)
            b.inc(b.v.tensor_mul(out=kb[h][:], in0=kn[h][:], in1=pX[:]), VE)
            b.inc(b.v.tensor_mul(out=vb[h][:], in0=vv[h][:], in1=pX[:]), VE)
            b.fence("vector")
            b.wait("vector", SE)
            b.inc(b.v.tensor_mul(out=qt_[h][:], in0=qn[h][:], in1=eb_t[:]), VE)
            b.inc(b.v.tensor_mul(out=kbar[h][:], in0=kn[h][:],
                                 in1=ebl_t[:]), VE)
            b.fence("vector")
            b.inc(b.v.tensor_mul(out=ktb[h][:], in0=kb[h][:], in1=eb_t[:]), VE)
            b.fence("vector")
            b.wait("tensor", VE)

        with b.scope():
            Pd = b.sbuf(tag + f"Pd{sb}", [128, DN_NCH * 256], BF16)
            Pl = b.sbuf(tag + f"Pl{sb}", [128, DN_NCH * 256], BF16)

            def strided_out(handle, r, width):
                return _BassAP(handle, 16 * r,
                               [[Pd_stride, 128], [256, DN_NCH], [17, width]])

            b.wait("vector", TE)
            b.inc(b.v.memset(Pl[:], 0.0), VE)
            b.fence("vector")
            GP = b.sem(tag + "GP")
            b.wait("gpsimd", TE)
            b.inc(b.g.memset(Pd[:], 0.0), GP)
            b.wait("vector", GP)
            for h in range(2):
                mark(f"rp_{sb}_{h}")
                if h == 1:
                    b.wait("vector", TE)
                b.inc(b.v.tensor_tensor(out=strided_out(Pl, 0, 16),
                                        in0=cview(qn[h][:]),
                                        in1=cview(kn[h][:]),
                                        op=ALU.mult), VE)
                drs = [dr_, sp1]
                w1s = [w1_, sp2]
                exp_c = {}
                for r in range(1, 16):
                    w = 16 - r
                    s2 = r % 2
                    bv3 = cview(ball[h][:])
                    if r >= 3:
                        b.wait("vector", SE, exp_c[r - 2])
                    b.inc(b.v.tensor_tensor(
                        out=cview(drs[s2][:])[:, :, 0:w],
                        in0=bv3[:, :, r:16], in1=bv3[:, :, 0:w],
                        op=ALU.subtract), VE)
                    b.fence("vector")
                    b.wait("scalar", VE)
                    e = b.s.activation(out=cview(w1s[s2][:])[:, :, 0:w],
                                       in_=cview(drs[s2][:])[:, :, 0:w],
                                       func=AF.Exp)
                    exp_c[r] = b.inc(e, SE)
                    b.wait("vector", SE, exp_c[r])
                    b.inc(b.v.tensor_tensor(
                        out=cview(drs[s2][:])[:, :, 0:w],
                        in0=cview(w1s[s2][:])[:, :, 0:w],
                        in1=cview(kn[h][:])[:, :, 0:w],
                        op=ALU.mult), VE)
                    b.fence("vector")
                    wv_ = cview(drs[s2][:])[:, :, 0:w]
                    b.inc(b.v.tensor_tensor(out=strided_out(Pd, r, w),
                                            in0=wv_,
                                            in1=cview(kb[h][:])[:, :, r:16],
                                            op=ALU.mult), VE)
                    b.inc(b.v.tensor_tensor(out=strided_out(Pl, r, w),
                                            in0=wv_,
                                            in1=cview(qn[h][:])[:, :, r:16],
                                            op=ALU.mult), VE)
                    b.fence("vector")
                mark(f"DK_{sb}_{h}")
                b.wait("tensor", VE)
                veA = {}
                seL = {}
                for half in range(16):
                    bnkA = pC if half % 2 == 0 else pX
                    bnkL = pD if half % 2 == 0 else pO
                    if half >= 2:
                        b.wait("tensor", VE, veA[half - 2])
                    mm = b.t.matmul(out=bnkA[0:16, :], lhsT=onesrep[:],
                                    rhs=Pd[:, half * 512:(half + 1) * 512],
                                    start=True, stop=True)
                    b.inc(mm, TE)
                    b.wait("vector", TE)
                    cp = b.v.tensor_copy(
                        out=Arep[0:16, half * 512:(half + 1) * 512],
                        in_=bnkA[0:16, :])
                    veA[half] = b.inc(cp, VE)
                    cve_last[0] = veA[half]
                    if half >= 2:
                        b.wait("tensor", SE, seL[half - 2])
                    mm = b.t.matmul(out=bnkL[0:16, :], lhsT=onesrep[:],
                                    rhs=Pl[:, half * 512:(half + 1) * 512],
                                    start=True, stop=True)
                    b.inc(mm, TE)
                    b.wait("scalar", TE)
                    if half % 2 == 0 and lwr_n[0] > 0:
                        b.wait("scalar", LWR, lwr_n[0] * 16)
                    cl = b.s.activation(out=Lsr[0:1,
                                                (half % 2) * 512:
                                                (half % 2 + 1) * 512],
                                        in_=bnkL[0:1, :], func=AF.Copy)
                    seL[half] = b.inc(cl, SE)
                    if half % 2 == 1:
                        b.wait("sync", SE)
                        if half == 1 and lrd_n[h] > 0:
                            b.wait("sync", LRD[h], lrd_n[h] * 16)
                        dsh = b.y.dma_start(
                            out=ldram[h][None, (half - 1) * 512:
                                         (half + 1) * 512],
                            in_=Lsr[0:1, :])
                        b.inc(dsh, LWR, 16)
                        lwr_n[0] += 1
                b.wait("gpsimd", LWR, lwr_n[0] * 16)
                dget = b.g.dma_start(
                    out=LtT[h][:],
                    in_=ldram[h].rearrange("(c t s) -> s (c t)", s=16, t=16),
                    allow_slow_non_contiguous=True)
                b.inc(dget, LRD[h], 16)
                lrd_n[h] += 1
                mark(f"Trec_{sb}_{h}")
                AV = Arep[:].rearrange("p (g x) -> p g x", x=256)
                TV = Tt[h][:].rearrange("p (g x) -> p g x", x=16)
                D3 = dr3[:].rearrange("p (g x) -> p g x", x=16)
                b.inc(b.v.memset(Tt[h][:], 0.0), VE)
                b.fence("vector")
                for i in range(16):
                    if i > 0:
                        b.inc(b.v.tensor_tensor(
                            out=D3[:, :, 0:i],
                            in0=AV[:, :, i * 16:i * 16 + i],
                            in1=TV[:, :, 0:i], op=ALU.mult), VE)
                        b.fence("vector")
                        b.inc(b.v.reduce_sum(out=tac[:], in_=D3[:, :, 0:i],
                                             axis=AX), VE)
                        b.fence("vector")
                        trec_last[h] = b.inc(b.v.tensor_scalar(
                            out=TV[:, :, i:i + 1],
                            in0=tac[:].unsqueeze(2), scalar1=-1.0,
                            scalar2=identf[0:16, i:i + 1],
                            op0=ALU.mult, op1=ALU.add), VE)
                    else:
                        b.inc(b.v.tensor_scalar(
                            out=TV[:, :, 0:1],
                            in0=TV[:, :, 0:1], scalar1=0.0,
                            scalar2=identf[0:16, 0:1],
                            op0=ALU.mult, op1=ALU.add), VE)
                    b.fence("vector")

        with b.scope():
            FT0 = ph(f"FT0_{sb}_", [128, DN_NCH * 128], BF16)
            kbTa = ph(f"kbTa{sb}_", [16, DN_NCH * 128], BF16)
            Wt = ph(f"Wt{sb}_", [16, DN_NCH * 128], BF16)
            nUt = ph(f"nUt{sb}_", [128, DN_NCH * 16], BF16)
            Ob = ph(f"Ob{sb}_", [16, 2 * 512], BF16)
            dlt = ph(f"dlt{sb}_", [16, 2 * 128], BF16)
            nust = b.sbuf(tag + f"nust{sb}", [16, DN_NCH * 128], BF16)
            rhsT = b.sbuf(tag + f"rhsT{sb}", [16, 8 * 256], BF16)
            b.wait("vector", TE)
            st = {}

            def eh_stream(h):
                trp = pT0 if h == 0 else pT1
                fbk = pC if h == 0 else pD
                gbk = pA if h == 0 else pB
                obk = pO if h == 0 else pX
                FTh, kbh, Wth = FT0[h][:], kbTa[h][:], Wt[h][:]
                nUh, Obh, dlh = nUt[h][:], Ob[h][:], dlt[h][:]
                b.wait("tensor", LRD[h], lrd_n[h] * 16)
                b.wait("tensor", SE)
                b.wait("tensor", VE, max(cve_last[0], trec_last[h]))
                mark(f"E1_{sb}_{h}")
                ev = 0
                for grp in range(4):
                    if ev:
                        b.wait("tensor", SE, ev)
                    tc = 0
                    for j2 in range(8):
                        c = grp * 8 + j2
                        tc = b.inc(b.t.transpose(
                            out=trp[0:16, j2 * 128:(j2 + 1) * 128],
                            in_=kbar[h][:, c * 16:(c + 1) * 16],
                            identity=identb[:]), TE)
                    b.wait("scalar", TE, tc)
                    ev = b.inc(b.s.activation(
                        out=kbh[:, grp * 1024:(grp + 1) * 1024],
                        in_=trp[0:16, 0:1024], func=AF.Copy), SE)
                    yield "e"
                mark(f"E2_{sb}_{h}")
                if st.get("nustfree"):
                    b.wait("scalar", TE, st["nustfree"])
                sv = {}
                for grp in range(4):
                    b.wait("tensor", SE, ev)
                    tc = 0
                    for j2 in range(8):
                        c = grp * 8 + j2
                        tc = b.inc(b.t.transpose(
                            out=trp[0:16, j2 * 128:(j2 + 1) * 128],
                            in_=vb[h][:, c * 16:(c + 1) * 16],
                            identity=identb[:]), TE)
                    b.wait("scalar", TE, tc)
                    ev = b.inc(b.s.activation(
                        out=rhsT[:].rearrange("p (g x) -> p g x",
                                              x=256)[:, :, 0:128],
                        in_=trp[0:16, 0:1024].rearrange(
                            "p (g x) -> p g x", x=128), func=AF.Copy), SE)
                    yield "e"
                    b.wait("tensor", SE, ev)
                    tc = 0
                    for j2 in range(8):
                        c = grp * 8 + j2
                        tc = b.inc(b.t.transpose(
                            out=trp[0:16, j2 * 128:(j2 + 1) * 128],
                            in_=ktb[h][:, c * 16:(c + 1) * 16],
                            identity=identb[:]), TE)
                    b.wait("scalar", TE, tc)
                    ev = b.inc(b.s.activation(
                        out=rhsT[:].rearrange("p (g x) -> p g x",
                                              x=256)[:, :, 128:256],
                        in_=trp[0:16, 0:1024].rearrange(
                            "p (g x) -> p g x", x=128), func=AF.Copy), SE)
                    yield "e"
                    b.wait("tensor", SE, ev)
                    for pair in range(4):
                        c0 = grp * 8 + pair * 2
                        bank = fbk if pair % 2 == 0 else gbk
                        if pair >= 2:
                            b.wait("tensor", SE, sv[pair - 2])
                        mmi = 0
                        for cc in (c0, c0 + 1):
                            g2 = cc - c0
                            mmi = b.inc(b.t.matmul(
                                out=bank[0:16, g2 * 256:(g2 + 1) * 256],
                                lhsT=Tt[h][:, cc * 16:(cc + 1) * 16],
                                rhs=rhsT[0:16, (cc % 8) * 256:
                                         ((cc % 8) + 1) * 256],
                                start=True, stop=True), TE)
                        b.wait("scalar", TE, mmi)
                        bv = bank[0:16, 0:512].rearrange(
                            "p (g x) -> p g x", x=256)
                        b.inc(b.s.activation(
                            out=Wth[:, c0 * 128:(c0 + 2) * 128].rearrange(
                                "p (g x) -> p g x", x=128),
                            in_=bv[:, :, 0:128], func=AF.Copy), SE)
                        sv[pair] = b.inc(b.s.activation(
                            out=nust[:, c0 * 128:(c0 + 2) * 128].rearrange(
                                "p (g x) -> p g x", x=128),
                            in_=bv[:, :, 128:256], func=AF.Copy,
                            scale=-1.0), SE)
                    yield "e"
                mark(f"E3_{sb}_{h}")
                b.wait("tensor", SE, sv[3])
                tc = 0
                for c in range(DN_NCH):
                    tc = b.inc(b.t.transpose(
                        out=trp[:, c * 16:(c + 1) * 16],
                        in_=nust[0:16, c * 128:(c + 1) * 128],
                        identity=identb[0:16, 0:16]), TE)
                b.wait("scalar", TE, tc)
                ev = b.inc(b.s.activation(out=nUh[:], in_=trp[:, 0:512],
                                          func=AF.Copy), SE)
                yield "e"
                mark(f"F_{sb}_{h}")
                fe = 0
                fmm = 0
                for c in range(DN_NCH):
                    if fe:
                        b.wait("tensor", SE, fe)
                    fmm = b.inc(b.t.matmul(
                        out=fbk[:, 0:128],
                        lhsT=nust[0:16, c * 128:(c + 1) * 128],
                        rhs=kbh[0:16, c * 128:(c + 1) * 128],
                        start=True, stop=True), TE)
                    b.wait("scalar", TE, fmm)
                    fe = b.inc(b.s.activation(
                        out=FTh[:, c * 128:(c + 1) * 128],
                        in_=fbk[:, 0:128], func=AF.Copy), SE)
                    if c % 4 == 3:
                        yield "e"
                st["nustfree"] = fmm
                mark(f"G_{sb}_{h}")
                stt_i = 0
                cp_i = sall_carry[h]
                b.wait("tensor", SE, fe)
                for c in range(DN_NCH):
                    if stt_i:
                        b.wait("tensor", VE, stt_i)
                    b.t.matmul(out=gbk[:, 0:128],
                               lhsT=kbh[0:16, c * 128:(c + 1) * 128],
                               rhs=Wth[0:16, c * 128:(c + 1) * 128],
                               start=True, stop=False)
                    if cp_i:
                        b.wait("tensor", VE, cp_i)
                    mmf = b.inc(b.t.matmul(
                        out=gbk[:, 0:128],
                        lhsT=FTh[:, c * 128:(c + 1) * 128],
                        rhs=Sall[h][:, c * 128:(c + 1) * 128],
                        start=False, stop=True), TE)
                    b.wait("vector", TE, mmf)
                    stt_i = b.inc(b.v.scalar_tensor_tensor(
                        out=S[h][:], in0=S[h][:],
                        scalar=eblast[h][:, c:c + 1], in1=gbk[:, 0:128],
                        op0=ALU.mult, op1=ALU.add), VE)
                    b.fence("vector")
                    cp_i = b.inc(b.v.tensor_copy(
                        out=Sall[h][:, (c + 1) * 128:(c + 2) * 128],
                        in_=S[h][:]), VE)
                    b.fence("vector")
                    yield "g"
                mark(f"H_{sb}_{h}")
                ad_i = cp_i
                ev_i = {}
                tr_i = {}
                mmo = 0
                for c in range(DN_NCH):
                    g4 = c // 4
                    if ad_i:
                        b.wait("tensor", VE, ad_i)
                    mmd = b.inc(b.t.matmul(
                        out=gbk[0:16, 0:128],
                        lhsT=nUh[:, c * 16:(c + 1) * 16],
                        rhs=Sall[h][:, c * 128:(c + 1) * 128],
                        start=True, stop=True), TE)
                    b.wait("vector", TE, mmd)
                    ad_i = b.inc(b.v.tensor_add(
                        out=dlh[0:16, (c % 2) * 128:(c % 2 + 1) * 128],
                        in0=gbk[0:16, 0:128],
                        in1=Wth[0:16, c * 128:(c + 1) * 128]), VE)
                    b.fence("vector")
                    if c % 4 == 0 and g4 >= 1:
                        b.wait("tensor", SE, ev_i[g4 - 1])
                    b.t.matmul(
                        out=obk[0:16, (c % 4) * 128:(c % 4 + 1) * 128],
                        lhsT=qt_[h][:, c * 16:(c + 1) * 16],
                        rhs=Sall[h][:, c * 128:(c + 1) * 128],
                        start=True, stop=False)
                    b.wait("tensor", VE, ad_i)
                    mmo = b.inc(b.t.matmul(
                        out=obk[0:16, (c % 4) * 128:(c % 4 + 1) * 128],
                        lhsT=LtT[h][:, c * 16:(c + 1) * 16],
                        rhs=dlh[0:16, (c % 2) * 128:(c % 2 + 1) * 128],
                        start=False, stop=True), TE)
                    if c % 4 == 3:
                        b.wait("scalar", TE, mmo)
                        oslot = g4 % 2
                        if g4 >= 2:
                            b.wait("scalar", TE, tr_i[g4 - 2])
                        ev_i[g4] = b.inc(b.s.activation(
                            out=Obh[0:16, oslot * 512:(oslot + 1) * 512],
                            in_=obk[0:16, 0:512], func=AF.Copy), SE)
                        b.wait("tensor", SE, ev_i[g4])
                        tc = 0
                        for j2 in range(4):
                            cc = g4 * 4 + j2
                            tc = b.inc(b.t.transpose(
                                out=trp[:, cc * 16:(cc + 1) * 16],
                                in_=Obh[0:16,
                                        oslot * 512 + j2 * 128:
                                        oslot * 512 + (j2 + 1) * 128],
                                identity=identb[0:16, 0:16]), TE)
                        tr_i[g4] = tc
                    yield "h"
                b.wait("vector", TE, mmo)
                sall_carry[h] = b.inc(b.v.tensor_copy(
                    out=Sall[h][:, 0:128],
                    in_=Sall[h][:, DN_NCH * 128:(DN_NCH + 1) * 128]), VE)
                b.fence("vector")
                mark(f"epi_{sb}_{h}")
                b.wait("scalar", TE)
                b.wait("scalar", VE)
                b.inc(b.s.activation(out=scr_bf[:], in_=trp[:, 0:512],
                                     func=AF.Square), SE)
                b.wait("tensor", SE)
                b.inc(b.t.matmul(out=fbk[0:1, 0:512], lhsT=onescolb[:],
                                 rhs=scr_bf[:], start=True, stop=True), TE)
                b.wait("scalar", TE)
                b.inc(b.s.activation(out=rstdr[:], in_=fbk[0:1, :],
                                     func=AF.Sqrt, scale=1.0 / 128.0,
                                     bias=eps6[0:1, :]), SE)
                b.fence("scalar")
                b.wait("scalar", VE)
                b.inc(b.s.activation(out=sp1[:], in_=g2r[h][:],
                                     func=AF.Sigmoid), SE)
                b.wait("vector", SE)
                b.inc(b.v.reciprocal(out=rstdr[:], in_=rstdr[:]), VE)
                b.fence("vector")
                b.wait("tensor", VE)
                b.inc(b.t.matmul(out=obk[:], lhsT=ones1[:], rhs=rstdr[:],
                                 start=True, stop=True), TE)
                b.wait("vector", TE)
                b.inc(b.v.tensor_copy(out=rawh[0][:], in_=trp[:, 0:512]), VE)
                b.fence("vector")
                if sb > 0:
                    b.wait("vector", OGS[h], og_n[h] * 16)
                b.inc(b.v.scalar_tensor_tensor(
                    out=sp2[:], in0=rawh[0][:], scalar=nwcol[:],
                    in1=obk[:], op0=ALU.mult, op1=ALU.mult), VE)
                b.fence("vector")
                b.inc(b.v.tensor_mul(out=og_sb[h][:], in0=sp2[:],
                                     in1=sp1[:]), VE)
                b.fence("vector")
                b.wait("sync", VE)
                dso = b.y.dma_start(out=io["og"](h)[:, t0:t0 + DN_SB],
                                    in_=og_sb[h][:])
                b.inc(dso, OGS[h], 16)
                og_n[h] += 1
                yield "z"

            def adv(g2):
                try:
                    return next(g2)
                except StopIteration:
                    return None

            sA = eh_stream(0)
            sB = eh_stream(1)
            tA = adv(sA)
            while tA == "e":
                tA = adv(sA)
            tB = adv(sB)
            while tA is not None and tB is not None:
                tA = adv(sA)
                tB = adv(sB)
            while tA is not None:
                tA = adv(sA)
            while tB is not None:
                tB = adv(sB)

    b.wait("sync", OGS[0], og_n[0] * 16)
    b.wait("sync", OGS[1], og_n[1] * 16)
    return {"OGS": OGS, "og_n": og_n}


def build_deltanet_launch():
    nc = bass.Bass("TRN2", target_bir_lowering=False, debug=False)
    lnT4 = nc.dram_tensor("lnT4", [16, 128, T], BF16, kind="ExternalInput").ap()
    names = [("wqh", [2, 16, 128, 128]), ("wkh", [2, 16, 128, 128]),
             ("wvh", [2, 16, 128, 128]), ("wfa", [16, 128, 128]),
             ("wga", [16, 128, 128]), ("wfbh", [2, 128, 128]),
             ("wgbh", [2, 128, 128]), ("wbt", [16, 128, 2]),
             ("convw", [3, 2, 128, 4]), ("nega", [1, 2]), ("dtb", [2, 128]),
             ("nw", [1, 128])]
    io = {nm: nc.dram_tensor(nm, sh, F32, kind="ExternalInput").ap()
          for nm, sh in names}
    o_og = nc.dram_tensor("o_og", [2, 128, T], BF16, kind="ExternalOutput").ap()
    io["ln_slice"] = lambda c, t0, n: lnT4[c, :, t0:t0 + n]
    io["og"] = lambda h: o_og[h]
    with ExitStack() as ctx:
        b = Builder(nc, ctx)
        sems = {k: b.sem(k) for k in ("TE", "VE", "SE")}
        consts = common_const_setup(b, need_identb=True, need_identf=True,
                                    const_vals=(1e-6,))
        consts["pT2"] = [b.psum("pT0", [128, 1024], BF16),
                        b.psum("pT1", [128, 1024], BF16)]
        psums = [b.psum(f"ps{i}", [128, 512], F32) for i in range(6)]
        emit_deltanet(b, consts, sems, psums, io)
        b.finalize()
    return nc




def build_ln_launch():
    nc = bass.Bass("TRN2", target_bir_lowering=False, debug=False)
    x = nc.dram_tensor("x", [TPC // 128, 4, 128, 512], F32,
                       kind="ExternalInput").ap()
    o = nc.dram_tensor("o", [16, 128, TPC], BF16, kind="ExternalOutput").ap()
    with ExitStack() as ctx:
        b = Builder(nc, ctx)
        sems = {k: b.sem(k) for k in ("TE", "VE", "SE")}
        sems["XS"] = b.slot_sems("XS", 2)
        SO = b.sem("SO")
        consts = common_const_setup(b, need_identb=True, const_vals=(1e-5,))
        consts["pT2"] = [b.psum("pT0", [128, 1024], BF16),
                        b.psum("pT1", [128, 1024], BF16)]
        lnT = [b.sbuf(f"lnT{c}", [128, TPC], BF16) for c in range(16)]
        emit_ln_transpose(b, consts, x, lnT, TPC, sems, name="ln")
        b.wait("gpsimd", sems["VE"])
        n = 0
        for c in range(16):
            b.inc(b.g.dma_start(out=o[c], in_=lnT[c][:]), SO, 16)
            n += 16
        b.wait("gpsimd", SO, n)
        b.finalize()
    return nc




def emit_wo(b, sems, psums, io, tag="wo"):
    TE, VE, SE = sems["TE"], sems["VE"], sems["SE"]
    wob = io["wo"]
    WOS = b.slot_sems(tag + "WOS", 4)
    ATS = b.slot_sems(tag + "ATS", 3)
    XE = b.slot_sems(tag + "XE", 2)
    SS = b.slot_sems(tag + "SS", 2)
    if True:
        pPS = psums[0:4]
        att = [b.sbuf(f"att{j}", [128, TPC], BF16) for j in range(3)]
        wobuf = [b.sbuf(f"wob{j}", [128, 512], BF16) for j in range(4)]
        xe_buf = [b.sbuf(f"xe{j}", [128, 512], F32) for j in range(2)]
        oe_buf = [b.sbuf(f"oe{j}", [128, 512], F32) for j in range(2)]

        at_n = [0, 0, 0]
        wo_n = [0, 0, 0, 0]
        te_cc = []
        ve_epi = []
        ld = 0
        for tg in range(2):
            for os_ in range(4):
                for cc in range(16):
                    ja = ld % 3
                    jw = ld % 4
                    if at_n[ja] >= 1:
                        b.wait("sync", TE, te_cc[-3] if len(te_cc) >= 3 else 0)
                    b.inc(b.y.dma_start(out=att[ja][:], in_=io["oT_chunk"](cc)),
                          ATS[ja], 16)
                    at_n[ja] += 1
                    if wo_n[jw] >= 1:
                        b.wait("sync", TE, te_cc[-4] if len(te_cc) >= 4 else 0)
                    b.inc(b.y.dma_start(out=wobuf[jw][:], in_=wob[cc, os_]),
                          WOS[jw], 16)
                    wo_n[jw] += 1
                    b.wait("tensor", ATS[ja], at_n[ja] * 16)
                    b.wait("tensor", WOS[jw], wo_n[jw] * 16)
                    if cc == 0 and ld > 0:
                        b.wait("tensor", VE)
                    mm = None
                    for tt in range(4):
                        t_abs = tg * 4 + tt
                        mm = b.t.matmul(out=pPS[tt][:],
                                        lhsT=att[ja][:, t_abs * 128:(t_abs + 1) * 128],
                                        rhs=wobuf[jw][:],
                                        start=(cc == 0), stop=(cc == 15))
                    b.inc(mm, TE)
                    te_cc.append(TE.count)
                    ld += 1
                for tt in range(4):
                    t_abs = tg * 4 + tt
                    jj = tt % 2
                    if len(ve_epi) >= 2:
                        b.wait("sync", VE, ve_epi[-2])
                    dx = b.y.dma_start(out=xe_buf[jj][:],
                                       in_=io["resid"](t_abs, os_))
                    b.inc(dx, XE[jj], 16)
                    b.wait("vector", XE[jj])
                    b.wait("vector", TE)
                    if len(ve_epi) >= 2:
                        b.wait("vector", SS[jj])
                    av = b.v.tensor_add(out=oe_buf[jj][:], in0=pPS[tt][:],
                                        in1=xe_buf[jj][:])
                    ve_epi.append(b.inc(av, VE))
                    b.wait("sync", VE)
                    so = b.y.dma_start(out=io["out"](t_abs, os_),
                                       in_=oe_buf[jj][:])
                    b.inc(so, SS[jj], 16)
        b.wait("sync", SS[0])
        b.wait("sync", SS[1])
    return {"SS": SS}


def build_wo_launch():
    nc = bass.Bass("TRN2", target_bir_lowering=False, debug=False)
    x = nc.dram_tensor("x", [TPC // 128, 4, 128, 512], F32,
                       kind="ExternalInput").ap()
    oT = nc.dram_tensor("oT", [16, 128, TPC], BF16, kind="ExternalInput").ap()
    wo = nc.dram_tensor("wo", [16, 4, 128, 512], F32, kind="ExternalInput").ap()
    o_x = nc.dram_tensor("o_x", [TPC // 128, 4, 128, 512], F32,
                         kind="ExternalOutput").ap()
    with ExitStack() as ctx:
        b = Builder(nc, ctx)
        sems = {k: b.sem(k) for k in ("TE", "VE", "SE")}
        PREP = b.sem("PREP")
        wob = nc.dram_tensor("wob", [16, 4, 128, 512], BF16, kind="Internal").ap()
        prep_n = 0
        for q in range(4):
            lo, hi = 4 * q, 4 * (q + 1)
            b.inc(b.g.dma_start(out=wob[lo:hi], in_=wo[lo:hi]), PREP, 16)
            prep_n += 16
        b.wait("sync", PREP, prep_n)
        psums = [b.psum(f"ps{i}", [128, 512], F32) for i in range(4)]
        io = {"oT_chunk": lambda cc: oT[cc], "wo": wob,
              "resid": lambda t, o_: x[t, o_],
              "out": lambda t, o_: o_x[t, o_]}
        emit_wo(b, sems, psums, io)
        b.finalize()
    return nc








class _TileView:

    def __init__(self, flat_ap, shape_tail):
        self.flat = flat_ap
        self.tail = shape_tail
        self.elems = int(np.prod(shape_tail))
        self.dtype = flat_ap.dtype

    def __getitem__(self, idx):
        if isinstance(idx, tuple):
            os_, ic = idx
            nit = 0
            lin = os_ * self.nit + ic
        else:
            lin = idx
        sl = self.flat[lin * self.elems:(lin + 1) * self.elems]
        return sl.rearrange("(p i) -> p i", p=self.tail[0])


def build_fused_launch():
    nc = bass.Bass("TRN2", target_bir_lowering=False, debug=False)
    RG = [[i for i in range(NCORES)]]
    E = lambda nm, sh, dt=F32: nc.dram_tensor(nm, sh, dt, kind="ExternalInput").ap()
    I_ = lambda nm, sh, dt=F32: nc.dram_tensor(nm, sh, dt, kind="Internal").ap()
    NIT0, NIT1 = 5632 // 128, 5504 // 128

    x0 = E("x0", [8, 4, 128, 512])
    f0gw = E("f0gw", [NIT0, 128, D])
    f0uw = E("f0uw", [NIT0, 128, D])
    f0dw = E("f0dw", [4, NIT0, 128, 512])
    f0gs = E("f0gs", [5632])
    f0us = E("f0us", [5632])
    f0ds = E("f0ds", [D])
    a0wq = E("a0wq", [NH0, 128, D])
    a0wk = E("a0wk", [NKV0, 128, D])
    a0wv = E("a0wv", [NKV0, 128, D])
    a0wo = E("a0wo", [16, 4, 128, 512])
    a0lam = E("a0lam", [NH0])
    a0cos = E("a0cos", [128, NTOK0], BF16)
    a0sin = E("a0sin", [128, NTOK0], BF16)
    a0mask = E("a0mask", [8, 128, 640], BF16)
    glut = E("glut", [16])
    gidx = {nm: E("gidx_" + nm, [128, GCH], I32) for nm in ("g", "u", "d")}
    dnio_names = [("wqh", [2, 16, 128, 128]), ("wkh", [2, 16, 128, 128]),
                  ("wvh", [2, 16, 128, 128]), ("wfa", [16, 128, 128]),
                  ("wga", [16, 128, 128]), ("wfbh", [2, 128, 128]),
                  ("wgbh", [2, 128, 128]), ("wbt", [16, 128, 2]),
                  ("convw", [3, 2, 128, 4]), ("nega", [1, 2]),
                  ("dtb", [2, 128]), ("nw", [1, 128])]
    dnw = {nm: E("dn" + nm, sh) for nm, sh in dnio_names}
    wo1 = E("wo1", [16, 4, 128, 512])
    o_x = nc.dram_tensor("o_x", [8, 4, 128, 512], F32, kind="ExternalOutput").ap()

    xres1 = I_("xres1", [8, 4, 128, 512])
    xres2 = I_("xres2", [8, 4, 128, 512])
    xres3 = I_("xres3", [8, 4, 128, 512])
    xres4 = I_("xres4", [8, 4, 128, 512])
    xres5 = I_("xres5", [8, 4, 128, 512])
    agt_in = I_("agt_in", [4 * 128, 2048])
    agt_out = I_("agt_out", [8 * 4 * 128, 2048])
    gh_in = {nm: I_("ghin_" + nm, [128 * GCH], BF16) for nm in ("g", "u", "d")}
    gh_out = {nm: I_("ghout_" + nm, [8 * 128 * GCH], BF16) for nm in ("g", "u", "d")}
    ln_in = I_("ln_in", [16 * 128 * TPC], BF16)
    ln_out = I_("ln_out", [8 * 16 * 128 * TPC], BF16)
    og_in = I_("og_in", [2 * 128 * T], BF16)
    og_out = I_("og_out", [8 * 2 * 128 * T], BF16)

    with ExitStack() as root:
        b = Builder(nc, root)
        sems = {k: b.sem(k) for k in ("TE", "VE", "SE")}
        sems["XS"] = b.slot_sems("XS", 2)
        TE, VE, SE = sems["TE"], sems["VE"], sems["SE"]
        CC = b.sem("CC")
        HB = b.sem("HB")
        consts = common_const_setup(b, need_identb=True, need_identf=True,
                                    const_vals=(1e-5, 1e-6, -20.0))
        consts["pT2"] = [b.psum("pT0", [128, 1024], BF16),
                        b.psum("pT1", [128, 1024], BF16)]
        psums = [b.psum(f"ps{i}", [128, 512], F32) for i in range(6)]

        def barrier(*store_sems):
            for ss in store_sems:
                for engq in ("sync", "gpsimd"):
                    b.wait(engq, ss)
            b.wait("tensor", VE)
            b.wait("tensor", SE)
            b.wait("vector", TE)

        f0bf, F0PREP, f0prep_n = emit_f0_prep(b, nc, f0gw, f0uw, f0dw, "fA")

        b.phase_mark("ghost")
        with b.scope():
            r = emit_ghost(b, sems, psums, {
                "lut": glut, "idxs": gidx,
                "outs": {nm: gh_in[nm].rearrange("(p c) -> p c", p=128)
                         for nm in ("g", "u", "d")}})
        for ss in r["OS"]:
            b.wait("gpsimd", ss)
        for nm in ("g", "u", "d"):
            it = b.g.collective_compute("AllGather", ALU.bypass,
                                        replica_groups=RG,
                                        ins=[gh_in[nm].opt()],
                                        outs=[gh_out[nm].opt()])
            b.inc(it, CC, 1)

        b.phase_mark("ffnA")
        b.wait("sync", F0PREP, f0prep_n)
        b.wait("gpsimd", F0PREP, f0prep_n)
        with b.scope():
            rA = build_ffn_program(nc, b, sems, consts, x0, xres1, f0bf[0],
                                   f0bf[1], f0bf[2], (f0gs, f0us, f0ds),
                                   5632, half=0.5, tag="fA", psums=psums)
        barrier(*rA["SS"])

        b.phase_mark("halo")
        for i in range(4):
            d = b.g.dma_start(out=agt_in[i * 128:(i + 1) * 128, :]
                              .rearrange("p (a c) -> p a c", a=4),
                              in_=xres1[4 + i].rearrange("a p c -> p a c"))
            b.inc(d, HB, 16)
        b.wait("gpsimd", HB)
        it = b.g.collective_compute("AllGather", ALU.bypass, replica_groups=RG,
                                    ins=[agt_in.opt()], outs=[agt_out.opt()])
        b.inc(it, CC, 1)
        b.wait("sync", CC)

        PREPA = b.sem("PREPA")
        wqb = I_("wqb", [NH0, 128, D], BF16)
        wkb = I_("wkb", [NKV0, 128, D], BF16)
        wvb = I_("wvb", [NKV0, 128, D], BF16)
        wob = I_("wob", [16, 4, 128, 512], BF16)
        wo1b = I_("wo1b", [16, 4, 128, 512], BF16)
        prep_n = 0
        for src, dst, nq in ((a0wq, wqb, 4), (a0wk, wkb, 2), (a0wv, wvb, 2),
                             (a0wo, wob, 4), (wo1, wo1b, 4)):
            for q in range(nq):
                lo = (src.shape[0] // nq) * q
                hi = (src.shape[0] // nq) * (q + 1)
                b.inc(b.g.dma_start(out=dst[lo:hi], in_=src[lo:hi]), PREPA, 16)
                prep_n += 16
        b.wait("sync", PREPA, prep_n)

        b.phase_mark("attn0")
        _symcache = {}

        def _pid_syms(eng):
            key = id(eng)
            if key not in _symcache:
                pid = eng.partition_id()
                halo_base = ((pid + 7) % 8) * (512 * 2048)
                tok_off = pid * TPC
                _symcache[key] = (halo_base, tok_off)
            return _symcache[key]

        def halo_ap(i):
            def fn(eng):
                halo_base, _ = _pid_syms(eng)
                return _BassAP(agt_out.tensor, halo_base + i * (128 * 2048),
                               [[2048, 128], [1, 2048]])
            return Deferred(fn, ndim=2)

        ln_aps = [halo_ap(i) for i in range(4)] + [
            xres1[i].rearrange("a p c -> p a c") for i in range(8)]
        with b.scope():
            rB = emit_attn0(b, consts, sems, psums, {
                "xh": ln_aps, "wq": wqb, "wk": wkb, "wv": wvb, "wo": wob,
                "lam": a0lam, "cosr": a0cos, "sinr": a0sin, "mask": a0mask,
                "resid": lambda t, o_: xres1[t, o_],
                "out": lambda t, o_: xres2[t, o_]})
        barrier(*rB["SS"])

        b.phase_mark("ffnC")
        with b.scope():
            rC = build_ffn_program(nc, b, sems, consts, xres2, xres3,
                                   f0bf[0], f0bf[1], f0bf[2],
                                   (f0gs, f0us, f0ds), 5632, half=0.5,
                                   tag="fC", psums=psums)
        barrier(*rC["SS"])

        b.phase_mark("ffnD")
        b.wait("sync", CC)
        b.wait("gpsimd", CC)
        ggw = _TileView(gh_out["g"], (128, D))
        guw = _TileView(gh_out["u"], (128, D))
        gdw = _TileView(gh_out["d"], (128, 512))
        gdw.nit = NIT1
        with b.scope():
            rD = build_ffn_program(nc, b, sems, consts, xres3, xres4,
                                   ggw, guw, gdw, None, 5504, half=0.5,
                                   tag="fD", psums=psums)
        barrier(*rD["SS"])

        b.phase_mark("ln4")
        LNS = b.sem("LNS")
        with b.scope():
            lnT = [b.sbuf(f"e_lnT{c}", [128, TPC], BF16) for c in range(16)]
            emit_ln_transpose(b, consts, xres4, lnT, TPC, sems, name="elng")
            b.wait("gpsimd", VE)
            n = 0
            for c in range(16):
                d = b.g.dma_start(
                    out=ln_in[c * 128 * TPC:(c + 1) * 128 * TPC]
                    .rearrange("(p t) -> p t", p=128), in_=lnT[c][:])
                b.inc(d, LNS, 16)
                n += 16
            b.wait("gpsimd", LNS, n)
        it = b.g.collective_compute("AllGather", ALU.bypass, replica_groups=RG,
                                    ins=[ln_in.opt()], outs=[ln_out.opt()])
        b.inc(it, CC, 1)
        b.wait("sync", CC)

        b.phase_mark("dn")
        lnov = ln_out.rearrange("(r c p t) -> r c p t", r=8, c=16, p=128)
        ogv = og_in.rearrange("(h p t) -> h p t", h=2, p=128)

        def ln_slice(c, t0, n):
            rk = t0 // TPC
            off = t0 % TPC
            return lnov[rk, c, :, off:off + n]

        with b.scope():
            rF = emit_deltanet(b, consts, sems, psums, dict(
                dnw, ln_slice=ln_slice, og=lambda h: ogv[h]))
        for ss in rF["OGS"]:
            b.wait("gpsimd", ss)
        it = b.g.collective_compute("AllGather", ALU.bypass, replica_groups=RG,
                                    ins=[og_in.opt()], outs=[og_out.opt()])
        b.inc(it, CC, 1)
        b.wait("sync", CC)

        b.phase_mark("wo")
        def oT_chunk(cc):
            def fn(eng):
                _, tok_off = _pid_syms(eng)
                return _BassAP(og_out.tensor, tok_off + cc * (128 * T),
                               [[T, 128], [1, TPC]])
            return Deferred(fn, ndim=2)

        with b.scope():
            rG = emit_wo(b, sems, psums, {
                "oT_chunk": oT_chunk, "wo": wo1b,
                "resid": lambda t, o_: xres4[t, o_],
                "out": lambda t, o_: xres5[t, o_]})
        barrier(*rG["SS"])

        b.phase_mark("ffnH")
        with b.scope():
            rH = build_ffn_program(nc, b, sems, consts, xres5, o_x,
                                   ggw, guw, gdw, None, 5504, half=0.5,
                                   tag="fH", psums=psums)
        b.wait("sync", rH["SS"][0])
        b.wait("sync", rH["SS"][1])
        b.finalize()
    return nc


def fused_host_inputs(ins):
    gwT = tile_gu(ins["L0_ffn_gate_w"])
    uwT = tile_gu(ins["L0_ffn_up_w"])
    dwT = tile_dw(ins["L0_ffn_down_w"])
    wq4 = np.ascontiguousarray(
        ins["L0_attn_Wq"].reshape(16, 128, 16, 128).transpose(2, 1, 0, 3)
        .reshape(16, 128, D))
    wk4 = np.ascontiguousarray(
        ins["L0_attn_Wk"].reshape(16, 128, 4, 128).transpose(2, 1, 0, 3)
        .reshape(4, 128, D))
    wv4 = np.ascontiguousarray(
        ins["L0_attn_Wv"].reshape(16, 128, 4, 128).transpose(2, 1, 0, 3)
        .reshape(4, 128, D))
    wo4 = np.ascontiguousarray(
        ins["L0_attn_Wo"].reshape(16, 128, 4, 512).transpose(0, 2, 1, 3))
    wo14 = np.ascontiguousarray(
        ins["L1_attn_Wo"].reshape(16, 128, 4, 512).transpose(0, 2, 1, 3))
    ghost_maps = ghost_host_inputs(ins)
    dn_maps = dn_host_inputs(ins, None)
    inv = (10000.0 ** (-(np.arange(64, dtype=np.float64) * 2.0 / 128.0)))
    x0f = np.ascontiguousarray(ins["L0_attn_x"], dtype=np.float32)[0]
    maps = []
    for c in range(NCORES):
        g0 = c * TPC
        pos = np.arange(g0 - HALO, g0 + TPC, dtype=np.float64)
        ang = pos[None, :] * inv[:, None]
        cosr = np.concatenate([np.cos(ang), np.cos(ang)], axis=0)
        sinr = np.concatenate([np.sin(ang), np.sin(ang)], axis=0)
        r = np.arange(128)[None, :, None]
        cix = np.arange(640)[None, None, :]
        qb = np.arange(8)[:, None, None]
        gj = g0 - HALO + qb * 128 + cix
        allowed = (cix >= r + 1) & (cix <= r + 512) & (gj >= 0)
        maskv = np.where(allowed, np.float32(0), np.float32(-1e30))
        m = {
            "x0": tile_x(x0f[g0:g0 + TPC]),
            "f0gw": gwT, "f0uw": uwT, "f0dw": dwT,
            "f0gs": ins["L0_ffn_gate_s"], "f0us": ins["L0_ffn_up_s"],
            "f0ds": ins["L0_ffn_down_s"],
            "a0wq": wq4, "a0wk": wk4, "a0wv": wv4, "a0wo": wo4,
            "a0lam": ins["L0_attn_lam"],
            "a0cos": cosr.astype(BF), "a0sin": sinr.astype(BF),
            "a0mask": maskv.astype(BF),
            "glut": ins["L1_ffn_lut"],
            "wo1": wo14,
        }
        for nm in ("g", "u", "d"):
            m["gidx_" + nm] = ghost_maps[c]["idx_" + nm]
        for nm, _sh in [("wqh", 0), ("wkh", 0), ("wvh", 0), ("wfa", 0),
                        ("wga", 0), ("wfbh", 0), ("wgbh", 0), ("wbt", 0),
                        ("convw", 0), ("nega", 0), ("dtb", 0), ("nw", 0)]:
            m["dn" + nm] = dn_maps[c][nm]
        maps.append(m)
    return maps


def dn_host_inputs(ins, ln4T):
    wq = ins["L1_attn_Wq"]; wk = ins["L1_attn_Wk"]; wv = ins["L1_attn_Wv"]
    wfa = np.ascontiguousarray(ins["L1_attn_Wfa"].reshape(16, 128, 128))
    wga = np.ascontiguousarray(ins["L1_attn_Wga"].reshape(16, 128, 128))
    wfb = ins["L1_attn_Wfb"]
    wgb = ins["L1_attn_Wgb"]
    wb = ins["L1_attn_Wb"]
    alog = ins["L1_attn_A_log"].reshape(16)
    dtbias = ins["L1_attn_dt_bias"].reshape(16, 128)
    maps = []
    for c in range(NCORES):
        hs = [2 * c, 2 * c + 1]

        def headtile(w):
            out = np.empty((2, 16, 128, 128), np.float32)
            for i, h in enumerate(hs):
                out[i] = w[:, h * 128:(h + 1) * 128].reshape(16, 128, 128)
            return out

        convw = np.empty((3, 2, 128, 4), np.float32)
        for ni, nm in enumerate(("conv_q", "conv_k", "conv_v")):
            cw = ins["L1_attn_" + nm]
            for i, h in enumerate(hs):
                convw[ni, i] = cw[h * 128:(h + 1) * 128]
        mm_ = {
            "wqh": headtile(wq), "wkh": headtile(wk), "wvh": headtile(wv),
            "wfa": wfa, "wga": wga,
            "wfbh": np.ascontiguousarray(
                np.stack([wfb[:, h * 128:(h + 1) * 128] for h in hs])),
            "wgbh": np.ascontiguousarray(
                np.stack([wgb[:, h * 128:(h + 1) * 128] for h in hs])),
            "wbt": np.ascontiguousarray(wb[:, hs].reshape(16, 128, 2)),
            "convw": convw,
            "nega": np.ascontiguousarray(alog[hs].reshape(1, 2), np.float32),
            "dtb": np.ascontiguousarray(dtbias[hs], np.float32),
            "nw": np.ascontiguousarray(ins["L1_attn_norm_w"].reshape(1, 128)),
        }
        if ln4T is not None:
            mm_["lnT4"] = ln4T
        maps.append(mm_)
    return maps




def kernel(**ins):
    ins = {k: np.asarray(v) for k, v in ins.items()}
    cores = list(range(NCORES))
    nc = _cached(("fused",), build_fused_launch)
    maps = fused_host_inputs(ins)
    res = run_bass_kernel_spmd(nc, maps, core_ids=cores)
    x6 = np.concatenate([untile_x(np.asarray(r["o_x"])) for r in res.results],
                        axis=0)
    return np.ascontiguousarray(x6.reshape(1, T, D), dtype=np.float32)


def kernel_unfused(**ins):
    ins = {k: np.asarray(v) for k, v in ins.items()}
    cores = list(range(NCORES))
    x0 = np.ascontiguousarray(ins["L0_attn_x"], dtype=np.float32)[0]

    x1 = run_ffn_f0(x0, ins["L0_ffn_gate_w"], ins["L0_ffn_up_w"],
                    ins["L0_ffn_down_w"], ins["L0_ffn_gate_s"],
                    ins["L0_ffn_up_s"], ins["L0_ffn_down_s"])

    nc2 = _cached(("attn0",), build_attn0_launch)
    res = run_bass_kernel_spmd(nc2, attn0_host_inputs(ins, x1), core_ids=cores)
    x2 = np.concatenate([untile_x(np.asarray(r["o_x"])) for r in res.results],
                        axis=0)

    x3 = run_ffn_f0(x2, ins["L0_ffn_gate_w"], ins["L0_ffn_up_w"],
                    ins["L0_ffn_down_w"], ins["L0_ffn_gate_s"],
                    ins["L0_ffn_up_s"], ins["L0_ffn_down_s"])

    nc4 = _cached(("ghost",), build_ghost_launch)
    res = run_bass_kernel_spmd(nc4, ghost_host_inputs(ins), core_ids=cores)
    gw = ghost_reassemble(res.results)

    x4 = run_ffn_f1(x3, gw)

    nc6 = _cached(("ln",), build_ln_launch)
    maps = [{"x": tile_x(x4[c * TPC:(c + 1) * TPC])} for c in cores]
    res = run_bass_kernel_spmd(nc6, maps, core_ids=cores)
    ln4T = np.concatenate([np.asarray(r["o"]) for r in res.results], axis=2)

    nc7 = _cached(("dn",), build_deltanet_launch)
    res = run_bass_kernel_spmd(nc7, dn_host_inputs(ins, ln4T), core_ids=cores)
    oT = np.concatenate([np.asarray(r["o_og"]) for r in res.results], axis=0)

    nc8 = _cached(("wo",), build_wo_launch)
    wo4 = np.ascontiguousarray(
        ins["L1_attn_Wo"].reshape(16, 128, 4, 512).transpose(0, 2, 1, 3))
    maps = [{"x": tile_x(x4[c * TPC:(c + 1) * TPC]),
             "oT": np.ascontiguousarray(oT[:, :, c * TPC:(c + 1) * TPC]),
             "wo": wo4} for c in cores]
    res = run_bass_kernel_spmd(nc8, maps, core_ids=cores)
    x5 = np.concatenate([untile_x(np.asarray(r["o_x"])) for r in res.results],
                        axis=0)

    x6 = run_ffn_f1(x5, gw)
    return np.ascontiguousarray(x6.reshape(1, T, D), dtype=np.float32)



# version 10
# speedup vs baseline: 1.1420x; 1.1420x over previous
import numpy as np
import ml_dtypes
import concourse.bass as bass
import concourse.mybir as mybir
from concourse.bass_utils import run_bass_kernel_spmd
from concourse.ap import AP as _BassAP
from contextlib import ExitStack

F32 = mybir.dt.float32
BF16 = mybir.dt.bfloat16
I32 = mybir.dt.int32
U32 = mybir.dt.uint32
AF = mybir.ActivationFunctionType
ALU = mybir.AluOpType
AX = mybir.AxisListType.X

NCORES = 8
T = 8192
D = 2048
TPC = T // NCORES
BF = ml_dtypes.bfloat16

ENGINES = ("sync", "gpsimd", "vector", "scalar", "tensor")


class SemModel:
    __slots__ = ("handle", "count", "name")

    def __init__(self, handle, name):
        self.handle = handle
        self.count = 0
        self.name = name


class Deferred:

    def __init__(self, fn, ndim=2):
        self.fn = fn
        self.ndim = ndim


class _EngProxy:

    def __init__(self, b, eng):
        self._b = b
        self._e = eng

    def __getattr__(self, op):
        def call(*args, **kwargs):
            item = [op, args, kwargs, []]
            self._b.q[self._e].append(item)
            return item

        return call


class Builder:
    def __init__(self, nc, ctx):
        self.nc = nc
        self.ctx = ctx
        self.root_ctx = ctx
        self.q = {e: [] for e in ENGINES}
        self._fence_sems = {}
        self.y = _EngProxy(self, "sync")
        self.g = _EngProxy(self, "gpsimd")
        self.v = _EngProxy(self, "vector")
        self.s = _EngProxy(self, "scalar")
        self.t = _EngProxy(self, "tensor")

    def sem(self, name):
        if not hasattr(self, "_semcache"):
            self._semcache = {}
        if name in self._semcache:
            return self._semcache[name]
        h = self.root_ctx.enter_context(self.nc.semaphore(name))
        sm = SemModel(h, name)
        self._semcache[name] = sm
        return sm

    def scope(self):
        import contextlib

        @contextlib.contextmanager
        def _sc():
            old = self.ctx
            with ExitStack() as es:
                self.ctx = es
                try:
                    yield
                finally:
                    self.ctx = old
        return _sc()

    def slot_sems(self, name, n):
        return [self.sem(f"{name}{j}") for j in range(n)]

    def inc(self, item, sem, n=1):
        sem.count += n
        if item is not None:
            item[3].append((sem, n))
        return sem.count

    def fence(self, engine):
        q = self.q[engine]
        for item in reversed(q):
            if item[0] != "wait_ge":
                if item[3]:
                    sem, n = item[3][-1]
                    self.q[engine].append(["wait_ge", (sem, sem.count), {}, []])
                    return
                fs = self._fence_sems.get(engine)
                if fs is None:
                    fs = self.sem("F_" + engine)
                    self._fence_sems[engine] = fs
                fs.count += 1
                item[3].append((fs, 1))
                self.q[engine].append(["wait_ge", (fs, fs.count), {}, []])
                return

    def phase_mark(self, name):
        if not PHASE_MARKERS:
            return
        PHASE_MARKS.append(name)
        if not hasattr(self, "_mkbuf"):
            self._mkbuf = self.root_ctx.enter_context(
                self.nc.sbuf_tensor("b_mkbuf", [1, 16], F32))
            self._mksem = self.sem("B_MK")
        for snm in ("VE", "TE"):
            sm = self._semcache.get(snm)
            if sm is not None:
                self.wait("gpsimd", sm)
        self.inc(self.g.memset(self._mkbuf[0:1, 0:1], 1.0), self._mksem)

    def wait(self, engine, sem, val=None):
        v = sem.count if val is None else val
        if v > 0:
            self.q[engine].append(["wait_ge", (sem, v), {}, []])

    def sbuf(self, name, shape, dtype):
        return self.ctx.enter_context(self.nc.sbuf_tensor(name, shape, dtype))

    def psum(self, name, shape, dtype):
        return self.root_ctx.enter_context(self.nc.psum_tensor(name, shape, dtype))

    def finalize(self):
        nc = self.nc
        block = self.ctx.enter_context(nc.Block())

        def runner(engine_name):
            def run(eng):
                for op, args, kwargs, incs in self.q[engine_name]:
                    if op == "__scope__":
                        continue
                    if op == "wait_ge":
                        sem, v = args
                        eng.wait_ge(sem.handle, v)
                        continue
                    args = [a.fn(eng) if isinstance(a, Deferred) else a
                            for a in args]
                    kwargs = {k2: (v2.fn(eng) if isinstance(v2, Deferred)
                                   else v2) for k2, v2 in kwargs.items()}
                    ins = getattr(eng, op)(*args, **kwargs)
                    for sem, n in incs:
                        ins.then_inc(sem.handle, n)
            return run

        block.sync(runner("sync"))
        block.gpsimd(runner("gpsimd"))
        block.vector(runner("vector"))
        block.scalar(runner("scalar"))
        block.tensor(runner("tensor"))


def common_const_setup(b, need_identb=True, need_identf=False,
                       const_vals=()):
    out = {}
    CONST = b.sem("CONST")
    cc = {}
    for i, val in enumerate(const_vals):
        t = b.sbuf(f"c_val{i}", [128, 1], F32)
        b.inc(b.v.memset(t[:], float(val)), CONST)
        cc[float(val)] = t
    out["cc"] = cc
    diag = b.sbuf("c_diag", [128, 128], F32)
    it = b.g.iota(diag[:], pattern=[[1, 128]], base=0, channel_multiplier=-1,
                  allow_small_or_imprecise_dtypes=True)
    b.inc(it, CONST)
    b.wait("vector", CONST)
    if need_identb:
        identb = b.sbuf("c_identb", [128, 128], BF16)
        ib = b.v.tensor_scalar(out=identb[:], in0=diag[:], scalar1=0.0,
                               scalar2=None, op0=ALU.is_equal)
        b.inc(ib, CONST)
        out["identb"] = identb
    if need_identf:
        identf = b.sbuf("c_identf", [128, 128], F32)
        if_ = b.v.tensor_scalar(out=identf[:], in0=diag[:], scalar1=0.0,
                                scalar2=None, op0=ALU.is_equal)
        b.inc(if_, CONST)
        out["identf"] = identf
    out["CONST"] = CONST
    return out


def emit_ln_transpose(b, consts, x_dram, lnT, n_tok, sems, x_sb=None,
                      name="ln"):
    TE, VE, SE = sems["TE"], sems["VE"], sems["SE"]
    XS = sems["XS"]
    nt = n_tok // 128
    if isinstance(x_dram, list):
        x_aps = x_dram
    else:
        x_aps = [x_dram[i].rearrange("a p c -> p a c") for i in range(nt)]
    xbuf = x_sb if x_sb is not None else [
        b.sbuf(name + "_xb0", [128, D], F32), b.sbuf(name + "_xb1", [128, D], F32)]
    sq = b.sbuf(name + "_sq", [128, D], F32)
    ssum = b.sbuf(name + "_ssum", [128, 1], F32)
    ssq = b.sbuf(name + "_ssq", [128, 1], F32)
    m = b.sbuf(name + "_m", [128, 1], F32)
    var = b.sbuf(name + "_var", [128, 1], F32)
    rstd = b.sbuf(name + "_rstd", [128, 1], F32)
    nmr = b.sbuf(name + "_nmr", [128, 1], F32)
    lntok = b.sbuf(name + "_tok", [128, D], BF16)
    identb = consts["identb"]
    pT2 = consts["pT2"]
    ve_evac = []

    se_done = []
    ve_var = []
    for i in range(nt):
        xb = xbuf[i] if x_sb is not None else xbuf[i % 2]
        if x_sb is None and i >= 2:
            b.wait("sync", VE, se_done[i - 2])
        src = x_aps[i]
        if src.ndim == 3:
            d = b.y.dma_start(out=xb[:].rearrange("p (a c) -> p a c", a=4),
                              in_=src)
        else:
            d = b.y.dma_start(out=xb[:], in_=src)
        b.inc(d, XS[i % 2], 16)
        b.wait("vector", XS[i % 2])
        b.wait("scalar", XS[i % 2])
        if i >= 1:
            b.wait("scalar", VE, ve_var[i - 1])
        b.inc(b.v.reduce_sum(out=ssum[:], in_=xb[:], axis=AX), VE)
        b.fence("vector")
        b.inc(b.s.activation(out=sq[:], in_=xb[:], func=AF.Square,
                             accum_out=ssq[:]), SE)
        b.wait("vector", SE)
        b.inc(b.v.tensor_scalar(out=m[:], in0=ssum[:], scalar1=1.0 / D,
                                scalar2=None, op0=ALU.mult), VE)
        b.fence("vector")
        b.inc(b.v.tensor_mul(out=nmr[:], in0=m[:], in1=m[:]), VE)
        b.fence("vector")
        ve_var.append(b.inc(b.v.scalar_tensor_tensor(out=var[:], in0=ssq[:],
                                                      scalar=1.0 / D,
                                                      in1=nmr[:], op0=ALU.mult,
                                                      op1=ALU.subtract), VE))
        b.fence("vector")
        b.wait("scalar", VE)
        b.inc(b.s.activation(out=var[:], in_=var[:], func=AF.Sqrt,
                             bias=consts["cc"][1e-5][:]), SE)
        b.wait("vector", SE)
        b.inc(b.v.reciprocal(out=rstd[:], in_=var[:]), VE)
        b.fence("vector")
        b.inc(b.v.scalar_tensor_tensor(out=nmr[:], in0=m[:], scalar=-1.0,
                                       in1=rstd[:], op0=ALU.mult,
                                       op1=ALU.mult), VE)
        b.fence("vector")
        b.wait("vector", TE)
        lnt = b.v.tensor_scalar(out=lntok[:], in0=xb[:], scalar1=rstd[:],
                                scalar2=nmr[:], op0=ALU.mult, op1=ALU.add)
        se_done.append(b.inc(lnt, VE))
        b.wait("tensor", VE)
        for c in range(16):
            k = i * 16 + c
            if k >= 2:
                b.wait("tensor", VE, ve_evac[k - 2])
            tr = b.t.transpose(out=pT2[k % 2][:, 0:128],
                               in_=lntok[:, c * 128:(c + 1) * 128],
                               identity=identb[:])
            b.inc(tr, TE)
            b.wait("vector", TE)
            ev = b.v.tensor_copy(out=lnT[c][:, i * 128:(i + 1) * 128],
                                 in_=pT2[k % 2][:, 0:128])
            ve_evac.append(b.inc(ev, VE))


_sc_uid = [0]


def emit_scale_cols(b, consts, sems, src_dram, n, dst_col, pF):
    TE, VE = sems["TE"], sems["VE"]
    _sc_uid[0] += 1
    CS = b.sem(f"CS{_sc_uid[0]}_" + src_dram.tensor.name)
    nt = (n + 127) // 128
    row = b.sbuf(f"sc_row{_sc_uid[0]}_" + src_dram.tensor.name, [nt, 128], F32)
    d = b.y.dma_start(out=row[:], in_=src_dram.rearrange("(a p) -> a p", p=128))
    b.inc(d, CS, 16)
    b.wait("tensor", CS)
    b.wait("tensor", consts["CONST"])
    tr = b.t.transpose(out=pF[:, 0:nt], in_=row[:], identity=consts["identf"][0:nt, 0:nt])
    b.inc(tr, TE)
    b.wait("vector", TE)
    b.inc(b.v.tensor_copy(out=dst_col[:, 0:nt], in_=pF[:, 0:nt]), VE)
    b.fence("vector")
    b.wait("tensor", VE)


def emit_f0_prep(b, nc_, gwT, uwT, dwT, tag):
    PREP = b.sem(tag + "PREP")
    NIT_ = gwT.shape[0]
    gwTb = nc_.dram_tensor(tag + "gwTb", [NIT_, 128, D], BF16,
                           kind="Internal").ap()
    uwTb = nc_.dram_tensor(tag + "uwTb", [NIT_, 128, D], BF16,
                           kind="Internal").ap()
    dwTb = nc_.dram_tensor(tag + "dwTb", [4, NIT_, 128, 512], BF16,
                           kind="Internal").ap()
    prep_n = 0
    for src, dst in ((gwT, gwTb), (uwT, uwTb), (dwT, dwTb)):
        nq = src.shape[0]
        for q in range(4):
            lo = (nq // 4) * q
            hi = (nq // 4) * (q + 1)
            b.inc(b.g.dma_start(out=dst[lo:hi], in_=src[lo:hi]), PREP, 16)
            prep_n += 16
    return (gwTb, uwTb, dwTb), PREP, prep_n


def build_ffn_program(nc, b, sems, consts, x, o_x, gwT, uwT, dwT, scales,
                      inter, half=0.5, tag="f", psums=None):
    TE, VE, SE = sems["TE"], sems["VE"], sems["SE"]
    XS = sems["XS"]
    GW = b.slot_sems("fGW", 3)
    UW = b.slot_sems("fUW", 3)
    DW = b.slot_sems("fDW", 6)
    XE = b.slot_sems("fXE", 2)
    SS = b.slot_sems("fSS", 2)
    NIT = inter // 128
    w_dtype = gwT.dtype
    nc_ = b.nc
    if w_dtype != BF16:
        (gwT, uwT, dwT), PREP, prep_n = emit_f0_prep(b, nc_, gwT, uwT, dwT,
                                                     tag)
        b.wait("sync", PREP, prep_n)
        b.wait("gpsimd", PREP, prep_n)

    if psums is None:
        psums = [b.psum(tag + f"ps{i}", [128, 512], F32) for i in range(6)]
    psG, psU = psums[0], psums[1]
    psD = psums[2:6]
    pF = psD[0]

    lnT = [b.sbuf(tag + f"lnT{c}", [128, TPC], BF16) for c in range(16)]
    h = [b.sbuf(tag + f"h{i}", [128, TPC], BF16) for i in range(NIT)]
    gw_buf = [b.sbuf(tag + f"gwb{j}", [128, 16 * 128], BF16) for j in range(3)]
    uw_buf = [b.sbuf(tag + f"uwb{j}", [128, 16 * 128], BF16) for j in range(3)]
    dw_buf = [b.sbuf(tag + f"dwb{j}", [128, 512], BF16) for j in range(6)]
    xe_buf = [b.sbuf(tag + f"xe{j}", [128, 512], F32) for j in range(2)]
    oe_buf = [b.sbuf(tag + f"oe{j}", [128, 512], F32) for j in range(2)]
    ge_buf = xe_buf

    if scales is not None:
        gs, us, ds = scales
        gs_col = b.sbuf(tag + "gs_col", [128, NIT], F32)
        us_col = b.sbuf(tag + "us_col", [128, NIT], F32)
        dsbc = b.sbuf(tag + "dsbc", [128, D], F32)
        ones1 = b.sbuf(tag + "ones1", [1, 128], F32)
        dsrow = b.sbuf(tag + "dsrow", [1, D], F32)
        emit_scale_cols(b, consts, sems, gs, inter, gs_col, pF)
        emit_scale_cols(b, consts, sems, us, inter, us_col, pF)
        CSd = b.sem(tag + "CSd")
        b.inc(b.y.dma_start(out=dsrow[:], in_=ds[None, :]), CSd, 16)
        b.wait("vector", consts["CONST"])
        b.inc(b.v.memset(ones1[:], 1.0), VE)
        b.wait("tensor", VE)
        b.wait("tensor", CSd)
        for jj in range(4):
            mm = b.t.matmul(out=pF[:, 0:512], lhsT=ones1[:],
                            rhs=dsrow[:, jj * 512:(jj + 1) * 512],
                            start=True, stop=True)
            b.inc(mm, TE)
            b.wait("vector", TE)
            b.inc(b.v.tensor_scalar(out=dsbc[:, jj * 512:(jj + 1) * 512],
                                    in0=pF[:, 0:512], scalar1=half,
                                    scalar2=None, op0=ALU.mult), VE)
            b.fence("vector")
            b.wait("tensor", VE)

    b.wait("scalar", consts["CONST"])
    emit_ln_transpose(b, consts, x, lnT, TPC, sems, name=tag + "ln")
    b.wait("tensor", sems["VE"])

    def load_giu(it, j):
        b.inc(b.y.dma_start(out=gw_buf[j][:], in_=gwT[it]), GW[j], 16)
        b.inc(b.y.dma_start(out=uw_buf[j][:], in_=uwT[it]), UW[j], 16)

    te_gate = []
    ve_h = []
    se_g = {}
    ve_g = {}
    load_giu(0, 0)
    load_giu(1, 1)
    for it in range(NIT):
        j = it % 3
        b.wait("tensor", GW[j])
        b.wait("tensor", UW[j])
        if it == 0 and scales is not None:
            b.wait("tensor", VE)
        if it + 2 < NIT:
            if it >= 1:
                b.wait("sync", TE, te_gate[it - 1])
            load_giu(it + 2, (it + 2) % 3)
        mm = None
        for ts in range(2):
            gidx = it * 2 + ts
            bs = gidx % 2
            psG_ = psums[0] if bs == 0 else psums[2]
            psU_ = psums[1] if bs == 0 else psums[3]
            if gidx >= 2:
                b.wait("tensor", SE, se_g[gidx - 2])
                b.wait("tensor", VE, ve_g[gidx - 2])
            for c in range(16):
                b.t.matmul(out=psG_[:], lhsT=gw_buf[j][:, c * 128:(c + 1) * 128],
                           rhs=lnT[c][:, ts * 512:(ts + 1) * 512],
                           start=(c == 0), stop=(c == 15))
                mm = b.t.matmul(out=psU_[:], lhsT=uw_buf[j][:, c * 128:(c + 1) * 128],
                                rhs=lnT[c][:, ts * 512:(ts + 1) * 512],
                                start=(c == 0), stop=(c == 15))
            mmg = b.inc(mm, TE)
            b.wait("scalar", TE, mmg)
            b.wait("scalar", VE)
            if scales is not None:
                si = b.s.activation(out=ge_buf[ts][:], in_=psG_[:], func=AF.Silu,
                                    scale=gs_col[:, it:it + 1])
            else:
                si = b.s.activation(out=ge_buf[ts][:], in_=psG_[:], func=AF.Silu)
            se_g[gidx] = b.inc(si, SE)
            b.wait("vector", SE, se_g[gidx])
            if scales is not None:
                b.inc(b.v.tensor_scalar(out=oe_buf[ts][:], in0=psU_[:],
                                        scalar1=us_col[:, it:it + 1],
                                        scalar2=None, op0=ALU.mult), VE)
                b.fence("vector")
                hv = b.v.tensor_mul(out=h[it][:, ts * 512:(ts + 1) * 512],
                                    in0=ge_buf[ts][:], in1=oe_buf[ts][:])
            else:
                hv = b.v.tensor_mul(out=h[it][:, ts * 512:(ts + 1) * 512],
                                    in0=ge_buf[ts][:], in1=psU_[:])
            ve_g[gidx] = b.inc(hv, VE)
        te_gate.append(TE.count)
        ve_h.append(VE.count)

    def load_dw(os_, ic, j):
        b.inc(b.y.dma_start(out=dw_buf[j][:], in_=dwT[os_, ic]), DW[j], 16)

    te_down = []
    ld_idx = 0
    b.wait("tensor", VE, ve_h[-1])
    b.wait("tensor", SE)
    ve_epi = []
    for tg in range(2):
        for os_ in range(4):
            for ic in range(NIT):
                j = ld_idx % 6
                if ld_idx >= 4:
                    b.wait("sync", TE, te_down[ld_idx - 4])
                load_dw(os_, ic, j)
                b.wait("tensor", DW[j])
                if ic == 0 and ld_idx > 0:
                    b.wait("tensor", VE)
                mm = None
                for tt in range(4):
                    t_abs = tg * 4 + tt
                    mm = b.t.matmul(out=psD[tt][:],
                                    lhsT=h[ic][:, t_abs * 128:(t_abs + 1) * 128],
                                    rhs=dw_buf[j][:],
                                    start=(ic == 0), stop=(ic == NIT - 1))
                b.inc(mm, TE)
                te_down.append(TE.count)
                ld_idx += 1
            for tt in range(4):
                t_abs = tg * 4 + tt
                jj = tt % 2
                if len(ve_epi) >= 2:
                    b.wait("sync", VE, ve_epi[-2])
                else:
                    b.wait("sync", VE, ve_h[-1])
                dx = b.y.dma_start(out=xe_buf[jj][:], in_=x[t_abs, os_])
                b.inc(dx, XE[jj], 16)
                b.wait("vector", XE[jj])
                b.wait("vector", TE)
                if len(ve_epi) >= 2:
                    b.wait("vector", SS[jj])
                if scales is not None:
                    b.inc(b.v.tensor_mul(out=oe_buf[jj][:], in0=psD[tt][:],
                                         in1=dsbc[:, os_ * 512:(os_ + 1) * 512]), VE)
                    b.fence("vector")
                    av = b.v.tensor_add(out=oe_buf[jj][:], in0=oe_buf[jj][:],
                                        in1=xe_buf[jj][:])
                else:
                    av = b.v.scalar_tensor_tensor(out=oe_buf[jj][:], in0=psD[tt][:],
                                                  scalar=half, in1=xe_buf[jj][:],
                                                  op0=ALU.mult, op1=ALU.add)
                ve_epi.append(b.inc(av, VE))
                b.wait("sync", VE)
                so = b.y.dma_start(out=o_x[t_abs, os_], in_=oe_buf[jj][:])
                b.inc(so, SS[jj], 16)
    b.wait("sync", SS[0])
    b.wait("sync", SS[1])
    return {"lnT": lnT, "h": h, "gw_buf": gw_buf,
            "wbf": (gwT, uwT, dwT), "SS": SS}


def build_ffn_launch(inter, with_scales, w_dtype, half=0.5, debug=False):
    nc = bass.Bass("TRN2", target_bir_lowering=False, debug=False)
    NIT_ = inter // 128
    x = nc.dram_tensor("x", [TPC // 128, 4, 128, 512], F32, kind="ExternalInput").ap()
    gwT = nc.dram_tensor("gwT", [NIT_, 128, D], w_dtype, kind="ExternalInput").ap()
    uwT = nc.dram_tensor("uwT", [NIT_, 128, D], w_dtype, kind="ExternalInput").ap()
    dwT = nc.dram_tensor("dwT", [4, NIT_, 128, 512], w_dtype, kind="ExternalInput").ap()
    scales = None
    if with_scales:
        gs = nc.dram_tensor("gs", [inter], F32, kind="ExternalInput").ap()
        us = nc.dram_tensor("us", [inter], F32, kind="ExternalInput").ap()
        ds = nc.dram_tensor("ds", [D], F32, kind="ExternalInput").ap()
        scales = (gs, us, ds)
    o_x = nc.dram_tensor("o_x", [TPC // 128, 4, 128, 512], F32,
                         kind="ExternalOutput").ap()
    dbg_spec = [("lnT0", [128, TPC], BF16), ("h0", [128, TPC], BF16),
                ("gw0", [128, 2048], BF16), ("hall", [inter, TPC], BF16)]
    dbg = {nm: nc.dram_tensor(nm, sh, dt, kind="ExternalOutput").ap()
           for nm, sh, dt in dbg_spec} if debug else None

    with ExitStack() as ctx:
        b = Builder(nc, ctx)
        sems = {k: b.sem(k) for k in ("TE", "VE", "SE", "CS")}
        sems["XS"] = b.slot_sems("XS", 2)
        consts = common_const_setup(b, need_identb=True, need_identf=with_scales,
                                    const_vals=(1e-5,))
        consts["pT2"] = [b.psum("pT0", [128, 1024], BF16),
                         b.psum("pT1", [128, 1024], BF16)]
        refs = build_ffn_program(nc, b, sems, consts, x, o_x, gwT, uwT, dwT,
                                 scales, inter, half=half)
        if debug:
            DBG = b.sem("DBG")
            b.wait("gpsimd", sems["VE"])
            b.wait("gpsimd", sems["TE"])
            for nm, src in (("lnT0", refs["lnT"][0]), ("h0", refs["h"][0]),
                            ("gw0", refs["gw_buf"][0])):
                b.inc(b.g.dma_start(out=dbg[nm], in_=src[:]), DBG, 16)
            for ic in range(inter // 128):
                b.inc(b.g.dma_start(out=dbg["hall"][ic * 128:(ic + 1) * 128, :],
                                    in_=refs["h"][ic][:]), DBG, 16)
            b.wait("gpsimd", DBG)
        b.finalize()
    return nc



_prog_cache = {}


def _cached(key, fn):
    if key not in _prog_cache:
        _prog_cache[key] = fn()
    return _prog_cache[key]


def tile_gu(w):
    inter = w.shape[0]
    return np.ascontiguousarray(
        w.reshape(inter // 128, 128, 16, 128).transpose(0, 3, 2, 1)
         .reshape(inter // 128, 128, D))


def tile_dw(w):
    inter = w.shape[1]
    return np.ascontiguousarray(
        w.reshape(4, 512, inter // 128, 128).transpose(0, 2, 3, 1))


def tile_x(x):
    return np.ascontiguousarray(
        x.reshape(TPC // 128, 128, 4, 512).transpose(0, 2, 1, 3))


def untile_x(x4):
    return np.ascontiguousarray(
        np.asarray(x4).transpose(0, 2, 1, 3).reshape(TPC, D))


def run_ffn_f1(x_full, w, half=0.5):
    inter = 5504
    nc = _cached(("ffn", inter, False, "bf16"),
                 lambda: build_ffn_launch(inter, False, BF16, half=half))
    in_maps = []
    for c in range(NCORES):
        in_maps.append({
            "x": tile_x(x_full[c * TPC:(c + 1) * TPC]),
            "gwT": w["g"], "uwT": w["u"], "dwT": w["d"],
        })
    res = run_bass_kernel_spmd(nc, in_maps, core_ids=list(range(NCORES)))
    return np.concatenate([untile_x(r["o_x"]) for r in res.results], axis=0)


def run_ffn_f0(x_full, gw, uw, dw, gs, us, ds, half=0.5, debug=False):
    inter = gw.shape[0]
    nc = _cached(("ffn", inter, True, "f32", debug),
                 lambda: build_ffn_launch(inter, True, F32, half=half,
                                          debug=debug))
    gwT = tile_gu(gw)
    uwT = tile_gu(uw)
    dwT = tile_dw(dw)
    in_maps = []
    for c in range(NCORES):
        in_maps.append({
            "x": tile_x(x_full[c * TPC:(c + 1) * TPC]),
            "gwT": gwT, "uwT": uwT, "dwT": dwT,
            "gs": gs, "us": us, "ds": ds,
        })
    res = run_bass_kernel_spmd(nc, in_maps, core_ids=list(range(NCORES)))
    if debug:
        return res
    return np.concatenate([untile_x(r["o_x"]) for r in res.results], axis=0)





NH0 = 16
NKV0 = 4
HALO = 512
NTOK0 = TPC + HALO


def emit_attn0(b, consts, sems, psums, io, tag="a0"):
    nc = b.nc
    TE, VE, SE = sems["TE"], sems["VE"], sems["SE"]
    xh = io["xh"]
    wq, wk, wv, wo = io["wq"], io["wk"], io["wv"], io["wo"]
    lam, cosr, sinr, mask = io["lam"], io["cosr"], io["sinr"], io["mask"]
    WS = b.slot_sems(tag + "WS", 3)
    WOS = b.slot_sems(tag + "WOS", 4)
    MS = b.slot_sems(tag + "MS", 2)
    XE = b.slot_sems(tag + "XE", 2)
    SS = b.slot_sems(tag + "SS", 2)
    ONE = b.sem(tag + "ONE")
    if True:
        pA, pB, pAx, pBx, pO, pW = psums[0:6]
        pT0, pT1 = consts["pT2"]

        lnT = [b.sbuf(tag + f"lnT{c}", [128, NTOK0], BF16) for c in range(16)]
        att = [lnT[h][:, 0:TPC] for h in range(16)]
        qh = [b.sbuf(tag + f"qh{h}", [128, TPC], BF16) for h in range(NH0)]
        kh = [b.sbuf(tag + f"kh{h}", [128, NTOK0], BF16) for h in range(NKV0)]
        vT = [b.sbuf(tag + f"vT{h}", [128, NTOK0], BF16) for h in range(NKV0)]
        wbuf = [b.sbuf(tag + f"wbuf{j}", [128, D], BF16) for j in range(3)]
        wobuf = [b.sbuf(tag + f"wobuf{j}", [128, 512], BF16) for j in range(4)]
        mbuf = [b.sbuf(tag + f"mbuf{j}", [128, 640], BF16) for j in range(2)]
        cos_s = b.sbuf(tag + "cos_s", [128, NTOK0], BF16)
        sin_s = b.sbuf(tag + "sin_s", [128, NTOK0], BF16)
        qstage = b.sbuf(tag + "qstage", [128, 512], F32)
        rstage = b.sbuf(tag + "rstage", [128, 512], F32)
        t1s = b.sbuf(tag + "t1s", [128, 512], F32)
        vstage = b.sbuf(tag + "vstage", [128, 512], BF16)
        def ph2(nm, shape, dt):
            return [b.sbuf(tag + nm + str(s_), shape, dt) for s_ in range(2)]
        m1 = ph2("m1", [128, 640], F32)
        m2 = ph2("m2", [128, 640], F32)
        e1 = ph2("e1", [128, 640], F32)
        e2 = ph2("e2", [128, 640], F32)
        pb = b.sbuf(tag + "pb", [128, 640], BF16)
        pbT = b.sbuf(tag + "pbT", [128, 640], BF16)
        Z1 = ph2("Z1", [128, 1], F32)
        Z2 = ph2("Z2", [128, 1], F32)
        cc1 = b.sbuf(tag + "cc1", [128, 1], F32)
        Rr = b.sbuf(tag + "Rr", [128, 1], F32)
        den = b.sbuf(tag + "den", [128, 1], F32)
        lamrow = b.sbuf(tag + "lamrow", [1, NH0], F32)
        nlam = b.sbuf(tag + "nlam", [128, NH0], F32)
        ones1 = b.sbuf(tag + "ones1", [1, 128], F32)
        xe_buf = [b.sbuf(tag + f"xe{j}", [128, 512], F32) for j in range(2)]
        oe_buf = [b.sbuf(tag + f"oe{j}", [128, 512], F32) for j in range(2)]

        n_one = 0
        for dst, src in ((cos_s, cosr), (sin_s, sinr)):
            b.inc(b.y.dma_start(out=dst[:], in_=src), ONE, 16)
            n_one += 16
        b.inc(b.y.dma_start(out=lamrow[:], in_=lam[None, :]), ONE, 16)
        n_one += 16
        b.wait("vector", ONE, n_one)
        b.inc(b.v.memset(ones1[:], 1.0), VE)
        b.fence("vector")
        b.wait("tensor", VE)
        mmn = b.t.matmul(out=pW[:, 0:NH0], lhsT=ones1[:], rhs=lamrow[:],
                         start=True, stop=True)
        b.inc(mmn, TE)
        b.wait("vector", TE)
        b.inc(b.v.tensor_scalar(out=nlam[:], in0=pW[:, 0:NH0], scalar1=-1.0,
                                scalar2=None, op0=ALU.mult), VE)
        b.fence("vector")
        b.wait("tensor", VE)

        emit_ln_transpose(b, consts, xh, lnT, NTOK0, sems, name=tag + "ln")
        b.wait("tensor", VE)

        wload_n = [0, 0, 0]

        def load_w(src, idx, j):
            b.inc(b.y.dma_start(out=wbuf[j][:], in_=src[idx]), WS[j], 16)
            wload_n[j] += 1

        te_proj = []

        plan = ([("q", h) for h in range(NH0)] +
                [("k", h) for h in range(NKV0)] +
                [("v", h) for h in range(NKV0)])
        srcs = {"q": wq, "k": wk, "v": wv}
        load_w(srcs[plan[0][0]], plan[0][1], 0)
        load_w(srcs[plan[1][0]], plan[1][1], 1)
        for pi, (kind, h) in enumerate(plan):
            j = pi % 3
            b.wait("tensor", WS[j], wload_n[j] * 16)
            if pi + 2 < len(plan):
                if pi >= 1:
                    b.wait("sync", TE, te_proj[pi - 1])
                load_w(srcs[plan[pi + 2][0]], plan[pi + 2][1], (pi + 2) % 3)
            nslab = 2 if kind == "q" else 3
            off = HALO if kind == "q" else 0
            for tsl in range(nslab):
                ps = pA if tsl % 2 == 0 else pB
                if pi > 0 or tsl > 1:
                    b.wait("tensor", VE)
                mm = None
                for c in range(16):
                    mm = b.t.matmul(out=ps[:],
                                    lhsT=wbuf[j][:, c * 128:(c + 1) * 128],
                                    rhs=lnT[c][:, off + tsl * 512:off + (tsl + 1) * 512],
                                    start=(c == 0), stop=(c == 15))
                b.inc(mm, TE)
                b.wait("vector", TE)
                col0 = tsl * 512
                if kind == "v":
                    b.inc(b.v.tensor_copy(out=vstage[:], in_=ps[:]), VE)
                    b.fence("vector")
                    for m in range(4):
                        k_ = tsl * 4 + m
                        b.wait("tensor", VE)
                        tr = b.t.transpose(out=consts["pT2"][k_ % 2][:, 0:128],
                                           in_=vstage[:, m * 128:(m + 1) * 128],
                                           identity=consts["identb"][:])
                        b.inc(tr, TE)
                        b.wait("vector", TE)
                        b.inc(b.v.tensor_copy(
                            out=vT[h][:, (tsl * 4 + m) * 128:(tsl * 4 + m + 1) * 128],
                            in_=consts["pT2"][k_ % 2][:, 0:128]), VE)
                else:
                    dst = qh[h] if kind == "q" else kh[h]
                    cos_sl = cos_s[:, off + col0:off + col0 + 512]
                    sin_sl = sin_s[:, off + col0:off + col0 + 512]
                    b.inc(b.v.tensor_copy(out=qstage[:], in_=ps[:]), VE)
                    b.fence("vector")
                    b.inc(b.v.tensor_scalar(out=rstage[64:128, :],
                                            in0=qstage[0:64, :], scalar1=1.0,
                                            scalar2=None, op0=ALU.mult), VE)
                    b.inc(b.v.tensor_scalar(out=rstage[0:64, :],
                                            in0=qstage[64:128, :], scalar1=-1.0,
                                            scalar2=None, op0=ALU.mult), VE)
                    b.fence("vector")
                    b.inc(b.v.tensor_mul(out=t1s[:], in0=qstage[:], in1=cos_sl), VE)
                    b.inc(b.v.tensor_mul(out=rstage[:], in0=rstage[:], in1=sin_sl), VE)
                    b.fence("vector")
                    b.inc(b.v.tensor_add(out=dst[:, col0:col0 + 512],
                                         in0=t1s[:], in1=rstage[:]), VE)
                b.fence("vector")
            te_proj.append(TE.count)
        b.wait("tensor", VE)

        mload = [0, 0]
        b.inc(b.y.dma_start(out=mbuf[0][:], in_=mask[0]), MS[0], 16)
        mload[0] += 1
        ve_att = {}
        mevac = {}
        expi = {}
        pbts = {}
        mmsc = {}
        trl = {}

        def stage1(i):
            qb, h = i // NH0, i % NH0
            jm = qb % 2
            kv = h // 4
            s = i % 2
            qsl = qh[h][:, qb * 128:(qb + 1) * 128]
            band0 = qb * 128
            if h == 0:
                b.wait("tensor", MS[jm], mload[jm] * 16)
                if qb + 1 < 8:
                    if qb >= 1:
                        b.wait("sync", VE, mevac[qb * NH0 - 1])
                    b.inc(b.y.dma_start(out=mbuf[(qb + 1) % 2][:],
                                        in_=mask[qb + 1]),
                          MS[(qb + 1) % 2], 16)
                    mload[(qb + 1) % 2] += 1
            if i >= 1:
                b.wait("tensor", VE, mevac[i - 1])
            b.t.matmul(out=pA[:], lhsT=qsl[0:64, :],
                       rhs=kh[kv][0:64, band0:band0 + 512],
                       start=True, stop=True)
            b.t.matmul(out=pAx[:, 0:128], lhsT=qsl[0:64, :],
                       rhs=kh[kv][0:64, band0 + 512:band0 + 640],
                       start=True, stop=True)
            b.t.matmul(out=pB[:], lhsT=qsl[64:128, :],
                       rhs=kh[kv][64:128, band0:band0 + 512],
                       start=True, stop=True)
            mm = b.t.matmul(out=pBx[:, 0:128], lhsT=qsl[64:128, :],
                            rhs=kh[kv][64:128, band0 + 512:band0 + 640],
                            start=True, stop=True)
            mmsc[i] = b.inc(mm, TE)
            b.wait("vector", TE, mmsc[i])
            if i >= 2:
                b.wait("vector", SE, expi[i - 2])
            b.inc(b.v.scalar_tensor_tensor(out=m1[s][:, 0:512], in0=pA[:],
                                           scalar=0.125,
                                           in1=mbuf[jm][:, 0:512],
                                           op0=ALU.mult, op1=ALU.add), VE)
            b.inc(b.v.scalar_tensor_tensor(out=m1[s][:, 512:640],
                                           in0=pAx[:, 0:128],
                                           scalar=0.125,
                                           in1=mbuf[jm][:, 512:640],
                                           op0=ALU.mult, op1=ALU.add), VE)
            b.inc(b.v.scalar_tensor_tensor(out=m2[s][:, 0:512], in0=pB[:],
                                           scalar=0.125,
                                           in1=mbuf[jm][:, 0:512],
                                           op0=ALU.mult, op1=ALU.add), VE)
            mevac[i] = b.inc(b.v.scalar_tensor_tensor(
                out=m2[s][:, 512:640], in0=pBx[:, 0:128], scalar=0.125,
                in1=mbuf[jm][:, 512:640], op0=ALU.mult, op1=ALU.add), VE)
            b.fence("vector")
            b.wait("scalar", VE, mevac[i])
            if i >= 2:
                b.wait("scalar", VE, pbts[i - 2])
            nb = consts["cc"][-20.0]
            b.inc(b.s.activation(out=e1[s][:], in_=m1[s][:], func=AF.Exp,
                                 bias=nb[:], accum_out=Z1[s][:]), SE)
            expi[i] = b.inc(b.s.activation(out=e2[s][:], in_=m2[s][:],
                                           func=AF.Exp, bias=nb[:],
                                           accum_out=Z2[s][:]), SE)

        def stage2(i):
            qb, h = i // NH0, i % NH0
            kv = h // 4
            s = i % 2
            b.wait("vector", SE, expi[i])
            b.inc(b.v.reciprocal(out=cc1[:], in_=Z2[s][:]), VE)
            b.fence("vector")
            b.inc(b.v.tensor_mul(out=cc1[:], in0=cc1[:], in1=Z1[s][:]), VE)
            b.fence("vector")
            b.inc(b.v.tensor_scalar(out=cc1[:], in0=cc1[:],
                                    scalar1=nlam[:, h:h + 1],
                                    scalar2=None, op0=ALU.mult), VE)
            b.fence("vector")
            mst = b.inc(b.v.scalar_tensor_tensor(
                out=m1[s][:], in0=e2[s][:], scalar=cc1[:], in1=e1[s][:],
                op0=ALU.mult, op1=ALU.add), VE)
            b.fence("vector")
            b.wait("scalar", VE, mst)
            rl = b.inc(b.s.activation(out=e2[s][:], in_=m1[s][:],
                                      func=AF.Relu, accum_out=Rr[:]), SE)
            b.wait("vector", SE, rl)
            b.inc(b.v.scalar_tensor_tensor(out=den[:], in0=Z1[s][:],
                                           scalar=1e-6, in1=Rr[:],
                                           op0=ALU.mult, op1=ALU.add), VE)
            b.fence("vector")
            b.inc(b.v.reciprocal(out=den[:], in_=den[:]), VE)
            b.fence("vector")
            if i >= 1:
                b.wait("vector", TE, trl[i - 1])
            pbts[i] = b.inc(b.v.tensor_scalar(out=pb[:], in0=e2[s][:],
                                              scalar1=den[:], scalar2=None,
                                              op0=ALU.mult), VE)
            b.fence("vector")
            evm = {}
            trm = 0
            b.wait("tensor", VE, pbts[i])
            for m in range(5):
                if m >= 2:
                    b.wait("tensor", VE, evm[m - 2])
                trm = b.inc(b.t.transpose(
                    out=consts["pT2"][m % 2][:, 0:128],
                    in_=pb[:, m * 128:(m + 1) * 128],
                    identity=consts["identb"][:]), TE)
                b.wait("vector", TE, trm)
                evm[m] = b.inc(b.v.tensor_copy(
                    out=pbT[:, m * 128:(m + 1) * 128],
                    in_=consts["pT2"][m % 2][:, 0:128]), VE)
            trl[i] = trm
            b.fence("vector")
            b.wait("tensor", VE, evm[4])
            mm = None
            for m in range(5):
                mm = b.t.matmul(out=pO[:, 0:128],
                                lhsT=vT[kv][:, (qb + m) * 128:(qb + m + 1) * 128],
                                rhs=pbT[:, m * 128:(m + 1) * 128],
                                start=(m == 0), stop=(m == 4))
            vmm = b.inc(mm, TE)
            b.wait("vector", TE, vmm)
            ev = b.v.tensor_copy(out=att[h][:, qb * 128:(qb + 1) * 128],
                                 in_=pO[:, 0:128])
            ve_att[i] = b.inc(ev, VE)
            b.fence("vector")

        for i in range(8 * NH0):
            stage1(i)
            if i >= 1:
                stage2(i - 1)
        stage2(8 * NH0 - 1)
        b.wait("tensor", VE)

        woload = [0, 0, 0, 0]

        def load_wo(cc, os_, j):
            b.inc(b.y.dma_start(out=wobuf[j][:], in_=wo[cc, os_]), WOS[j], 16)
            woload[j] += 1

        te_wo = []
        ve_epi = []
        ld_idx = 0
        for tg in range(2):
            for os_ in range(4):
                for cc in range(16):
                    j = ld_idx % 4
                    if ld_idx >= 2:
                        b.wait("sync", TE, te_wo[ld_idx - 2])
                    load_wo(cc, os_, j)
                    b.wait("tensor", WOS[j], woload[j] * 16)
                    if cc == 0 and ld_idx > 0:
                        b.wait("tensor", VE)
                    mm = None
                    for tt in range(4):
                        t_abs = tg * 4 + tt
                        ps = (pA, pB, pAx, pBx)[tt]
                        mm = b.t.matmul(out=ps[:],
                                        lhsT=att[cc][:, t_abs * 128:(t_abs + 1) * 128],
                                        rhs=wobuf[j][:],
                                        start=(cc == 0), stop=(cc == 15))
                    b.inc(mm, TE)
                    te_wo.append(TE.count)
                    ld_idx += 1
                for tt in range(4):
                    t_abs = tg * 4 + tt
                    jj = tt % 2
                    if len(ve_epi) >= 2:
                        b.wait("sync", VE, ve_epi[-2])
                    dx = b.y.dma_start(out=xe_buf[jj][:],
                                       in_=io["resid"](t_abs, os_))
                    b.inc(dx, XE[jj], 16)
                    b.wait("vector", XE[jj])
                    b.wait("vector", TE)
                    if len(ve_epi) >= 2:
                        b.wait("vector", SS[jj])
                    ps = (pA, pB, pAx, pBx)[tt]
                    av = b.v.tensor_add(out=oe_buf[jj][:], in0=ps[:],
                                        in1=xe_buf[jj][:])
                    ve_epi.append(b.inc(av, VE))
                    b.wait("sync", VE)
                    so = b.y.dma_start(out=io["out"](t_abs, os_),
                                       in_=oe_buf[jj][:])
                    b.inc(so, SS[jj], 16)
        b.wait("sync", SS[0])
        b.wait("sync", SS[1])
    return {"SS": SS}


def build_attn0_launch():
    nc = bass.Bass("TRN2", target_bir_lowering=False, debug=False)
    xh = nc.dram_tensor("xh", [NTOK0 // 128, 4, 128, 512], F32,
                        kind="ExternalInput").ap()
    wq = nc.dram_tensor("wq", [NH0, 128, D], F32, kind="ExternalInput").ap()
    wk = nc.dram_tensor("wk", [NKV0, 128, D], F32, kind="ExternalInput").ap()
    wv = nc.dram_tensor("wv", [NKV0, 128, D], F32, kind="ExternalInput").ap()
    wo = nc.dram_tensor("wo", [16, 4, 128, 512], F32, kind="ExternalInput").ap()
    lam = nc.dram_tensor("lam", [NH0], F32, kind="ExternalInput").ap()
    cosr = nc.dram_tensor("cosr", [128, NTOK0], BF16, kind="ExternalInput").ap()
    sinr = nc.dram_tensor("sinr", [128, NTOK0], BF16, kind="ExternalInput").ap()
    mask = nc.dram_tensor("mask", [8, 128, 640], BF16, kind="ExternalInput").ap()
    o_x = nc.dram_tensor("o_x", [TPC // 128, 4, 128, 512], F32,
                         kind="ExternalOutput").ap()
    with ExitStack() as ctx:
        b = Builder(nc, ctx)
        sems = {k: b.sem(k) for k in ("TE", "VE", "SE")}
        sems["XS"] = b.slot_sems("XS", 2)
        consts = common_const_setup(b, need_identb=True,
                                    const_vals=(1e-5, -20.0))
        consts["pT2"] = [b.psum("pT0", [128, 1024], BF16),
                        b.psum("pT1", [128, 1024], BF16)]
        psums = [b.psum(f"ps{i}", [128, 512], F32) for i in range(6)]
        PREP = b.sem("PREP")
        wqb = nc.dram_tensor("wqb", [NH0, 128, D], BF16, kind="Internal").ap()
        wkb = nc.dram_tensor("wkb", [NKV0, 128, D], BF16, kind="Internal").ap()
        wvb = nc.dram_tensor("wvb", [NKV0, 128, D], BF16, kind="Internal").ap()
        wob = nc.dram_tensor("wob", [16, 4, 128, 512], BF16, kind="Internal").ap()
        prep_n = 0
        for src, dst, nq in ((wq, wqb, 4), (wk, wkb, 2), (wv, wvb, 2),
                             (wo, wob, 4)):
            for q in range(nq):
                lo = (src.shape[0] // nq) * q
                hi = (src.shape[0] // nq) * (q + 1)
                b.inc(b.g.dma_start(out=dst[lo:hi], in_=src[lo:hi]), PREP, 16)
                prep_n += 16
        b.wait("sync", PREP, prep_n)
        io = {"xh": xh, "wq": wqb, "wk": wkb, "wv": wvb, "wo": wob,
              "lam": lam, "cosr": cosr, "sinr": sinr, "mask": mask,
              "resid": lambda t, o_: xh[HALO // 128 + t, o_],
              "out": lambda t, o_: o_x[t, o_]}
        emit_attn0(b, consts, sems, psums, io)
        b.finalize()
    return nc


def attn0_host_inputs(ins, x1_full):
    wq4 = np.ascontiguousarray(
        ins["L0_attn_Wq"].reshape(16, 128, 16, 128).transpose(2, 1, 0, 3)
        .reshape(16, 128, D))
    wk4 = np.ascontiguousarray(
        ins["L0_attn_Wk"].reshape(16, 128, 4, 128).transpose(2, 1, 0, 3)
        .reshape(4, 128, D))
    wv4 = np.ascontiguousarray(
        ins["L0_attn_Wv"].reshape(16, 128, 4, 128).transpose(2, 1, 0, 3)
        .reshape(4, 128, D))
    wo4 = np.ascontiguousarray(
        ins["L0_attn_Wo"].reshape(16, 128, 4, 512).transpose(0, 2, 1, 3))
    inv = (10000.0 ** (-(np.arange(64, dtype=np.float64) * 2.0 / 128.0)))
    in_maps = []
    for c in range(NCORES):
        g0 = c * TPC
        xs = np.zeros((NTOK0, D), np.float32)
        lo = max(0, g0 - HALO)
        xs[HALO - (g0 - lo):HALO] = x1_full[lo:g0]
        xs[HALO:] = x1_full[g0:g0 + TPC]
        xh = np.ascontiguousarray(
            xs.reshape(NTOK0 // 128, 128, 4, 512).transpose(0, 2, 1, 3))
        pos = np.arange(g0 - HALO, g0 + TPC, dtype=np.float64)
        ang = pos[None, :] * inv[:, None]
        cosr = np.concatenate([np.cos(ang), np.cos(ang)], axis=0)
        sinr = np.concatenate([np.sin(ang), np.sin(ang)], axis=0)
        r = np.arange(128)[None, :, None]
        cix = np.arange(640)[None, None, :]
        qb = np.arange(8)[:, None, None]
        gj = g0 - HALO + qb * 128 + cix
        allowed = (cix >= r + 1) & (cix <= r + 512) & (gj >= 0)
        maskv = np.where(allowed, np.float32(0), np.float32(-1e30))
        in_maps.append({
            "xh": xh, "wq": wq4, "wk": wk4, "wv": wv4, "wo": wo4,
            "lam": ins["L0_attn_lam"],
            "cosr": cosr.astype(BF), "sinr": sinr.astype(BF),
            "mask": maskv.astype(BF),
        })
    return in_maps



GCH = 11264


def emit_ghost(b, sems, psums, io, tag="gh"):
    nc = b.nc
    TE, VE, SE = sems["TE"], sems["VE"], sems["SE"]
    lut = io["lut"]
    idxs = io["idxs"]
    outs = io["outs"]
    NCH = GCH // 1024
    ONE = b.sem(tag + "ONE")
    IS = b.slot_sems(tag + "IS", 2)
    OS = b.slot_sems(tag + "OS", 2)
    if True:
        lutrow = b.sbuf(tag + "lutrow", [1, 16], F32)
        lutbc = b.sbuf(tag + "lutbc", [128, 16], F32)
        ones1 = b.sbuf(tag + "ones1", [1, 128], F32)
        ibuf = [b.sbuf(tag + f"ibuf{j}", [128, 1024], I32) for j in range(2)]
        ibf = b.sbuf(tag + "ibf", [128, 1024], BF16)
        tmp = b.sbuf(tag + "tmp", [128, 1024], BF16)
        acc = [b.sbuf(tag + f"acc{j}", [128, 1024], BF16) for j in range(2)]
        pW = psums[0]

        b.inc(b.y.dma_start(out=lutrow[:], in_=lut[None, :]), ONE, 16)
        b.wait("vector", ONE, 16)
        b.inc(b.v.memset(ones1[:], 1.0), VE)
        b.fence("vector")
        b.wait("tensor", VE)
        b.inc(b.t.matmul(out=pW[:, 0:16], lhsT=ones1[:], rhs=lutrow[:],
                         start=True, stop=True), TE)
        b.wait("vector", TE)
        b.inc(b.v.tensor_copy(out=lutbc[:], in_=pW[:, 0:16]), VE)
        b.fence("vector")

        iload = [0, 0]
        ve_done = []
        ci = 0
        for nm in ("g", "u", "d"):
            for ch in range(NCH):
                j = ci % 2
                if ci >= 2:
                    b.wait("sync", VE, ve_done[ci - 2])
                d = b.y.dma_start(out=ibuf[j][:],
                                  in_=idxs[nm][:, ch * 1024:(ch + 1) * 1024])
                b.inc(d, IS[j], 16)
                iload[j] += 1
                b.wait("vector", IS[j], iload[j] * 16)
                if ci >= 2:
                    b.wait("vector", OS[j])
                b.inc(b.v.tensor_copy(out=ibf[:], in_=ibuf[j][:]), VE)
                b.fence("vector")
                for v in range(16):
                    b.inc(b.v.tensor_scalar(out=tmp[:], in0=ibf[:],
                                            scalar1=float(v),
                                            scalar2=lutbc[:, v:v + 1],
                                            op0=ALU.is_equal, op1=ALU.mult), VE)
                    b.fence("vector")
                    if v == 0:
                        b.inc(b.v.tensor_copy(out=acc[j][:], in_=tmp[:]), VE)
                    else:
                        b.inc(b.v.tensor_add(out=acc[j][:], in0=acc[j][:],
                                             in1=tmp[:]), VE)
                    b.fence("vector")
                ve_done.append(VE.count)
                b.wait("sync", VE)
                so = b.y.dma_start(out=outs[nm][:, ch * 1024:(ch + 1) * 1024],
                                   in_=acc[j][:])
                b.inc(so, OS[j], 16)
                ci += 1
        b.wait("sync", OS[0])
        b.wait("sync", OS[1])
    return {"OS": OS}


def build_ghost_launch():
    nc = bass.Bass("TRN2", target_bir_lowering=False, debug=False)
    lut = nc.dram_tensor("lut", [16], F32, kind="ExternalInput").ap()
    idxs = {}
    outs = {}
    for nm in ("g", "u", "d"):
        idxs[nm] = nc.dram_tensor("idx_" + nm, [128, GCH], I32,
                                  kind="ExternalInput").ap()
        outs[nm] = nc.dram_tensor("out_" + nm, [128, GCH], BF16,
                                  kind="ExternalOutput").ap()
    with ExitStack() as ctx:
        b = Builder(nc, ctx)
        sems = {k: b.sem(k) for k in ("TE", "VE", "SE")}
        psums = [b.psum("ps0", [128, 512], F32)]
        emit_ghost(b, sems, psums, {"lut": lut, "idxs": idxs, "outs": outs})
        b.finalize()
    return nc


def ghost_host_inputs(ins):
    NIT1 = 5504 // 128
    gidx = tile_gu(ins["L1_ffn_gate_idx"]).reshape(-1)
    uidx = tile_gu(ins["L1_ffn_up_idx"]).reshape(-1)
    didx = tile_dw(ins["L1_ffn_down_idx"]).reshape(-1)
    n = gidx.size
    per = 128 * GCH
    in_maps = []
    for c in range(NCORES):
        m = {"lut": ins["L1_ffn_lut"]}
        for nm, arr in (("g", gidx), ("u", uidx), ("d", didx)):
            sl = np.zeros(per, np.int32)
            lo = c * per
            hi = min(n, (c + 1) * per)
            if hi > lo:
                sl[:hi - lo] = arr[lo:hi]
            m["idx_" + nm] = sl.reshape(128, GCH)
        in_maps.append(m)
    return in_maps


def ghost_reassemble(results):
    NIT1 = 5504 // 128
    out = {}
    for nm, shape in (("g", (NIT1, 128, D)), ("u", (NIT1, 128, D)),
                      ("d", (4, NIT1, 128, 512))):
        flat = np.concatenate([np.asarray(r["out_" + nm]).reshape(-1)
                               for r in results])
        n = int(np.prod(shape))
        out[nm] = flat[:n].reshape(shape)
    return out



DN_SB = 512
DN_MARKERS = False
DN_MARKS = []
PHASE_MARKERS = False
PHASE_MARKS = []
DN_NSB = T // DN_SB
DN_C = 16
DN_NCH = DN_SB // DN_C
SQ128 = float(np.sqrt(128.0))


def emit_deltanet(b, consts, sems, psums, io, tag="dn"):
    nc = b.nc
    TE, VE, SE = sems["TE"], sems["VE"], sems["SE"]
    lnsl = io["ln_slice"]
    wqh, wkh, wvh = io["wqh"], io["wkh"], io["wvh"]
    wfa, wga, wfbh, wgbh = io["wfa"], io["wga"], io["wfbh"], io["wgbh"]
    wbt, convw, nega, dtb, nw = (io["wbt"], io["convw"], io["nega"],
                                 io["dtb"], io["nw"])
    ldram = nc.dram_tensor(tag + "ldram", [2, DN_NCH * 256], BF16,
                           kind="Internal").ap()
    ONE = b.sem(tag + "ONE")
    LS = b.slot_sems(tag + "LS", 6)
    OGS = b.slot_sems(tag + "OGS", 2)
    LWR = b.sem(tag + "LWR")
    LRD = b.slot_sems(tag + "LRD", 2)
    identb = consts["identb"]
    identf = consts["identf"]
    eps6 = consts["cc"][1e-6]
    pA, pB, pC, pD, pX, pO = psums[0:6]
    pT0, pT1 = consts["pT2"]

    def ph(nm, shape, dt):
        return [b.sbuf(tag + nm + str(h), shape, dt) for h in range(2)]

    DN_MARKS.clear()
    mkbuf = b.sbuf(tag + "mkbuf", [1, 16], F32)

    def mark(nm):
        if not DN_MARKERS:
            return
        DN_MARKS.append(nm)
        b.wait("gpsimd", VE)
        b.wait("gpsimd", TE)
        b.inc(b.g.memset(mkbuf[0:1, 0:1], 1.0), b.sem(tag + "MK"))

    ones1 = b.sbuf(tag + "ones1", [1, 128], F32)
    onescolb = b.sbuf(tag + "onescolb", [128, 1], BF16)
    onesrep = b.sbuf(tag + "onesrep", [128, 16], BF16)
    ones16f = b.sbuf(tag + "ones16f", [128, 16], F32)
    acol = b.sbuf(tag + "acol", [128, 2], F32)
    dtbc = b.sbuf(tag + "dtbc", [128, 2], F32)
    nwcol = b.sbuf(tag + "nwcol", [128, 1], F32)
    negarow = b.sbuf(tag + "negarow", [1, 2], F32)
    dtbrow = b.sbuf(tag + "dtbrow", [2, 128], F32)
    nwrow = b.sbuf(tag + "nwrow", [1, 128], F32)
    S = ph("S", [128, 128], F32)
    Sall = ph("Sall", [128, (DN_NCH + 1) * 128], BF16)
    qx = {nm: ph("qx" + nm, [128, DN_SB + 3], F32) for nm in "qkv"}
    qn = ph("qn", [128, DN_SB], BF16)
    kn = ph("kn", [128, DN_SB], BF16)
    vv = ph("vv", [128, DN_SB], F32)
    kb = ph("kb", [128, DN_SB], BF16)
    ktb = ph("ktb", [128, DN_SB], BF16)
    vb = ph("vb", [128, DN_SB], BF16)
    qt_ = ph("qt", [128, DN_SB], BF16)
    kbar = ph("kbar", [128, DN_SB], BF16)
    ball = ph("ball", [128, DN_SB], F32)
    eblast = ph("eblast", [128, DN_NCH], F32)
    g2r = ph("g2r", [128, DN_SB], BF16)
    rawh = ph("raw", [128, DN_SB], F32)
    brow = ph("brow", [1, DN_SB], F32)
    og_sb = ph("ogsb", [128, DN_SB], BF16)
    silq = ph("silq", [128, DN_SB], F32)
    silk = ph("silk", [128, DN_SB], F32)
    wlog = b.sbuf(tag + "wlog", [128, DN_SB], F32)
    eb_t = b.sbuf(tag + "ebt", [128, DN_SB], BF16)
    ebl_t = b.sbuf(tag + "eblt", [128, DN_SB], BF16)
    fa_t = b.sbuf(tag + "fa_t", [128, DN_SB], BF16)
    ga_t = b.sbuf(tag + "ga_t", [128, DN_SB], BF16)
    sp1 = b.sbuf(tag + "sp1", [128, DN_SB], F32)
    sp2 = b.sbuf(tag + "sp2", [128, DN_SB], F32)
    dr_ = b.sbuf(tag + "dr_", [128, DN_SB], F32)
    w1_ = b.sbuf(tag + "w1_", [128, DN_SB], F32)
    scr_bf = b.sbuf(tag + "scrbf", [128, DN_SB], BF16)
    scr2_bf = b.sbuf(tag + "scr2bf", [128, DN_SB], BF16)
    rr2 = [b.sbuf(tag + f"rr{i}", [1, DN_SB], F32) for i in range(2)]
    rstdr = b.sbuf(tag + "rstdr", [1, DN_SB], F32)
    Arep = b.sbuf(tag + "Arep", [16, DN_NCH * 256], BF16)
    Tt = ph("Tt", [16, DN_NCH * 16], BF16)
    tac = b.sbuf(tag + "tac", [16, DN_NCH], F32)
    dr3 = b.sbuf(tag + "dr3", [16, DN_NCH * 16], F32)
    LtT = ph("LtT", [16, DN_NCH * 16], BF16)
    Lsr = b.sbuf(tag + "Lsr", [1, 1024], BF16)
    lnbuf = [b.sbuf(tag + f"lnbuf{j}", [128, DN_SB], BF16) for j in range(6)]
    wfb_s = ph("wfb_s", [128, 128], BF16)
    wgb_s = ph("wgb_s", [128, 128], BF16)
    wbt_s = b.sbuf(tag + "wbt_s", [128, 32], BF16)
    cv_s = b.sbuf(tag + "cv_s", [128, 24], F32)

    n_one = [0]

    def oneload(dst_ap, src_ap):
        b.inc(b.g.dma_start(out=dst_ap, in_=src_ap), ONE, 16)
        n_one[0] += 16

    dnwbf = nc.dram_tensor(tag + "wbf", [8, 128, 2048], BF16,
                           kind="Internal").ap()
    wsrcs = [wqh[0], wqh[1], wkh[0], wkh[1], wvh[0], wvh[1], wfa, wga]
    for i2, s2 in enumerate(wsrcs):
        oneload(dnwbf[i2].rearrange("p (c i) -> p c i", c=16),
                s2.rearrange("c p i -> p c i"))
    for h in range(2):
        oneload(wfb_s[h][:], wfbh[h])
        oneload(wgb_s[h][:], wgbh[h])
    oneload(wbt_s[:].rearrange("p (c i) -> p c i", c=16),
            wbt.rearrange("c p i -> p c i"))
    oneload(cv_s[:].rearrange("p (n h i) -> p n h i", n=3, h=2),
            convw.rearrange("n h p i -> p n h i"))
    oneload(negarow[:], nega)
    oneload(dtbrow[:], dtb)
    oneload(nwrow[:], nw)
    b.wait("vector", ONE, n_one[0])
    b.wait("scalar", ONE, n_one[0])
    b.wait("tensor", ONE, n_one[0])
    for t_, v_ in ((ones1, 1.0), (onescolb, 1.0), (onesrep, 1.0),
                   (ones16f, 1.0)):
        b.inc(b.v.memset(t_[:], v_), VE)
    for h in range(2):
        b.inc(b.v.memset(S[h][:], 0.0), VE)
        b.inc(b.v.memset(Sall[h][:, 0:128], 0.0), VE)
        for nm in "qkv":
            b.inc(b.v.memset(qx[nm][h][:, 0:3], 0.0), VE)
    b.fence("vector")
    b.wait("scalar", VE)
    b.inc(b.s.activation(out=negarow[:], in_=negarow[:], func=AF.Exp), SE)
    b.wait("vector", SE)
    b.inc(b.v.tensor_scalar(out=negarow[:], in0=negarow[:], scalar1=-1.0,
                            scalar2=None, op0=ALU.mult), VE)
    b.fence("vector")
    b.wait("tensor", VE)
    b.inc(b.t.matmul(out=pA[:, 0:2], lhsT=ones1[:], rhs=negarow[:],
                     start=True, stop=True), TE)
    b.wait("vector", TE)
    b.inc(b.v.tensor_copy(out=acol[:], in_=pA[:, 0:2]), VE)
    b.fence("vector")
    b.wait("tensor", VE)
    tr = b.t.transpose(out=pX[:, 0:2], in_=dtbrow[:],
                       identity=consts["identf"][0:2, 0:2])
    b.inc(tr, TE)
    b.wait("vector", TE)
    b.inc(b.v.tensor_copy(out=dtbc[:], in_=pX[:, 0:2]), VE)
    b.fence("vector")
    b.wait("tensor", VE)
    tr = b.t.transpose(out=pD[:, 0:1], in_=nwrow[:],
                       identity=consts["identf"][0:1, 0:1])
    b.inc(tr, TE)
    b.wait("vector", TE)
    b.inc(b.v.tensor_copy(out=nwcol[:], in_=pD[:, 0:1]), VE)
    b.fence("vector")
    b.wait("tensor", VE)

    ln_n = [0] * 6
    ln_te = [0] * 6
    og_n = [0, 0]
    lwr_n = [0]
    lrd_n = [0, 0]
    wt_n = [0]
    WT = b.sem(tag + "WT")
    sall_carry = [0, 0]
    trec_last = [0, 0]
    cve_last = [0]
    Pd_stride = DN_NCH * 256

    def cview(t_ap, inner=16):
        return t_ap.rearrange("p (c x) -> p c x", x=inner)

    for sb in range(DN_NSB):
        t0 = sb * DN_SB

        mark(f"A_{sb}")
        wctx = ExitStack()
        _oldctx = b.ctx
        b.ctx = wctx
        wq_s = [b.sbuf(tag + f"wq{sb}_{h}", [128, 2048], BF16)
                for h in range(2)]
        wk_s = [b.sbuf(tag + f"wk{sb}_{h}", [128, 2048], BF16)
                for h in range(2)]
        wv_s = [b.sbuf(tag + f"wv{sb}_{h}", [128, 2048], BF16)
                for h in range(2)]
        wfa_s = b.sbuf(tag + f"wfa{sb}", [128, 2048], BF16)
        wga_s = b.sbuf(tag + f"wga{sb}", [128, 2048], BF16)
        b.ctx = _oldctx
        if sb == 0:
            b.wait("sync", ONE, n_one[0])
        else:
            b.wait("sync", TE)
            b.wait("sync", VE)
        for i2, dst in enumerate([wq_s[0], wq_s[1], wk_s[0], wk_s[1],
                                  wv_s[0], wv_s[1], wfa_s, wga_s]):
            b.inc(b.y.dma_start(out=dst[:], in_=dnwbf[i2]), WT, 16)
            wt_n[0] += 1
        b.wait("tensor", WT, wt_n[0] * 16)
        groups = [[("q", 0, pA), ("k", 0, pB), ("v", 0, pC)],
                  [("q", 1, pA), ("k", 1, pB), ("v", 1, pC)],
                  [("fa", None, pA), ("ga", None, pB),
                   ("bt0", None, pC), ("bt1", None, pD)]]
        for gi, grp in enumerate(groups):
            for c in range(16):
                j = (sb * 48 + gi * 16 + c) % 6
                if ln_n[j] >= 1:
                    b.wait("sync", TE, ln_te[j])
                d = b.y.dma_start(out=lnbuf[j][:],
                                  in_=lnsl(c, t0, DN_SB))
                b.inc(d, LS[j], 16)
                ln_n[j] += 1
                b.wait("tensor", LS[j], ln_n[j] * 16)
                if c == 0 and (sb > 0 or gi > 0):
                    b.wait("tensor", VE)
                    b.wait("tensor", SE)
                mm = None
                for kind, h, ps in grp:
                    if kind == "q":
                        lhs, oap = wq_s[h][:, c * 128:(c + 1) * 128], ps[:]
                    elif kind == "k":
                        lhs, oap = wk_s[h][:, c * 128:(c + 1) * 128], ps[:]
                    elif kind == "v":
                        lhs, oap = wv_s[h][:, c * 128:(c + 1) * 128], ps[:]
                    elif kind == "fa":
                        lhs, oap = wfa_s[:, c * 128:(c + 1) * 128], ps[:]
                    elif kind == "ga":
                        lhs, oap = wga_s[:, c * 128:(c + 1) * 128], ps[:]
                    elif kind == "bt0":
                        lhs, oap = wbt_s[:, c * 2:c * 2 + 1], ps[0:1, :]
                    else:
                        lhs, oap = wbt_s[:, c * 2 + 1:c * 2 + 2], ps[0:1, :]
                    mm = b.t.matmul(out=oap, lhsT=lhs, rhs=lnbuf[j][:],
                                    start=(c == 0), stop=(c == 15))
                b.inc(mm, TE)
                ln_te[j] = TE.count

            b.wait("vector", TE)
            b.wait("scalar", TE)
            if gi < 2:
                h = gi
                for nm, ps in (("q", pA), ("k", pB), ("v", pC)):
                    if sb > 0:
                        b.inc(b.v.tensor_copy(out=qx[nm][h][:, 0:3],
                                              in_=qx[nm][h][:, DN_SB:DN_SB + 3]), VE)
                        b.fence("vector")
                    b.inc(b.v.tensor_copy(out=qx[nm][h][:, 3:DN_SB + 3],
                                          in_=ps[:]), VE)
                    b.fence("vector")
            else:
                b.inc(b.v.tensor_copy(out=fa_t[:], in_=pA[:]), VE)
                b.inc(b.v.tensor_copy(out=ga_t[:], in_=pB[:]), VE)
                b.fence("vector")
                b.wait("scalar", VE)
                b.inc(b.s.activation(out=brow[0][:], in_=pC[0:1, :],
                                     func=AF.Sigmoid), SE)
                b.inc(b.s.activation(out=brow[1][:], in_=pD[0:1, :],
                                     func=AF.Sigmoid), SE)
                b.wait("vector", SE)

        convdst = {(0, "q"): dr_, (0, "k"): w1_, (0, "v"): sp2,
                   (1, "q"): sp1, (1, "k"): rawh[0], (1, "v"): rawh[1]}
        for h in range(2):
            for nm in "qkv":
                ni = "qkv".index(nm)
                dst = convdst[(h, nm)]

                def cvcol(tap):
                    o = ni * 8 + h * 4 + tap
                    return cv_s[:, o:o + 1]
                b.inc(b.v.tensor_scalar(out=dst[:],
                                        in0=qx[nm][h][:, 0:DN_SB],
                                        scalar1=cvcol(0), scalar2=None,
                                        op0=ALU.mult), VE)
                b.fence("vector")
                for tap in range(1, 4):
                    b.inc(b.v.scalar_tensor_tensor(
                        out=dst[:], in0=qx[nm][h][:, tap:tap + DN_SB],
                        scalar=cvcol(tap), in1=dst[:],
                        op0=ALU.mult, op1=ALU.add), VE)
                    b.fence("vector")
        b.wait("scalar", VE)
        for h in range(2):
            b.inc(b.s.activation(out=silq[h][:], in_=convdst[(h, "q")][:],
                                 func=AF.Silu), SE)
            b.inc(b.s.activation(out=silk[h][:], in_=convdst[(h, "k")][:],
                                 func=AF.Silu), SE)
            b.inc(b.s.activation(out=vv[h][:], in_=convdst[(h, "v")][:],
                                 func=AF.Silu), SE)
        b.fence("scalar")
        for wave in range(2):
            srcs = (silq[0], silq[1]) if wave == 0 else (silk[0], silk[1])
            dstn = qn if wave == 0 else kn
            if wave == 1:
                b.wait("scalar", TE)
                b.wait("scalar", VE)
            b.inc(b.s.activation(out=scr_bf[:], in_=srcs[0][:],
                                 func=AF.Square), SE)
            b.inc(b.s.activation(out=scr2_bf[:], in_=srcs[1][:],
                                 func=AF.Square), SE)
            b.wait("tensor", SE)
            b.wait("tensor", VE)
            b.inc(b.t.matmul(out=pA[0:1, :], lhsT=onescolb[:],
                             rhs=scr_bf[:], start=True, stop=True), TE)
            b.inc(b.t.matmul(out=pB[0:1, :], lhsT=onescolb[:],
                             rhs=scr2_bf[:], start=True, stop=True), TE)
            b.wait("scalar", TE)
            b.inc(b.s.activation(out=rr2[0][:], in_=pA[0:1, :],
                                 func=AF.Sqrt, bias=eps6[0:1, :]), SE)
            b.inc(b.s.activation(out=rr2[1][:], in_=pB[0:1, :],
                                 func=AF.Sqrt, bias=eps6[0:1, :]), SE)
            b.wait("vector", SE)
            if wave == 0:
                b.inc(b.v.tensor_scalar(out=rr2[0][:], in0=rr2[0][:],
                                        scalar1=SQ128, scalar2=None,
                                        op0=ALU.mult), VE)
                b.inc(b.v.tensor_scalar(out=rr2[1][:], in0=rr2[1][:],
                                        scalar1=SQ128, scalar2=None,
                                        op0=ALU.mult), VE)
                b.fence("vector")
            b.inc(b.v.reciprocal(out=rr2[0][:], in_=rr2[0][:]), VE)
            b.fence("vector")
            b.inc(b.v.reciprocal(out=rr2[1][:], in_=rr2[1][:]), VE)
            b.fence("vector")
            b.wait("tensor", VE)
            b.inc(b.t.matmul(out=pC[:], lhsT=ones1[:], rhs=rr2[0][:],
                             start=True, stop=True), TE)
            b.inc(b.t.matmul(out=pD[:], lhsT=ones1[:], rhs=rr2[1][:],
                             start=True, stop=True), TE)
            b.wait("vector", TE)
            b.inc(b.v.tensor_mul(out=dstn[0][:], in0=srcs[0][:],
                                 in1=pC[:]), VE)
            b.inc(b.v.tensor_mul(out=dstn[1][:], in0=srcs[1][:],
                                 in1=pD[:]), VE)
            b.fence("vector")
        b.wait("tensor", VE)
        wctx.close()

        mark(f"B_{sb}")
        b.inc(b.t.matmul(out=pA[:], lhsT=wfb_s[0][:], rhs=fa_t[:],
                         start=True, stop=True), TE)
        b.inc(b.t.matmul(out=pB[:], lhsT=wfb_s[1][:], rhs=fa_t[:],
                         start=True, stop=True), TE)
        b.inc(b.t.matmul(out=pC[:], lhsT=wgb_s[0][:], rhs=ga_t[:],
                         start=True, stop=True), TE)
        b.inc(b.t.matmul(out=pD[:], lhsT=wgb_s[1][:], rhs=ga_t[:],
                         start=True, stop=True), TE)
        b.wait("vector", TE)
        b.inc(b.v.tensor_scalar(out=rawh[0][:], in0=pA[:],
                                scalar1=dtbc[:, 0:1], scalar2=None,
                                op0=ALU.add), VE)
        b.inc(b.v.tensor_scalar(out=rawh[1][:], in0=pB[:],
                                scalar1=dtbc[:, 1:2], scalar2=None,
                                op0=ALU.add), VE)
        b.inc(b.v.tensor_copy(out=g2r[0][:], in_=pC[:]), VE)
        b.inc(b.v.tensor_copy(out=g2r[1][:], in_=pD[:]), VE)
        b.fence("vector")
        b.wait("scalar", VE)
        b.inc(b.s.activation(out=sp1[:], in_=rawh[0][:], func=AF.Abs), SE)
        b.inc(b.s.activation(out=sp2[:], in_=rawh[1][:], func=AF.Abs), SE)
        b.fence("scalar")
        b.inc(b.s.activation(out=sp1[:], in_=sp1[:], func=AF.Exp,
                             scale=-1.0), SE)
        b.inc(b.s.activation(out=sp2[:], in_=sp2[:], func=AF.Exp,
                             scale=-1.0), SE)
        b.fence("scalar")
        b.inc(b.s.activation(out=sp1[:], in_=sp1[:], func=AF.Ln, bias=1.0), SE)
        b.inc(b.s.activation(out=sp2[:], in_=sp2[:], func=AF.Ln, bias=1.0), SE)
        b.wait("vector", SE)
        b.wait("tensor", VE)
        for h in range(2):
            sph = sp1 if h == 0 else sp2
            b.inc(b.v.tensor_scalar(out=dr_[:], in0=rawh[h][:], scalar1=0.0,
                                    scalar2=None, op0=ALU.max), VE)
            b.fence("vector")
            b.inc(b.v.tensor_add(out=dr_[:], in0=dr_[:], in1=sph[:]), VE)
            b.fence("vector")
            b.inc(b.v.tensor_scalar(out=wlog[:], in0=dr_[:],
                                    scalar1=acol[:, h:h + 1], scalar2=None,
                                    op0=ALU.mult), VE)
            b.fence("vector")
            for i in range(DN_NCH):
                sl = slice(i * 16, (i + 1) * 16)
                b.inc(b.v.tensor_tensor_scan(
                    out=ball[h][:, sl], data0=ones16f[:],
                    data1=wlog[:, sl], initial=0.0,
                    op0=ALU.mult, op1=ALU.add), VE)
            b.fence("vector")
            for i in range(DN_NCH):
                sl = slice(i * 16, (i + 1) * 16)
                b.inc(b.v.tensor_scalar(
                    out=w1_[:, sl], in0=ball[h][:, sl],
                    scalar1=ball[h][:, i * 16 + 15:i * 16 + 16],
                    scalar2=None, op0=ALU.subtract), VE)
            b.fence("vector")
            if h == 1:
                b.wait("scalar", VE)
            else:
                b.wait("scalar", VE)
            b.inc(b.s.activation(out=eb_t[:], in_=ball[h][:],
                                 func=AF.Exp), SE)
            b.inc(b.s.activation(out=ebl_t[:], in_=w1_[:], func=AF.Exp,
                                 scale=-1.0), SE)
            b.inc(b.s.activation(
                out=eblast[h][:].unsqueeze(2),
                in_=ball[h][:].rearrange("p (c x) -> p c x", x=16)[:, :, 15:16],
                func=AF.Exp), SE)
            b.wait("tensor", VE)
            b.inc(b.t.matmul(out=pX[:], lhsT=ones1[:], rhs=brow[h][:],
                             start=True, stop=True), TE)
            b.wait("vector", TE)
            b.inc(b.v.tensor_mul(out=kb[h][:], in0=kn[h][:], in1=pX[:]), VE)
            b.inc(b.v.tensor_mul(out=vb[h][:], in0=vv[h][:], in1=pX[:]), VE)
            b.fence("vector")
            b.wait("vector", SE)
            b.inc(b.v.tensor_mul(out=qt_[h][:], in0=qn[h][:], in1=eb_t[:]), VE)
            b.inc(b.v.tensor_mul(out=kbar[h][:], in0=kn[h][:],
                                 in1=ebl_t[:]), VE)
            b.fence("vector")
            b.inc(b.v.tensor_mul(out=ktb[h][:], in0=kb[h][:], in1=eb_t[:]), VE)
            b.fence("vector")
            b.wait("tensor", VE)

        with b.scope():
            Pd = b.sbuf(tag + f"Pd{sb}", [128, DN_NCH * 256], BF16)
            Pl = b.sbuf(tag + f"Pl{sb}", [128, DN_NCH * 256], BF16)

            def strided_out(handle, r, width):
                return _BassAP(handle, 16 * r,
                               [[Pd_stride, 128], [256, DN_NCH], [17, width]])

            b.wait("vector", TE)
            b.inc(b.v.memset(Pl[:], 0.0), VE)
            b.fence("vector")
            GP = b.sem(tag + "GP")
            b.wait("gpsimd", TE)
            b.inc(b.g.memset(Pd[:], 0.0), GP)
            b.wait("vector", GP)
            for h in range(2):
                mark(f"rp_{sb}_{h}")
                if h == 1:
                    b.wait("vector", TE)
                b.inc(b.v.tensor_tensor(out=strided_out(Pl, 0, 16),
                                        in0=cview(qn[h][:]),
                                        in1=cview(kn[h][:]),
                                        op=ALU.mult), VE)
                b.wait("gpsimd", VE)
                drs = [dr_, sp1]
                w1s = [w1_, sp2]
                w1b = [scr_bf, scr2_bf]
                exp_c = {}
                mul_c = {}
                for r in range(1, 16):
                    n = DN_SB - r
                    s2 = r % 2
                    if r >= 3:
                        b.wait("gpsimd", SE, exp_c[r - 2])
                    gsub = b.g.tensor_tensor(out=drs[s2][:, 0:n],
                                             in0=ball[h][:, r:DN_SB],
                                             in1=ball[h][:, 0:n],
                                             op=ALU.subtract)
                    b.inc(gsub, GP, 1)
                    b.wait("scalar", GP)
                    if r >= 3:
                        b.wait("scalar", VE, mul_c[r - 2])
                    e = b.s.activation(out=w1s[s2][:, 0:n],
                                       in_=drs[s2][:, 0:n], func=AF.Exp)
                    exp_c[r] = b.inc(e, SE)
                    b.wait("vector", SE, exp_c[r])
                    m = b.v.tensor_mul(out=w1b[s2][:, 0:n],
                                       in0=w1s[s2][:, 0:n],
                                       in1=kn[h][:, 0:n])
                    mul_c[r] = b.inc(m, VE)
                    b.fence("vector")
                    wv_ = cview(w1b[s2][:])[:, :, 0:16 - r]
                    b.inc(b.v.tensor_tensor(out=strided_out(Pd, r, 16 - r),
                                            in0=wv_,
                                            in1=cview(kb[h][:])[:, :, r:16],
                                            op=ALU.mult), VE)
                    b.inc(b.v.tensor_tensor(out=strided_out(Pl, r, 16 - r),
                                            in0=wv_,
                                            in1=cview(qn[h][:])[:, :, r:16],
                                            op=ALU.mult), VE)
                    b.fence("vector")
                mark(f"DK_{sb}_{h}")
                b.wait("tensor", VE)
                veA = {}
                seL = {}
                for half in range(16):
                    bnkA = pC if half % 2 == 0 else pX
                    bnkL = pD if half % 2 == 0 else pO
                    if half >= 2:
                        b.wait("tensor", VE, veA[half - 2])
                    mm = b.t.matmul(out=bnkA[0:16, :], lhsT=onesrep[:],
                                    rhs=Pd[:, half * 512:(half + 1) * 512],
                                    start=True, stop=True)
                    b.inc(mm, TE)
                    b.wait("vector", TE)
                    cp = b.v.tensor_copy(
                        out=Arep[0:16, half * 512:(half + 1) * 512],
                        in_=bnkA[0:16, :])
                    veA[half] = b.inc(cp, VE)
                    cve_last[0] = veA[half]
                    if half >= 2:
                        b.wait("tensor", SE, seL[half - 2])
                    mm = b.t.matmul(out=bnkL[0:16, :], lhsT=onesrep[:],
                                    rhs=Pl[:, half * 512:(half + 1) * 512],
                                    start=True, stop=True)
                    b.inc(mm, TE)
                    b.wait("scalar", TE)
                    if half % 2 == 0 and lwr_n[0] > 0:
                        b.wait("scalar", LWR, lwr_n[0] * 16)
                    cl = b.s.activation(out=Lsr[0:1,
                                                (half % 2) * 512:
                                                (half % 2 + 1) * 512],
                                        in_=bnkL[0:1, :], func=AF.Copy)
                    seL[half] = b.inc(cl, SE)
                    if half % 2 == 1:
                        b.wait("sync", SE)
                        if half == 1 and lrd_n[h] > 0:
                            b.wait("sync", LRD[h], lrd_n[h] * 16)
                        dsh = b.y.dma_start(
                            out=ldram[h][None, (half - 1) * 512:
                                         (half + 1) * 512],
                            in_=Lsr[0:1, :])
                        b.inc(dsh, LWR, 16)
                        lwr_n[0] += 1
                b.wait("gpsimd", LWR, lwr_n[0] * 16)
                dget = b.g.dma_start(
                    out=LtT[h][:],
                    in_=ldram[h].rearrange("(c t s) -> s (c t)", s=16, t=16),
                    allow_slow_non_contiguous=True)
                b.inc(dget, LRD[h], 16)
                lrd_n[h] += 1
                mark(f"Trec_{sb}_{h}")
                AV = Arep[:].rearrange("p (g x) -> p g x", x=256)
                TV = Tt[h][:].rearrange("p (g x) -> p g x", x=16)
                D3 = dr3[:].rearrange("p (g x) -> p g x", x=16)
                b.inc(b.v.memset(Tt[h][:], 0.0), VE)
                b.fence("vector")
                for i in range(16):
                    if i > 0:
                        b.inc(b.v.tensor_tensor(
                            out=D3[:, :, 0:i],
                            in0=AV[:, :, i * 16:i * 16 + i],
                            in1=TV[:, :, 0:i], op=ALU.mult), VE)
                        b.fence("vector")
                        b.inc(b.v.reduce_sum(out=tac[:], in_=D3[:, :, 0:i],
                                             axis=AX), VE)
                        b.fence("vector")
                        trec_last[h] = b.inc(b.v.tensor_scalar(
                            out=TV[:, :, i:i + 1],
                            in0=tac[:].unsqueeze(2), scalar1=-1.0,
                            scalar2=identf[0:16, i:i + 1],
                            op0=ALU.mult, op1=ALU.add), VE)
                    else:
                        b.inc(b.v.tensor_scalar(
                            out=TV[:, :, 0:1],
                            in0=TV[:, :, 0:1], scalar1=0.0,
                            scalar2=identf[0:16, 0:1],
                            op0=ALU.mult, op1=ALU.add), VE)
                    b.fence("vector")

        with b.scope():
            FT0 = ph(f"FT0_{sb}_", [128, DN_NCH * 128], BF16)
            kbTa = ph(f"kbTa{sb}_", [16, DN_NCH * 128], BF16)
            Wt = ph(f"Wt{sb}_", [16, DN_NCH * 128], BF16)
            nUt = ph(f"nUt{sb}_", [128, DN_NCH * 16], BF16)
            Ob = ph(f"Ob{sb}_", [16, 2 * 512], BF16)
            dlt = ph(f"dlt{sb}_", [16, 2 * 128], BF16)
            nust = b.sbuf(tag + f"nust{sb}", [16, DN_NCH * 128], BF16)
            rhsT = b.sbuf(tag + f"rhsT{sb}", [16, 8 * 256], BF16)
            b.wait("vector", TE)
            st = {}

            def eh_stream(h):
                trp = pT0 if h == 0 else pT1
                fbk = pC if h == 0 else pD
                gbk = pA if h == 0 else pB
                obk = pO if h == 0 else pX
                FTh, kbh, Wth = FT0[h][:], kbTa[h][:], Wt[h][:]
                nUh, Obh, dlh = nUt[h][:], Ob[h][:], dlt[h][:]
                b.wait("tensor", LRD[h], lrd_n[h] * 16)
                b.wait("tensor", SE)
                b.wait("tensor", VE, max(cve_last[0], trec_last[h]))
                mark(f"E1_{sb}_{h}")
                ev = 0
                for grp in range(4):
                    if ev:
                        b.wait("tensor", SE, ev)
                    tc = 0
                    for j2 in range(8):
                        c = grp * 8 + j2
                        tc = b.inc(b.t.transpose(
                            out=trp[0:16, j2 * 128:(j2 + 1) * 128],
                            in_=kbar[h][:, c * 16:(c + 1) * 16],
                            identity=identb[:]), TE)
                    b.wait("scalar", TE, tc)
                    ev = b.inc(b.s.activation(
                        out=kbh[:, grp * 1024:(grp + 1) * 1024],
                        in_=trp[0:16, 0:1024], func=AF.Copy), SE)
                    yield "e"
                mark(f"E2_{sb}_{h}")
                if st.get("nustfree"):
                    b.wait("scalar", TE, st["nustfree"])
                sv = {}
                for grp in range(4):
                    b.wait("tensor", SE, ev)
                    tc = 0
                    for j2 in range(8):
                        c = grp * 8 + j2
                        tc = b.inc(b.t.transpose(
                            out=trp[0:16, j2 * 128:(j2 + 1) * 128],
                            in_=vb[h][:, c * 16:(c + 1) * 16],
                            identity=identb[:]), TE)
                    b.wait("scalar", TE, tc)
                    ev = b.inc(b.s.activation(
                        out=rhsT[:].rearrange("p (g x) -> p g x",
                                              x=256)[:, :, 0:128],
                        in_=trp[0:16, 0:1024].rearrange(
                            "p (g x) -> p g x", x=128), func=AF.Copy), SE)
                    yield "e"
                    b.wait("tensor", SE, ev)
                    tc = 0
                    for j2 in range(8):
                        c = grp * 8 + j2
                        tc = b.inc(b.t.transpose(
                            out=trp[0:16, j2 * 128:(j2 + 1) * 128],
                            in_=ktb[h][:, c * 16:(c + 1) * 16],
                            identity=identb[:]), TE)
                    b.wait("scalar", TE, tc)
                    ev = b.inc(b.s.activation(
                        out=rhsT[:].rearrange("p (g x) -> p g x",
                                              x=256)[:, :, 128:256],
                        in_=trp[0:16, 0:1024].rearrange(
                            "p (g x) -> p g x", x=128), func=AF.Copy), SE)
                    yield "e"
                    b.wait("tensor", SE, ev)
                    for pair in range(4):
                        c0 = grp * 8 + pair * 2
                        bank = fbk if pair % 2 == 0 else gbk
                        if pair >= 2:
                            b.wait("tensor", SE, sv[pair - 2])
                        mmi = 0
                        for cc in (c0, c0 + 1):
                            g2 = cc - c0
                            mmi = b.inc(b.t.matmul(
                                out=bank[0:16, g2 * 256:(g2 + 1) * 256],
                                lhsT=Tt[h][:, cc * 16:(cc + 1) * 16],
                                rhs=rhsT[0:16, (cc % 8) * 256:
                                         ((cc % 8) + 1) * 256],
                                start=True, stop=True), TE)
                        b.wait("scalar", TE, mmi)
                        bv = bank[0:16, 0:512].rearrange(
                            "p (g x) -> p g x", x=256)
                        b.inc(b.s.activation(
                            out=Wth[:, c0 * 128:(c0 + 2) * 128].rearrange(
                                "p (g x) -> p g x", x=128),
                            in_=bv[:, :, 0:128], func=AF.Copy), SE)
                        sv[pair] = b.inc(b.s.activation(
                            out=nust[:, c0 * 128:(c0 + 2) * 128].rearrange(
                                "p (g x) -> p g x", x=128),
                            in_=bv[:, :, 128:256], func=AF.Copy,
                            scale=-1.0), SE)
                    yield "e"
                mark(f"E3_{sb}_{h}")
                b.wait("tensor", SE, sv[3])
                tc = 0
                for c in range(DN_NCH):
                    tc = b.inc(b.t.transpose(
                        out=trp[:, c * 16:(c + 1) * 16],
                        in_=nust[0:16, c * 128:(c + 1) * 128],
                        identity=identb[0:16, 0:16]), TE)
                b.wait("scalar", TE, tc)
                ev = b.inc(b.s.activation(out=nUh[:], in_=trp[:, 0:512],
                                          func=AF.Copy), SE)
                yield "e"
                mark(f"F_{sb}_{h}")
                fe = 0
                fmm = 0
                for c in range(DN_NCH):
                    if fe:
                        b.wait("tensor", SE, fe)
                    fmm = b.inc(b.t.matmul(
                        out=fbk[:, 0:128],
                        lhsT=nust[0:16, c * 128:(c + 1) * 128],
                        rhs=kbh[0:16, c * 128:(c + 1) * 128],
                        start=True, stop=True), TE)
                    b.wait("scalar", TE, fmm)
                    fe = b.inc(b.s.activation(
                        out=FTh[:, c * 128:(c + 1) * 128],
                        in_=fbk[:, 0:128], func=AF.Copy), SE)
                    if c % 4 == 3:
                        yield "e"
                st["nustfree"] = fmm
                mark(f"G_{sb}_{h}")
                stt_i = 0
                cp_i = sall_carry[h]
                b.wait("tensor", SE, fe)
                for c in range(DN_NCH):
                    if stt_i:
                        b.wait("tensor", VE, stt_i)
                    b.t.matmul(out=gbk[:, 0:128],
                               lhsT=kbh[0:16, c * 128:(c + 1) * 128],
                               rhs=Wth[0:16, c * 128:(c + 1) * 128],
                               start=True, stop=False)
                    if cp_i:
                        b.wait("tensor", VE, cp_i)
                    mmf = b.inc(b.t.matmul(
                        out=gbk[:, 0:128],
                        lhsT=FTh[:, c * 128:(c + 1) * 128],
                        rhs=Sall[h][:, c * 128:(c + 1) * 128],
                        start=False, stop=True), TE)
                    b.wait("vector", TE, mmf)
                    stt_i = b.inc(b.v.scalar_tensor_tensor(
                        out=S[h][:], in0=S[h][:],
                        scalar=eblast[h][:, c:c + 1], in1=gbk[:, 0:128],
                        op0=ALU.mult, op1=ALU.add), VE)
                    b.fence("vector")
                    cp_i = b.inc(b.v.tensor_copy(
                        out=Sall[h][:, (c + 1) * 128:(c + 2) * 128],
                        in_=S[h][:]), VE)
                    b.fence("vector")
                    yield "g"
                mark(f"H_{sb}_{h}")
                ad_i = cp_i
                ev_i = {}
                tr_i = {}
                mmo = 0
                for c in range(DN_NCH):
                    g4 = c // 4
                    if ad_i:
                        b.wait("tensor", VE, ad_i)
                    mmd = b.inc(b.t.matmul(
                        out=gbk[0:16, 0:128],
                        lhsT=nUh[:, c * 16:(c + 1) * 16],
                        rhs=Sall[h][:, c * 128:(c + 1) * 128],
                        start=True, stop=True), TE)
                    b.wait("vector", TE, mmd)
                    ad_i = b.inc(b.v.tensor_add(
                        out=dlh[0:16, (c % 2) * 128:(c % 2 + 1) * 128],
                        in0=gbk[0:16, 0:128],
                        in1=Wth[0:16, c * 128:(c + 1) * 128]), VE)
                    b.fence("vector")
                    if c % 4 == 0 and g4 >= 1:
                        b.wait("tensor", SE, ev_i[g4 - 1])
                    b.t.matmul(
                        out=obk[0:16, (c % 4) * 128:(c % 4 + 1) * 128],
                        lhsT=qt_[h][:, c * 16:(c + 1) * 16],
                        rhs=Sall[h][:, c * 128:(c + 1) * 128],
                        start=True, stop=False)
                    b.wait("tensor", VE, ad_i)
                    mmo = b.inc(b.t.matmul(
                        out=obk[0:16, (c % 4) * 128:(c % 4 + 1) * 128],
                        lhsT=LtT[h][:, c * 16:(c + 1) * 16],
                        rhs=dlh[0:16, (c % 2) * 128:(c % 2 + 1) * 128],
                        start=False, stop=True), TE)
                    if c % 4 == 3:
                        b.wait("scalar", TE, mmo)
                        oslot = g4 % 2
                        if g4 >= 2:
                            b.wait("scalar", TE, tr_i[g4 - 2])
                        ev_i[g4] = b.inc(b.s.activation(
                            out=Obh[0:16, oslot * 512:(oslot + 1) * 512],
                            in_=obk[0:16, 0:512], func=AF.Copy), SE)
                        b.wait("tensor", SE, ev_i[g4])
                        tc = 0
                        for j2 in range(4):
                            cc = g4 * 4 + j2
                            tc = b.inc(b.t.transpose(
                                out=trp[:, cc * 16:(cc + 1) * 16],
                                in_=Obh[0:16,
                                        oslot * 512 + j2 * 128:
                                        oslot * 512 + (j2 + 1) * 128],
                                identity=identb[0:16, 0:16]), TE)
                        tr_i[g4] = tc
                    yield "h"
                b.wait("vector", TE, mmo)
                sall_carry[h] = b.inc(b.v.tensor_copy(
                    out=Sall[h][:, 0:128],
                    in_=Sall[h][:, DN_NCH * 128:(DN_NCH + 1) * 128]), VE)
                b.fence("vector")
                mark(f"epi_{sb}_{h}")
                b.wait("scalar", TE)
                b.wait("scalar", VE)
                b.inc(b.s.activation(out=scr_bf[:], in_=trp[:, 0:512],
                                     func=AF.Square), SE)
                b.wait("tensor", SE)
                b.inc(b.t.matmul(out=fbk[0:1, 0:512], lhsT=onescolb[:],
                                 rhs=scr_bf[:], start=True, stop=True), TE)
                b.wait("scalar", TE)
                b.inc(b.s.activation(out=rstdr[:], in_=fbk[0:1, :],
                                     func=AF.Sqrt, scale=1.0 / 128.0,
                                     bias=eps6[0:1, :]), SE)
                b.fence("scalar")
                b.wait("scalar", VE)
                b.inc(b.s.activation(out=sp1[:], in_=g2r[h][:],
                                     func=AF.Sigmoid), SE)
                b.wait("vector", SE)
                b.inc(b.v.reciprocal(out=rstdr[:], in_=rstdr[:]), VE)
                b.fence("vector")
                b.wait("tensor", VE)
                b.inc(b.t.matmul(out=obk[:], lhsT=ones1[:], rhs=rstdr[:],
                                 start=True, stop=True), TE)
                b.wait("vector", TE)
                b.inc(b.v.tensor_copy(out=rawh[0][:], in_=trp[:, 0:512]), VE)
                b.fence("vector")
                if sb > 0:
                    b.wait("vector", OGS[h], og_n[h] * 16)
                b.inc(b.v.scalar_tensor_tensor(
                    out=sp2[:], in0=rawh[0][:], scalar=nwcol[:],
                    in1=obk[:], op0=ALU.mult, op1=ALU.mult), VE)
                b.fence("vector")
                b.inc(b.v.tensor_mul(out=og_sb[h][:], in0=sp2[:],
                                     in1=sp1[:]), VE)
                b.fence("vector")
                b.wait("sync", VE)
                dso = b.y.dma_start(out=io["og"](h)[:, t0:t0 + DN_SB],
                                    in_=og_sb[h][:])
                b.inc(dso, OGS[h], 16)
                og_n[h] += 1
                yield "z"

            def adv(g2):
                try:
                    return next(g2)
                except StopIteration:
                    return None

            sA = eh_stream(0)
            sB = eh_stream(1)
            tA = adv(sA)
            while tA == "e":
                tA = adv(sA)
            tB = adv(sB)
            while tA is not None and tB is not None:
                tA = adv(sA)
                tB = adv(sB)
            while tA is not None:
                tA = adv(sA)
            while tB is not None:
                tB = adv(sB)

    b.wait("sync", OGS[0], og_n[0] * 16)
    b.wait("sync", OGS[1], og_n[1] * 16)
    return {"OGS": OGS, "og_n": og_n}


def build_deltanet_launch():
    nc = bass.Bass("TRN2", target_bir_lowering=False, debug=False)
    lnT4 = nc.dram_tensor("lnT4", [16, 128, T], BF16, kind="ExternalInput").ap()
    names = [("wqh", [2, 16, 128, 128]), ("wkh", [2, 16, 128, 128]),
             ("wvh", [2, 16, 128, 128]), ("wfa", [16, 128, 128]),
             ("wga", [16, 128, 128]), ("wfbh", [2, 128, 128]),
             ("wgbh", [2, 128, 128]), ("wbt", [16, 128, 2]),
             ("convw", [3, 2, 128, 4]), ("nega", [1, 2]), ("dtb", [2, 128]),
             ("nw", [1, 128])]
    io = {nm: nc.dram_tensor(nm, sh, F32, kind="ExternalInput").ap()
          for nm, sh in names}
    o_og = nc.dram_tensor("o_og", [2, 128, T], BF16, kind="ExternalOutput").ap()
    io["ln_slice"] = lambda c, t0, n: lnT4[c, :, t0:t0 + n]
    io["og"] = lambda h: o_og[h]
    with ExitStack() as ctx:
        b = Builder(nc, ctx)
        sems = {k: b.sem(k) for k in ("TE", "VE", "SE")}
        consts = common_const_setup(b, need_identb=True, need_identf=True,
                                    const_vals=(1e-6,))
        consts["pT2"] = [b.psum("pT0", [128, 1024], BF16),
                        b.psum("pT1", [128, 1024], BF16)]
        psums = [b.psum(f"ps{i}", [128, 512], F32) for i in range(6)]
        emit_deltanet(b, consts, sems, psums, io)
        b.finalize()
    return nc




def build_ln_launch():
    nc = bass.Bass("TRN2", target_bir_lowering=False, debug=False)
    x = nc.dram_tensor("x", [TPC // 128, 4, 128, 512], F32,
                       kind="ExternalInput").ap()
    o = nc.dram_tensor("o", [16, 128, TPC], BF16, kind="ExternalOutput").ap()
    with ExitStack() as ctx:
        b = Builder(nc, ctx)
        sems = {k: b.sem(k) for k in ("TE", "VE", "SE")}
        sems["XS"] = b.slot_sems("XS", 2)
        SO = b.sem("SO")
        consts = common_const_setup(b, need_identb=True, const_vals=(1e-5,))
        consts["pT2"] = [b.psum("pT0", [128, 1024], BF16),
                        b.psum("pT1", [128, 1024], BF16)]
        lnT = [b.sbuf(f"lnT{c}", [128, TPC], BF16) for c in range(16)]
        emit_ln_transpose(b, consts, x, lnT, TPC, sems, name="ln")
        b.wait("gpsimd", sems["VE"])
        n = 0
        for c in range(16):
            b.inc(b.g.dma_start(out=o[c], in_=lnT[c][:]), SO, 16)
            n += 16
        b.wait("gpsimd", SO, n)
        b.finalize()
    return nc




def emit_wo(b, sems, psums, io, tag="wo"):
    TE, VE, SE = sems["TE"], sems["VE"], sems["SE"]
    wob = io["wo"]
    WOS = b.slot_sems(tag + "WOS", 4)
    ATS = b.slot_sems(tag + "ATS", 3)
    XE = b.slot_sems(tag + "XE", 2)
    SS = b.slot_sems(tag + "SS", 2)
    if True:
        pPS = psums[0:4]
        att = [b.sbuf(f"att{j}", [128, TPC], BF16) for j in range(3)]
        wobuf = [b.sbuf(f"wob{j}", [128, 512], BF16) for j in range(4)]
        xe_buf = [b.sbuf(f"xe{j}", [128, 512], F32) for j in range(2)]
        oe_buf = [b.sbuf(f"oe{j}", [128, 512], F32) for j in range(2)]

        at_n = [0, 0, 0]
        wo_n = [0, 0, 0, 0]
        te_cc = []
        ve_epi = []
        ld = 0
        for tg in range(2):
            for os_ in range(4):
                for cc in range(16):
                    ja = ld % 3
                    jw = ld % 4
                    if at_n[ja] >= 1:
                        b.wait("sync", TE, te_cc[-3] if len(te_cc) >= 3 else 0)
                    b.inc(b.y.dma_start(out=att[ja][:], in_=io["oT_chunk"](cc)),
                          ATS[ja], 16)
                    at_n[ja] += 1
                    if wo_n[jw] >= 1:
                        b.wait("sync", TE, te_cc[-4] if len(te_cc) >= 4 else 0)
                    b.inc(b.y.dma_start(out=wobuf[jw][:], in_=wob[cc, os_]),
                          WOS[jw], 16)
                    wo_n[jw] += 1
                    b.wait("tensor", ATS[ja], at_n[ja] * 16)
                    b.wait("tensor", WOS[jw], wo_n[jw] * 16)
                    if cc == 0 and ld > 0:
                        b.wait("tensor", VE)
                    mm = None
                    for tt in range(4):
                        t_abs = tg * 4 + tt
                        mm = b.t.matmul(out=pPS[tt][:],
                                        lhsT=att[ja][:, t_abs * 128:(t_abs + 1) * 128],
                                        rhs=wobuf[jw][:],
                                        start=(cc == 0), stop=(cc == 15))
                    b.inc(mm, TE)
                    te_cc.append(TE.count)
                    ld += 1
                for tt in range(4):
                    t_abs = tg * 4 + tt
                    jj = tt % 2
                    if len(ve_epi) >= 2:
                        b.wait("sync", VE, ve_epi[-2])
                    dx = b.y.dma_start(out=xe_buf[jj][:],
                                       in_=io["resid"](t_abs, os_))
                    b.inc(dx, XE[jj], 16)
                    b.wait("vector", XE[jj])
                    b.wait("vector", TE)
                    if len(ve_epi) >= 2:
                        b.wait("vector", SS[jj])
                    av = b.v.tensor_add(out=oe_buf[jj][:], in0=pPS[tt][:],
                                        in1=xe_buf[jj][:])
                    ve_epi.append(b.inc(av, VE))
                    b.wait("sync", VE)
                    so = b.y.dma_start(out=io["out"](t_abs, os_),
                                       in_=oe_buf[jj][:])
                    b.inc(so, SS[jj], 16)
        b.wait("sync", SS[0])
        b.wait("sync", SS[1])
    return {"SS": SS}


def build_wo_launch():
    nc = bass.Bass("TRN2", target_bir_lowering=False, debug=False)
    x = nc.dram_tensor("x", [TPC // 128, 4, 128, 512], F32,
                       kind="ExternalInput").ap()
    oT = nc.dram_tensor("oT", [16, 128, TPC], BF16, kind="ExternalInput").ap()
    wo = nc.dram_tensor("wo", [16, 4, 128, 512], F32, kind="ExternalInput").ap()
    o_x = nc.dram_tensor("o_x", [TPC // 128, 4, 128, 512], F32,
                         kind="ExternalOutput").ap()
    with ExitStack() as ctx:
        b = Builder(nc, ctx)
        sems = {k: b.sem(k) for k in ("TE", "VE", "SE")}
        PREP = b.sem("PREP")
        wob = nc.dram_tensor("wob", [16, 4, 128, 512], BF16, kind="Internal").ap()
        prep_n = 0
        for q in range(4):
            lo, hi = 4 * q, 4 * (q + 1)
            b.inc(b.g.dma_start(out=wob[lo:hi], in_=wo[lo:hi]), PREP, 16)
            prep_n += 16
        b.wait("sync", PREP, prep_n)
        psums = [b.psum(f"ps{i}", [128, 512], F32) for i in range(4)]
        io = {"oT_chunk": lambda cc: oT[cc], "wo": wob,
              "resid": lambda t, o_: x[t, o_],
              "out": lambda t, o_: o_x[t, o_]}
        emit_wo(b, sems, psums, io)
        b.finalize()
    return nc








class _TileView:

    def __init__(self, flat_ap, shape_tail):
        self.flat = flat_ap
        self.tail = shape_tail
        self.elems = int(np.prod(shape_tail))
        self.dtype = flat_ap.dtype

    def __getitem__(self, idx):
        if isinstance(idx, tuple):
            os_, ic = idx
            nit = 0
            lin = os_ * self.nit + ic
        else:
            lin = idx
        sl = self.flat[lin * self.elems:(lin + 1) * self.elems]
        return sl.rearrange("(p i) -> p i", p=self.tail[0])


def build_fused_launch():
    nc = bass.Bass("TRN2", target_bir_lowering=False, debug=False)
    RG = [[i for i in range(NCORES)]]
    E = lambda nm, sh, dt=F32: nc.dram_tensor(nm, sh, dt, kind="ExternalInput").ap()
    I_ = lambda nm, sh, dt=F32: nc.dram_tensor(nm, sh, dt, kind="Internal").ap()
    NIT0, NIT1 = 5632 // 128, 5504 // 128

    x0 = E("x0", [8, 4, 128, 512])
    f0gw = E("f0gw", [NIT0, 128, D])
    f0uw = E("f0uw", [NIT0, 128, D])
    f0dw = E("f0dw", [4, NIT0, 128, 512])
    f0gs = E("f0gs", [5632])
    f0us = E("f0us", [5632])
    f0ds = E("f0ds", [D])
    a0wq = E("a0wq", [NH0, 128, D])
    a0wk = E("a0wk", [NKV0, 128, D])
    a0wv = E("a0wv", [NKV0, 128, D])
    a0wo = E("a0wo", [16, 4, 128, 512])
    a0lam = E("a0lam", [NH0])
    a0cos = E("a0cos", [128, NTOK0], BF16)
    a0sin = E("a0sin", [128, NTOK0], BF16)
    a0mask = E("a0mask", [8, 128, 640], BF16)
    glut = E("glut", [16])
    gidx = {nm: E("gidx_" + nm, [128, GCH], I32) for nm in ("g", "u", "d")}
    dnio_names = [("wqh", [2, 16, 128, 128]), ("wkh", [2, 16, 128, 128]),
                  ("wvh", [2, 16, 128, 128]), ("wfa", [16, 128, 128]),
                  ("wga", [16, 128, 128]), ("wfbh", [2, 128, 128]),
                  ("wgbh", [2, 128, 128]), ("wbt", [16, 128, 2]),
                  ("convw", [3, 2, 128, 4]), ("nega", [1, 2]),
                  ("dtb", [2, 128]), ("nw", [1, 128])]
    dnw = {nm: E("dn" + nm, sh) for nm, sh in dnio_names}
    wo1 = E("wo1", [16, 4, 128, 512])
    o_x = nc.dram_tensor("o_x", [8, 4, 128, 512], F32, kind="ExternalOutput").ap()

    xres1 = I_("xres1", [8, 4, 128, 512])
    xres2 = I_("xres2", [8, 4, 128, 512])
    xres3 = I_("xres3", [8, 4, 128, 512])
    xres4 = I_("xres4", [8, 4, 128, 512])
    xres5 = I_("xres5", [8, 4, 128, 512])
    agt_in = I_("agt_in", [4 * 128, 2048])
    agt_out = I_("agt_out", [8 * 4 * 128, 2048])
    gh_in = {nm: I_("ghin_" + nm, [128 * GCH], BF16) for nm in ("g", "u", "d")}
    gh_out = {nm: I_("ghout_" + nm, [8 * 128 * GCH], BF16) for nm in ("g", "u", "d")}
    ln_in = I_("ln_in", [16 * 128 * TPC], BF16)
    ln_out = I_("ln_out", [8 * 16 * 128 * TPC], BF16)
    og_in = I_("og_in", [2 * 128 * T], BF16)
    og_out = I_("og_out", [8 * 2 * 128 * T], BF16)

    with ExitStack() as root:
        b = Builder(nc, root)
        sems = {k: b.sem(k) for k in ("TE", "VE", "SE")}
        sems["XS"] = b.slot_sems("XS", 2)
        TE, VE, SE = sems["TE"], sems["VE"], sems["SE"]
        CC = b.sem("CC")
        HB = b.sem("HB")
        consts = common_const_setup(b, need_identb=True, need_identf=True,
                                    const_vals=(1e-5, 1e-6, -20.0))
        consts["pT2"] = [b.psum("pT0", [128, 1024], BF16),
                        b.psum("pT1", [128, 1024], BF16)]
        psums = [b.psum(f"ps{i}", [128, 512], F32) for i in range(6)]

        def barrier(*store_sems):
            for ss in store_sems:
                for engq in ("sync", "gpsimd"):
                    b.wait(engq, ss)
            b.wait("tensor", VE)
            b.wait("tensor", SE)
            b.wait("vector", TE)

        f0bf, F0PREP, f0prep_n = emit_f0_prep(b, nc, f0gw, f0uw, f0dw, "fA")

        b.phase_mark("ghost")
        with b.scope():
            r = emit_ghost(b, sems, psums, {
                "lut": glut, "idxs": gidx,
                "outs": {nm: gh_in[nm].rearrange("(p c) -> p c", p=128)
                         for nm in ("g", "u", "d")}})
        for ss in r["OS"]:
            b.wait("gpsimd", ss)
        for nm in ("g", "u", "d"):
            it = b.g.collective_compute("AllGather", ALU.bypass,
                                        replica_groups=RG,
                                        ins=[gh_in[nm].opt()],
                                        outs=[gh_out[nm].opt()])
            b.inc(it, CC, 1)

        b.phase_mark("ffnA")
        b.wait("sync", F0PREP, f0prep_n)
        b.wait("gpsimd", F0PREP, f0prep_n)
        with b.scope():
            rA = build_ffn_program(nc, b, sems, consts, x0, xres1, f0bf[0],
                                   f0bf[1], f0bf[2], (f0gs, f0us, f0ds),
                                   5632, half=0.5, tag="fA", psums=psums)
        barrier(*rA["SS"])

        b.phase_mark("halo")
        for i in range(4):
            d = b.g.dma_start(out=agt_in[i * 128:(i + 1) * 128, :]
                              .rearrange("p (a c) -> p a c", a=4),
                              in_=xres1[4 + i].rearrange("a p c -> p a c"))
            b.inc(d, HB, 16)
        b.wait("gpsimd", HB)
        it = b.g.collective_compute("AllGather", ALU.bypass, replica_groups=RG,
                                    ins=[agt_in.opt()], outs=[agt_out.opt()])
        b.inc(it, CC, 1)
        b.wait("sync", CC)

        PREPA = b.sem("PREPA")
        wqb = I_("wqb", [NH0, 128, D], BF16)
        wkb = I_("wkb", [NKV0, 128, D], BF16)
        wvb = I_("wvb", [NKV0, 128, D], BF16)
        wob = I_("wob", [16, 4, 128, 512], BF16)
        wo1b = I_("wo1b", [16, 4, 128, 512], BF16)
        prep_n = 0
        for src, dst, nq in ((a0wq, wqb, 4), (a0wk, wkb, 2), (a0wv, wvb, 2),
                             (a0wo, wob, 4), (wo1, wo1b, 4)):
            for q in range(nq):
                lo = (src.shape[0] // nq) * q
                hi = (src.shape[0] // nq) * (q + 1)
                b.inc(b.g.dma_start(out=dst[lo:hi], in_=src[lo:hi]), PREPA, 16)
                prep_n += 16
        b.wait("sync", PREPA, prep_n)

        b.phase_mark("attn0")
        _symcache = {}

        def _pid_syms(eng):
            key = id(eng)
            if key not in _symcache:
                pid = eng.partition_id()
                halo_base = ((pid + 7) % 8) * (512 * 2048)
                tok_off = pid * TPC
                _symcache[key] = (halo_base, tok_off)
            return _symcache[key]

        def halo_ap(i):
            def fn(eng):
                halo_base, _ = _pid_syms(eng)
                return _BassAP(agt_out.tensor, halo_base + i * (128 * 2048),
                               [[2048, 128], [1, 2048]])
            return Deferred(fn, ndim=2)

        ln_aps = [halo_ap(i) for i in range(4)] + [
            xres1[i].rearrange("a p c -> p a c") for i in range(8)]
        with b.scope():
            rB = emit_attn0(b, consts, sems, psums, {
                "xh": ln_aps, "wq": wqb, "wk": wkb, "wv": wvb, "wo": wob,
                "lam": a0lam, "cosr": a0cos, "sinr": a0sin, "mask": a0mask,
                "resid": lambda t, o_: xres1[t, o_],
                "out": lambda t, o_: xres2[t, o_]})
        barrier(*rB["SS"])

        b.phase_mark("ffnC")
        with b.scope():
            rC = build_ffn_program(nc, b, sems, consts, xres2, xres3,
                                   f0bf[0], f0bf[1], f0bf[2],
                                   (f0gs, f0us, f0ds), 5632, half=0.5,
                                   tag="fC", psums=psums)
        barrier(*rC["SS"])

        b.phase_mark("ffnD")
        b.wait("sync", CC)
        b.wait("gpsimd", CC)
        ggw = _TileView(gh_out["g"], (128, D))
        guw = _TileView(gh_out["u"], (128, D))
        gdw = _TileView(gh_out["d"], (128, 512))
        gdw.nit = NIT1
        with b.scope():
            rD = build_ffn_program(nc, b, sems, consts, xres3, xres4,
                                   ggw, guw, gdw, None, 5504, half=0.5,
                                   tag="fD", psums=psums)
        barrier(*rD["SS"])

        b.phase_mark("ln4")
        LNS = b.sem("LNS")
        with b.scope():
            lnT = [b.sbuf(f"e_lnT{c}", [128, TPC], BF16) for c in range(16)]
            emit_ln_transpose(b, consts, xres4, lnT, TPC, sems, name="elng")
            b.wait("gpsimd", VE)
            n = 0
            for c in range(16):
                d = b.g.dma_start(
                    out=ln_in[c * 128 * TPC:(c + 1) * 128 * TPC]
                    .rearrange("(p t) -> p t", p=128), in_=lnT[c][:])
                b.inc(d, LNS, 16)
                n += 16
            b.wait("gpsimd", LNS, n)
        it = b.g.collective_compute("AllGather", ALU.bypass, replica_groups=RG,
                                    ins=[ln_in.opt()], outs=[ln_out.opt()])
        b.inc(it, CC, 1)
        b.wait("sync", CC)

        b.phase_mark("dn")
        lnov = ln_out.rearrange("(r c p t) -> r c p t", r=8, c=16, p=128)
        ogv = og_in.rearrange("(h p t) -> h p t", h=2, p=128)

        def ln_slice(c, t0, n):
            rk = t0 // TPC
            off = t0 % TPC
            return lnov[rk, c, :, off:off + n]

        with b.scope():
            rF = emit_deltanet(b, consts, sems, psums, dict(
                dnw, ln_slice=ln_slice, og=lambda h: ogv[h]))
        for ss in rF["OGS"]:
            b.wait("gpsimd", ss)
        it = b.g.collective_compute("AllGather", ALU.bypass, replica_groups=RG,
                                    ins=[og_in.opt()], outs=[og_out.opt()])
        b.inc(it, CC, 1)
        b.wait("sync", CC)

        b.phase_mark("wo")
        def oT_chunk(cc):
            def fn(eng):
                _, tok_off = _pid_syms(eng)
                return _BassAP(og_out.tensor, tok_off + cc * (128 * T),
                               [[T, 128], [1, TPC]])
            return Deferred(fn, ndim=2)

        with b.scope():
            rG = emit_wo(b, sems, psums, {
                "oT_chunk": oT_chunk, "wo": wo1b,
                "resid": lambda t, o_: xres4[t, o_],
                "out": lambda t, o_: xres5[t, o_]})
        barrier(*rG["SS"])

        b.phase_mark("ffnH")
        with b.scope():
            rH = build_ffn_program(nc, b, sems, consts, xres5, o_x,
                                   ggw, guw, gdw, None, 5504, half=0.5,
                                   tag="fH", psums=psums)
        b.wait("sync", rH["SS"][0])
        b.wait("sync", rH["SS"][1])
        b.finalize()
    return nc


def fused_host_inputs(ins):
    gwT = tile_gu(ins["L0_ffn_gate_w"])
    uwT = tile_gu(ins["L0_ffn_up_w"])
    dwT = tile_dw(ins["L0_ffn_down_w"])
    wq4 = np.ascontiguousarray(
        ins["L0_attn_Wq"].reshape(16, 128, 16, 128).transpose(2, 1, 0, 3)
        .reshape(16, 128, D))
    wk4 = np.ascontiguousarray(
        ins["L0_attn_Wk"].reshape(16, 128, 4, 128).transpose(2, 1, 0, 3)
        .reshape(4, 128, D))
    wv4 = np.ascontiguousarray(
        ins["L0_attn_Wv"].reshape(16, 128, 4, 128).transpose(2, 1, 0, 3)
        .reshape(4, 128, D))
    wo4 = np.ascontiguousarray(
        ins["L0_attn_Wo"].reshape(16, 128, 4, 512).transpose(0, 2, 1, 3))
    wo14 = np.ascontiguousarray(
        ins["L1_attn_Wo"].reshape(16, 128, 4, 512).transpose(0, 2, 1, 3))
    ghost_maps = ghost_host_inputs(ins)
    dn_maps = dn_host_inputs(ins, None)
    inv = (10000.0 ** (-(np.arange(64, dtype=np.float64) * 2.0 / 128.0)))
    x0f = np.ascontiguousarray(ins["L0_attn_x"], dtype=np.float32)[0]
    maps = []
    for c in range(NCORES):
        g0 = c * TPC
        pos = np.arange(g0 - HALO, g0 + TPC, dtype=np.float64)
        ang = pos[None, :] * inv[:, None]
        cosr = np.concatenate([np.cos(ang), np.cos(ang)], axis=0)
        sinr = np.concatenate([np.sin(ang), np.sin(ang)], axis=0)
        r = np.arange(128)[None, :, None]
        cix = np.arange(640)[None, None, :]
        qb = np.arange(8)[:, None, None]
        gj = g0 - HALO + qb * 128 + cix
        allowed = (cix >= r + 1) & (cix <= r + 512) & (gj >= 0)
        maskv = np.where(allowed, np.float32(0), np.float32(-1e30))
        m = {
            "x0": tile_x(x0f[g0:g0 + TPC]),
            "f0gw": gwT, "f0uw": uwT, "f0dw": dwT,
            "f0gs": ins["L0_ffn_gate_s"], "f0us": ins["L0_ffn_up_s"],
            "f0ds": ins["L0_ffn_down_s"],
            "a0wq": wq4, "a0wk": wk4, "a0wv": wv4, "a0wo": wo4,
            "a0lam": ins["L0_attn_lam"],
            "a0cos": cosr.astype(BF), "a0sin": sinr.astype(BF),
            "a0mask": maskv.astype(BF),
            "glut": ins["L1_ffn_lut"],
            "wo1": wo14,
        }
        for nm in ("g", "u", "d"):
            m["gidx_" + nm] = ghost_maps[c]["idx_" + nm]
        for nm, _sh in [("wqh", 0), ("wkh", 0), ("wvh", 0), ("wfa", 0),
                        ("wga", 0), ("wfbh", 0), ("wgbh", 0), ("wbt", 0),
                        ("convw", 0), ("nega", 0), ("dtb", 0), ("nw", 0)]:
            m["dn" + nm] = dn_maps[c][nm]
        maps.append(m)
    return maps


def dn_host_inputs(ins, ln4T):
    wq = ins["L1_attn_Wq"]; wk = ins["L1_attn_Wk"]; wv = ins["L1_attn_Wv"]
    wfa = np.ascontiguousarray(ins["L1_attn_Wfa"].reshape(16, 128, 128))
    wga = np.ascontiguousarray(ins["L1_attn_Wga"].reshape(16, 128, 128))
    wfb = ins["L1_attn_Wfb"]
    wgb = ins["L1_attn_Wgb"]
    wb = ins["L1_attn_Wb"]
    alog = ins["L1_attn_A_log"].reshape(16)
    dtbias = ins["L1_attn_dt_bias"].reshape(16, 128)
    maps = []
    for c in range(NCORES):
        hs = [2 * c, 2 * c + 1]

        def headtile(w):
            out = np.empty((2, 16, 128, 128), np.float32)
            for i, h in enumerate(hs):
                out[i] = w[:, h * 128:(h + 1) * 128].reshape(16, 128, 128)
            return out

        convw = np.empty((3, 2, 128, 4), np.float32)
        for ni, nm in enumerate(("conv_q", "conv_k", "conv_v")):
            cw = ins["L1_attn_" + nm]
            for i, h in enumerate(hs):
                convw[ni, i] = cw[h * 128:(h + 1) * 128]
        mm_ = {
            "wqh": headtile(wq), "wkh": headtile(wk), "wvh": headtile(wv),
            "wfa": wfa, "wga": wga,
            "wfbh": np.ascontiguousarray(
                np.stack([wfb[:, h * 128:(h + 1) * 128] for h in hs])),
            "wgbh": np.ascontiguousarray(
                np.stack([wgb[:, h * 128:(h + 1) * 128] for h in hs])),
            "wbt": np.ascontiguousarray(wb[:, hs].reshape(16, 128, 2)),
            "convw": convw,
            "nega": np.ascontiguousarray(alog[hs].reshape(1, 2), np.float32),
            "dtb": np.ascontiguousarray(dtbias[hs], np.float32),
            "nw": np.ascontiguousarray(ins["L1_attn_norm_w"].reshape(1, 128)),
        }
        if ln4T is not None:
            mm_["lnT4"] = ln4T
        maps.append(mm_)
    return maps




def kernel(**ins):
    ins = {k: np.asarray(v) for k, v in ins.items()}
    cores = list(range(NCORES))
    nc = _cached(("fused",), build_fused_launch)
    maps = fused_host_inputs(ins)
    res = run_bass_kernel_spmd(nc, maps, core_ids=cores)
    x6 = np.concatenate([untile_x(np.asarray(r["o_x"])) for r in res.results],
                        axis=0)
    return np.ascontiguousarray(x6.reshape(1, T, D), dtype=np.float32)


def kernel_unfused(**ins):
    ins = {k: np.asarray(v) for k, v in ins.items()}
    cores = list(range(NCORES))
    x0 = np.ascontiguousarray(ins["L0_attn_x"], dtype=np.float32)[0]

    x1 = run_ffn_f0(x0, ins["L0_ffn_gate_w"], ins["L0_ffn_up_w"],
                    ins["L0_ffn_down_w"], ins["L0_ffn_gate_s"],
                    ins["L0_ffn_up_s"], ins["L0_ffn_down_s"])

    nc2 = _cached(("attn0",), build_attn0_launch)
    res = run_bass_kernel_spmd(nc2, attn0_host_inputs(ins, x1), core_ids=cores)
    x2 = np.concatenate([untile_x(np.asarray(r["o_x"])) for r in res.results],
                        axis=0)

    x3 = run_ffn_f0(x2, ins["L0_ffn_gate_w"], ins["L0_ffn_up_w"],
                    ins["L0_ffn_down_w"], ins["L0_ffn_gate_s"],
                    ins["L0_ffn_up_s"], ins["L0_ffn_down_s"])

    nc4 = _cached(("ghost",), build_ghost_launch)
    res = run_bass_kernel_spmd(nc4, ghost_host_inputs(ins), core_ids=cores)
    gw = ghost_reassemble(res.results)

    x4 = run_ffn_f1(x3, gw)

    nc6 = _cached(("ln",), build_ln_launch)
    maps = [{"x": tile_x(x4[c * TPC:(c + 1) * TPC])} for c in cores]
    res = run_bass_kernel_spmd(nc6, maps, core_ids=cores)
    ln4T = np.concatenate([np.asarray(r["o"]) for r in res.results], axis=2)

    nc7 = _cached(("dn",), build_deltanet_launch)
    res = run_bass_kernel_spmd(nc7, dn_host_inputs(ins, ln4T), core_ids=cores)
    oT = np.concatenate([np.asarray(r["o_og"]) for r in res.results], axis=0)

    nc8 = _cached(("wo",), build_wo_launch)
    wo4 = np.ascontiguousarray(
        ins["L1_attn_Wo"].reshape(16, 128, 4, 512).transpose(0, 2, 1, 3))
    maps = [{"x": tile_x(x4[c * TPC:(c + 1) * TPC]),
             "oT": np.ascontiguousarray(oT[:, :, c * TPC:(c + 1) * TPC]),
             "wo": wo4} for c in cores]
    res = run_bass_kernel_spmd(nc8, maps, core_ids=cores)
    x5 = np.concatenate([untile_x(np.asarray(r["o_x"])) for r in res.results],
                        axis=0)

    x6 = run_ffn_f1(x5, gw)
    return np.ascontiguousarray(x6.reshape(1, T, D), dtype=np.float32)

